# version 09
# baseline (speedup 1.0000x reference)
import numpy as np
import ml_dtypes

import concourse.bass as bass
import concourse.mybir as mybir
from concourse.bass_utils import run_bass_kernel_spmd

F32 = mybir.dt.float32
BF16 = mybir.dt.bfloat16
AF = mybir.ActivationFunctionType
ALU = mybir.AluOpType
BF = ml_dtypes.bfloat16

P = 128
D = 2048
DC = 16
INTER = 8192
IC = 64
TT = 512
TB = 4
HALF = 256
E8 = 64
NWARM = 512
S0 = 1.0 / (64.0 * np.sqrt(2048.0))
EPS = 1e-5
ENG = ("pe", "v", "s", "g", "sp")


def _hadamard128():
    i = np.arange(128)
    pc = np.zeros((128, 128), np.int64)
    for b in range(7):
        pc += ((i[:, None] >> b) & 1) * ((i[None, :] >> b) & 1)
    return np.where(pc % 2 == 0, 1.0, -1.0).astype(BF)


class Prog:
    def __init__(self, nc, es):
        self.nc = nc
        self.sem = {}
        self.cnt = {}
        for name in ("pe", "v", "s", "g"):
            self.sem[name] = es.enter_context(nc.semaphore(name))
            self.cnt[name] = 0
        self.wm = {e: {} for e in ENG}
        self.eng = {"pe": nc.tensor, "v": nc.vector, "s": nc.scalar,
                    "g": nc.gpsimd, "sp": nc.sync}

    def wait(self, eng, tok):
        if tok is None:
            return
        if isinstance(tok, list):
            for t in tok:
                self.wait(eng, t)
            return
        sname, val = tok
        if self.wm[eng].get(sname, 0) >= val:
            return
        self.eng[eng].wait_ge(self.sem[sname], val)
        self.wm[eng][sname] = val

    def inc(self, eng, inst):
        self.cnt[eng] += 1
        inst.then_inc(self.sem[eng], 1)
        return (eng, self.cnt[eng])

    def dinc(self, dsem, inst):
        self.cnt[dsem] += 16
        inst.then_inc(self.sem[dsem], 16)
        return (dsem, self.cnt[dsem])

    def mark(self, eng):
        return (eng, self.cnt[eng])


class DQ:

    def __init__(self, pr, eng_name, basename, es, nsem):
        self.pr = pr
        self.eng_name = eng_name
        self.eng = pr.eng[eng_name]
        self.names = []
        for i in range(nsem):
            name = f"{basename}{i}"
            pr.sem[name] = es.enter_context(pr.nc.semaphore(name))
            pr.cnt[name] = 0
            self.names.append(name)
        self.i = 0
        self.cur = None

    def dma(self, out, in_, pre=None, group=False, **kw):
        pr = self.pr
        if group and self.cur is not None:
            k = self.cur
        else:
            k = self.names[self.i]
            self.i = (self.i + 1) % len(self.names)
            if pr.cnt[k] > 0:
                pr.wait(self.eng_name, (k, pr.cnt[k]))
            self.cur = k
        pr.wait(self.eng_name, pre)
        inst = self.eng.dma_start(out=out, in_=in_, **kw)
        pr.cnt[k] += 16
        inst.then_inc(pr.sem[k], 16)
        return (k, pr.cnt[k])

    def close(self):
        self.cur = None

    def all_last(self):
        return [(k, self.pr.cnt[k]) for k in self.names if self.pr.cnt[k] > 0]


class Pool:

    def __init__(self, pr, tiles):
        self.pr = pr
        self.tiles = tiles
        self.free = [None] * len(tiles)
        self.i = 0

    def get(self, eng):
        k = self.i
        self.i = (self.i + 1) % len(self.tiles)
        self.pr.wait(eng, self.free[k])
        return k, self.tiles[k]

    def put(self, k, tok):
        self.free[k] = tok


def build_nc(n_full=8):
    nc = bass.Bass("TRN2", target_bir_lowering=False)
    n_pass = n_full + 1
    rows = n_pass * TT

    xD = nc.dram_tensor("x", [rows, D], F32, kind="ExternalInput").ap()
    wiD = nc.dram_tensor("wi", [D, D], F32, kind="ExternalInput").ap()
    wfD = nc.dram_tensor("wf", [D, D], F32, kind="ExternalInput").ap()
    wgD = nc.dram_tensor("wg", [D, D], F32, kind="ExternalInput").ap()
    woD = nc.dram_tensor("wo", [D, D], F32, kind="ExternalInput").ap()
    wupD = nc.dram_tensor("wup", [INTER, D], F32, kind="ExternalInput").ap()
    wgtD = nc.dram_tensor("wgt", [INTER, D], F32, kind="ExternalInput").ap()
    wdnD = nc.dram_tensor("wdn", [D, INTER], F32, kind="ExternalInput").ap()
    gnwD = nc.dram_tensor("gnw", [P, DC], F32, kind="ExternalInput").ap()
    suugD = nc.dram_tensor("suug", [P, DC], F32, kind="ExternalInput").ap()
    sudpD = nc.dram_tensor("sudp", [P, IC], F32, kind="ExternalInput").ap()
    wsuD = nc.dram_tensor("wsu", [P, IC], F32, kind="ExternalInput").ap()
    wsgD = nc.dram_tensor("wsg", [P, IC], F32, kind="ExternalInput").ap()
    wsdD = nc.dram_tensor("wsd", [P, DC], F32, kind="ExternalInput").ap()
    h128D = nc.dram_tensor("h128", [P, P], BF16, kind="ExternalInput").ap()
    identD = nc.dram_tensor("ident", [P, P], F32, kind="ExternalInput").ap()
    outD = nc.dram_tensor("out", [n_full * TT, D], F32, kind="ExternalOutput").ap()

    xbD = nc.dram_tensor("xb", [rows, D], BF16, kind="Internal").ap()
    wiB = nc.dram_tensor("wiB", [DC, D, P], BF16, kind="Internal").ap()
    wfB = nc.dram_tensor("wfB", [DC, D, P], BF16, kind="Internal").ap()
    wgB = nc.dram_tensor("wgB", [DC, D, P], BF16, kind="Internal").ap()
    woB = nc.dram_tensor("woB", [D, D], BF16, kind="Internal").ap()
    wupB = nc.dram_tensor("wupB", [INTER, D], BF16, kind="Internal").ap()
    wgtB = nc.dram_tensor("wgtB", [INTER, D], BF16, kind="Internal").ap()
    wdnB = nc.dram_tensor("wdnB", [D, INTER], BF16, kind="Internal").ap()
    wdnF = nc.dram_tensor("wdnF", [D, INTER], F32, kind="Internal").ap()
    wupT = nc.dram_tensor("wupT", [32, P, DC * HALF], BF16, kind="Internal").ap()
    wgtT = nc.dram_tensor("wgtT", [32, P, DC * HALF], BF16, kind="Internal").ap()
    wdnT = nc.dram_tensor("wdnT", [IC, P, 2048], BF16, kind="Internal").ap()
    y1fD = nc.dram_tensor("y1f", [TT, D], F32, kind="Internal").ap()

    from contextlib import ExitStack
    es = ExitStack()
    pr = Prog(nc, es)
    W, I = pr.wait, pr.inc
    te, ve, se, ge, sy = nc.tensor, nc.vector, nc.scalar, nc.gpsimd, nc.sync
    qsp = DQ(pr, "sp", "qw", es, 8)
    qac = DQ(pr, "s", "qa", es, 4)
    qgp = DQ(pr, "g", "qg", es, 4)
    qdo = DQ(pr, "sp", "qo", es, 2)

    off = [17408]

    def at(name, shape, dtype, o=None):
        nonlocal off
        nbytes = int(np.prod(shape[1:])) * mybir.dt.size(dtype)
        if o is None:
            o = off[0]
            off[0] += (nbytes + 31) // 32 * 32
        return nc.alloc_sbuf_tensor_at(name, shape, dtype, offset=o)

    h128 = at("h128", [P, P], BF16)
    ident = at("ident", [P, P], F32)
    gnw = at("gnw", [P, DC], F32)
    suug = at("suug", [P, DC], F32)
    sdpc = at("sdpc", [P, IC], F32)
    wsu = at("wsu", [P, IC], F32)
    wsg = at("wsg", [P, IC], F32)
    wsd = at("wsd", [P, DC], F32)
    ones_c = at("ones_c", [P, 1], BF16)
    ones_r = at("ones_r", [1, P], F32)
    state = at("state", [P, DC], F32)
    rmsrow = at("rmsrow", [1, TT], F32)
    rstd_b = at("rstd_b", [P, TT], F32)
    accm = at("accm", [P, 16], F32)
    accs = at("accs", [P, 16], F32)
    lnstat = at("lnstat", [P, 10 * TB], F32)
    wdb = [at(f"wdb{i}", [P, 2048], BF16) for i in range(2)]
    fpool_t = [at(f"fp{i}", [P, TT], F32) for i in range(7)]
    bpool_t = [at(f"bp{i}", [P, TT], BF16) for i in range(4)]
    prepf = [at(f"prepf{i}", [P, 1024], F32) for i in range(2)]
    prepb = [at(f"prepb{i}", [P, 1024], BF16) for i in range(2)]
    base_small = off[0]
    assert base_small <= 65504, base_small

    OFF_W = 65504
    OFF_X = 98272
    OFF_H = 131040
    OFF_Z = 163808
    OFF_WS = 196576
    OFF_XT = 212960

    wib = [at(f"wib{i}", [P, D], BF16, OFF_W + i * 4096) for i in range(3)]
    wfb = [at(f"wfb{i}", [P, D], BF16, OFF_W + 12288 + i * 4096) for i in range(3)]
    wob = [at(f"wob{i}", [P, 1024], BF16, OFF_W + 24576 + i * 2048) for i in range(3)]
    wub = [at(f"wub{i}", [P, DC * HALF], BF16, OFF_W + i * 8192) for i in range(2)]
    wgb = [at(f"wgb{i}", [P, DC * HALF], BF16, OFF_W + 16384 + i * 8192) for i in range(2)]

    xtok = at("xtok", [P, TB * D], F32, OFF_X)
    xrot = at("xrot", [P, DC * TT], BF16, OFF_X)

    hbuf = at("hbuf", [P, DC * TT], F32, OFF_H)
    r1 = at("r1", [P, TB * D], F32, OFF_H)
    zt1 = at("zt1", [P, IC * HALF], BF16, OFF_H)
    r2 = at("r2", [P, 2 * D], F32, OFF_WS)
    foldf = [at(f"foldf{i}", [P, 1024], F32, OFF_WS + i * 4096) for i in range(2)]
    foldf2 = [at(f"foldf2{i}", [P, 1024], F32, OFF_WS + 8192 + i * 4096) for i in range(2)]

    o4 = at("o4", [P, DC * TT], BF16, OFF_Z)
    fb = at("fb", [P, DC * HALF], F32, OFF_Z)
    fb2 = at("fb2", [P, DC * HALF], F32, OFF_Z + 16 * 1024)
    zt0 = at("zt0", [P, IC * HALF], BF16, OFF_Z)

    xT = at("xT", [P, DC * TT], BF16, OFF_XT)

    pb = [nc.alloc_psum_tensor(f"pb{i}", [P, TT], F32) for i in range(8)]
    pb_free = [None] * 8
    rr = [0]

    def bank(eng="pe", k=None):
        if k is None:
            k = rr[0]
            rr[0] = (rr[0] + 1) % 6
        W(eng, pb_free[k])
        return k

    fpool = Pool(pr, [t[:] for t in fpool_t])
    bpool = Pool(pr, [t[:] for t in bpool_t])

    ld = {}
    ldt = None
    for name, dst, srcT in [("gnw", gnw, gnwD), ("suug", suug, suugD), ("sdp", sdpc, sudpD),
                            ("wsu", wsu, wsuD), ("wsg", wsg, wsgD), ("wsd", wsd, wsdD),
                            ("h128", h128, h128D), ("ident", ident, identD)]:
        ldt = qsp.dma(dst[:], srcT, group=(ldt is not None))
    for name in ("gnw", "suug", "sdp", "wsu", "wsg", "wsd", "h128", "ident"):
        ld[name] = ldt
    qsp.close()

    ve.memset(ones_c[:], 1.0)
    ve.memset(ones_r[:], 1.0)
    tk0 = I("v", ve.memset(state[:], 0.0))
    W("v", ld["sdp"])
    sdp_tok = I("v", ve.tensor_scalar(sdpc[:], sdpc[:], S0 / 2.0, None, ALU.mult))

    xb_tok = []
    xb_tok.append(qgp.dma(xbD[0:TT, :], xD[0:TT, :]))
    wiB_tok, wfB_tok, wgB_tok = [], [], []
    for eb in range(DC):
        s_ = wiD.rearrange("d (eb q) -> eb d q", q=P)[eb]
        wiB_tok.append(qgp.dma(wiB[eb], s_))
        s_ = wfD.rearrange("d (eb q) -> eb d q", q=P)[eb]
        wfB_tok.append(qgp.dma(wfB[eb], s_))
    for p in range(1, n_pass):
        r0 = p * TT
        xb_tok.append(qgp.dma(xbD[r0:r0 + TT, :], xD[r0:r0 + TT, :]))
    for eb in range(DC):
        s_ = wgD.rearrange("d (eb q) -> eb d q", q=P)[eb]
        wgB_tok.append(qgp.dma(wgB[eb], s_))
    woB_tok = []
    for b in range(4):
        woB_tok.append(qgp.dma(woB[b * TT:(b + 1) * TT, :], woD[b * TT:(b + 1) * TT, :]))

    prep_jobs = []
    for blk in range(INTER // P):
        prep_jobs.append(("up", blk, 0))
        prep_jobs.append(("up", blk, 1))
        prep_jobs.append(("gt", blk, 0))
        prep_jobs.append(("gt", blk, 1))
    prep_jobs.append(("FOLD", 0, 0))
    for blk in range(DC):
        for piece in range(8):
            prep_jobs.append(("dn", blk, piece))
    wupB_tok = [[] for _ in range(INTER // P)]
    wgtB_tok = [[] for _ in range(INTER // P)]
    wdn_toks = []
    fw_free = [None, None]
    for j, (kind, blk, piece) in enumerate(prep_jobs):
        pi = j % 2
        if kind == "FOLD":
            fold1_tok = [[] for _ in range(DC)]
            fold2_tok = [[] for _ in range(DC)]
            fold_free = [None, None]
            for sweep in range(2):
                for jb in range(DC):
                    for grp in range(8):
                        pi3 = (jb * 8 + grp) % 2
                        if sweep == 0:
                            src_ap = wdnD[jb * P:(jb + 1) * P, grp * 1024:(grp + 1) * 1024]
                            dst_ap = wdnF[jb * P:(jb + 1) * P, grp * 1024:(grp + 1) * 1024]
                            pre = [fold_free[pi3]]
                        else:
                            gath = wdnF[jb * P:(jb + 1) * P, :].rearrange(
                                "p (g sub q) -> p sub g q", g=8, q=P)[:, grp, :, :]
                            src_ap = gath
                            dst_ap = gath
                            pre = [fold_free[pi3]] + (fold1_tok[jb] if grp == 0 else [])
                        lt3 = qgp.dma(foldf[pi3][:], src_ap, pre=pre)
                        fa, fz = foldf[pi3], foldf2[pi3]
                        tokf = lt3
                        feng = "g"
                        fe = ge
                        for s_ in (1, 2, 4):
                            blklen = s_ * P
                            vin = fa[:].rearrange("p (a two bt) -> p a two bt", two=2, bt=blklen)
                            vout = fz[:].rearrange("p (a two bt) -> p a two bt", two=2, bt=blklen)
                            W(feng, tokf)
                            I(feng, fe.tensor_add(vout[:, :, 0, :], vin[:, :, 0, :], vin[:, :, 1, :]))
                            tokf = I(feng, fe.tensor_sub(vout[:, :, 1, :], vin[:, :, 0, :], vin[:, :, 1, :]))
                            fa, fz = fz, fa
                        st3 = qgp.dma(dst_ap, fa[:], pre=[tokf])
                        fold_free[pi3] = st3
                        (fold1_tok if sweep == 0 else fold2_tok)[jb].append(st3)

            continue
        if kind == "up":
            srcT, dstT, scv, sc2, ldk = wupD, wupB, wsu, 64.0 * S0, "wsu"
        elif kind == "gt":
            srcT, dstT, scv, sc2, ldk = wgtD, wgtB, wsg, 64.0 * S0, "wsg"
        else:
            srcT, dstT, scv, sc2, ldk = wdnF, wdnB, wsd, 64.0, "wsd"
        sl = (slice(blk * P, (blk + 1) * P), slice(piece * 1024, (piece + 1) * 1024))
        q_ = qgp if kind == "dn" else qsp
        pre_ld = [fw_free[pi]] + (fold2_tok[blk] if kind == "dn" and piece == 0 else [])
        lt = q_.dma(prepf[pi][:], srcT[sl[0], sl[1]], pre=pre_ld)
        W("v", lt)
        W("v", ld[ldk])
        vt = I("v", ve.tensor_scalar(prepb[pi][:], prepf[pi][:],
                                     scv[:, blk:blk + 1], sc2, ALU.mult, ALU.mult))
        st = q_.dma(dstT[sl[0], sl[1]], prepb[pi][:], pre=vt)
        fw_free[pi] = st
        if kind == "up":
            wupB_tok[blk].append(st)
        elif kind == "gt":
            wgtB_tok[blk].append(st)
        else:
            wdn_toks.append(st)

    def sem_maxima(toks):
        seen = {}
        for t_ in toks:
            if t_ is None:
                continue
            seen[t_[0]] = max(seen.get(t_[0], 0), t_[1])
        return [(k, v) for k, v in seen.items()]

    wdn_last = sem_maxima(wdn_toks)

    wdb_tok = [None, None]
    upT_tok = [[] for _ in range(4)]
    gtT_tok = [[] for _ in range(4)]
    dnT_tok = []
    wdb_free = [None, None]

    def emit_phase2_upgt():
        for mat, srcB, dstT, tokf_, toks_out in (
                ("up", wupB, wupT, wupB_tok, upT_tok), ("gt", wgtB, wgtT, wgtB_tok, gtT_tok)):
            for ib in range(4):
                pre_blk = sem_maxima([t for bb in range(ib * 16, (ib + 1) * 16)
                                      for t in tokf_[bb]])
                for dch in range(DC):
                    pi2 = (ib * DC + dch) % 2
                    tt = qac.dma(wdb[pi2][:],
                                 srcB[ib * 2048:(ib + 1) * 2048, dch * P:(dch + 1) * P],
                                 pre=[wdb_tok[pi2]] + (pre_blk if dch == 0 else []),
                                 transpose=True)
                    st2 = qac.dma(dstT[ib * 8:(ib + 1) * 8, :, dch * HALF:(dch + 1) * HALF]
                                  .rearrange("g p t -> p g t"),
                                  wdb[pi2][:].rearrange("p (g t) -> p g t", t=HALF),
                                  pre=[tt])
                    wdb_tok[pi2] = st2
                    toks_out[ib].append(st2)
        for ib in range(4):
            upT_tok[ib] = sem_maxima(upT_tok[ib])
            gtT_tok[ib] = sem_maxima(gtT_tok[ib])

    def emit_phase2_dn():
        for c in range(IC):
            pi2 = c % 2
            tt = qac.dma(wdb[pi2][:], wdnB[0:2048, c * P:(c + 1) * P],
                         pre=[wdb_tok[pi2]] + (wdn_last if c == 0 else []), transpose=True)
            st2 = qac.dma(wdnT[c], wdb[pi2][:], pre=[tt])
            wdb_tok[pi2] = st2
            dnT_tok.append(st2)
        wdb_free[0] = wdb_tok[0]
        wdb_free[1] = wdb_tok[1]

    xT_free = None
    xtok_free = None
    hbuf_free = None
    o4_free = None
    r1_free = None
    fb_free = None
    xrot_free = None
    zt0_free = None
    zt1_free = None
    wub_area_free = None
    y1f_war = None
    state_toks = [tk0] * DC
    wib_free = [None, None, None]
    wfb_free = [None, None, None]
    wob_free = [None, None, None]
    wub_free = [None, None]
    wgb_free = [None, None]
    r2_free = [t for t in fold_free if t is not None]

    def ln_finalize(n_idx, ntb, tb_off):
        tm = I("v", ve.tensor_reduce(lnstat[:, 0:ntb],
                                     accm[:, 0:n_idx].rearrange("p (tb s) -> p tb s", s=4),
                                     mybir.AxisListType.X, ALU.add))
        tq = I("v", ve.tensor_reduce(lnstat[:, TB:TB + ntb],
                                     accs[:, 0:n_idx].rearrange("p (tb s) -> p tb s", s=4),
                                     mybir.AxisListType.X, ALU.add))
        W("v", tm)
        tmn = I("v", ve.tensor_scalar(lnstat[:, 2 * TB:2 * TB + ntb], lnstat[:, 0:ntb],
                                      1.0 / D, None, ALU.mult))
        W("v", tmn)
        tms = I("v", ve.tensor_mul(lnstat[:, 3 * TB:3 * TB + ntb],
                                   lnstat[:, 2 * TB:2 * TB + ntb],
                                   lnstat[:, 2 * TB:2 * TB + ntb]))
        W("v", tq)
        tv1 = I("v", ve.tensor_scalar(lnstat[:, 4 * TB:4 * TB + ntb], lnstat[:, TB:TB + ntb],
                                      1.0 / D, EPS, ALU.mult, ALU.add))
        W("v", tms)
        W("v", tv1)
        tvar = I("v", ve.tensor_sub(lnstat[:, 5 * TB:5 * TB + ntb],
                                    lnstat[:, 4 * TB:4 * TB + ntb],
                                    lnstat[:, 3 * TB:3 * TB + ntb]))
        W("s", tvar)
        tsd = I("s", se.activation(lnstat[:, 6 * TB:6 * TB + ntb],
                                   lnstat[:, 5 * TB:5 * TB + ntb], AF.Sqrt))
        W("v", tsd)
        trs = I("v", ve.reciprocal(lnstat[:, 7 * TB:7 * TB + ntb],
                                   lnstat[:, 6 * TB:6 * TB + ntb]))
        return trs

    def load_xT(p2):
        r2_ = p2 * TT
        for d in range(DC):
            txT = qac.dma(xT[:, d * TT:(d + 1) * TT],
                          xbD[r2_:r2_ + TT, d * P:(d + 1) * P],
                          pre=[xT_free, xb_tok[p2]] if d == 0 else None,
                          group=(d > 0), transpose=True)
        qac.close()
        return txT

    xT_pre = [None] * n_pass
    preA = {}

    for p in range(n_pass):
        warm = (p == 0)
        r0 = p * TT

        if xT_pre[p] is None:
            xT_pre[p] = load_xT(p)
        xT_tok = xT_pre[p]

        if not warm:
            for tb in range(TB):
                txk = qsp.dma(xtok[:, tb * D:(tb + 1) * D],
                              xD[r0 + tb * P:r0 + (tb + 1) * P, :],
                              pre=[xtok_free] if tb == 0 else None, group=(tb > 0))
            qsp.close()
            xtok_tok = txk

        pend_rms = None
        preA_cur = preA if not warm else {}

        def emit_rms(pz):
            e2, kq2, hqa, tok2 = pz
            W("pe", tok2)
            mq = te.matmul(pb[7][0:1, :], ones_c[:], hqa,
                           start=(e2 == 0), stop=(e2 == DC - 1))
            tq2 = I("pe", mq)
            bpool.put(kq2, tq2)
            return tq2

        if not warm:
            W("pe", pb_free[7])
        for e in range(DC):
            bi = e % 3
            if ("i", e) in preA_cur:
                wti = preA_cur[("i", e)]
                wtf = preA_cur[("f", e)]
            else:
                wti = qsp.dma(wib[bi][:].rearrange("p (db q) -> p db q", q=P),
                              wiB.rearrange("eb (db pp) q -> eb pp db q", pp=P)[e],
                              pre=[wub_area_free, wib_free[bi], wiB_tok[e]] if e == 0
                                  else [wib_free[bi], wiB_tok[e]])
                wtf = qsp.dma(wfb[bi][:].rearrange("p (db q) -> p db q", q=P),
                              wfB.rearrange("eb (db pp) q -> eb pp db q", pp=P)[e],
                              pre=[wfb_free[bi], wfB_tok[e]])
            W("pe", wti)
            W("pe", xT_tok)
            if pend_rms is not None:
                rms_tok = emit_rms(pend_rms)
                pend_rms = None
            ki = bank()
            for d in range(DC):
                mi = te.matmul(pb[ki][:], wib[bi][:, d * P:(d + 1) * P],
                               xT[:, d * TT:(d + 1) * TT], start=(d == 0), stop=(d == DC - 1))
            tok_i = I("pe", mi)
            wib_free[bi] = tok_i
            W("pe", wtf)
            kf = bank()
            for d in range(DC):
                mf = te.matmul(pb[kf][:], wfb[bi][:, d * P:(d + 1) * P],
                               xT[:, d * TT:(d + 1) * TT], start=(d == 0), stop=(d == DC - 1))
            tok_f = I("pe", mf)
            wfb_free[bi] = tok_f
            W("s", tok_f)
            ka, a_t = fpool.get("s")
            ta = I("s", se.activation(a_t, pb[kf][:], AF.Sigmoid))
            kn, sn_t = fpool.get("s")
            tsn = I("s", se.activation(sn_t, pb[kf][:], AF.Sigmoid, scale=-1.0))
            pb_free[kf] = tsn
            W("s", tok_i)
            ks, si_t = fpool.get("s")
            tsi = I("s", se.activation(si_t, pb[ki][:], AF.Sigmoid))
            W("v", tsi)
            k1, i1_t = fpool.get("v")
            t1 = I("v", ve.tensor_mul(i1_t, pb[ki][:], si_t))
            pb_free[ki] = t1
            fpool.put(ks, t1)
            W("v", tsn)
            W("v", t1)
            k2, i2_t = fpool.get("v")
            t2 = I("v", ve.tensor_mul(i2_t, i1_t, sn_t))
            fpool.put(k1, t2)
            fpool.put(kn, t2)
            W("v", ta)
            W("v", t2)
            W("v", state_toks[e])
            if warm:
                kh, h_dst = fpool.get("v")
                hview = h_dst
            else:
                W("v", hbuf_free)
                hview = hbuf[:, e * TT:(e + 1) * TT]
            tscan = I("v", ve.tensor_tensor_scan(hview, a_t, i2_t, state[:, e:e + 1],
                                                 ALU.mult, ALU.add))
            fpool.put(ka, tscan)
            fpool.put(k2, tscan)
            W("v", tscan)
            tst = I("v", ve.tensor_copy(state[:, e:e + 1], hview[:, TT - 1:TT]))
            state_toks[e] = tst
            if warm:
                fpool.put(kh, tst)
            else:
                kq, hq_t = bpool.get("v")
                W("v", tscan)
                thq = I("v", ve.tensor_mul(hq_t, hview, hview))
                pend_rms = (e, kq, hq_t, thq)
        if warm:
            xT_free = pr.mark("pe")
            continue
        rms_tok = emit_rms(pend_rms)
        pend_rms = None

        W("v", rms_tok)
        tr = I("v", ve.tensor_scalar(rmsrow[0:1, :], pb[7][0:1, :], 1.0 / D, EPS,
                                     ALU.mult, ALU.add))
        pb_free[7] = tr
        W("s", tr)
        tsq = I("s", se.activation(rstd_b[0:1, :], rmsrow[0:1, :], AF.Sqrt))
        W("v", tsq)
        trc = I("v", ve.reciprocal(rmsrow[0:1, :], rstd_b[0:1, :]))
        W("pe", trc)
        kb = bank()
        tbc = I("pe", te.matmul(pb[kb][:], ones_r[:], rmsrow[0:1, :], start=True, stop=True))
        W("s", tbc)
        trb = I("s", se.copy(rstd_b[:], pb[kb][:]))
        pb_free[kb] = trb

        for e in range(DC):
            bi = e % 3
            wtg = qsp.dma(wib[bi][:].rearrange("p (db q) -> p db q", q=P),
                          wgB.rearrange("eb (db pp) q -> eb pp db q", pp=P)[e],
                          pre=[wib_free[bi], wgB_tok[e]])
            W("pe", wtg)
            kg = bank()
            for d in range(DC):
                mg = te.matmul(pb[kg][:], wib[bi][:, d * P:(d + 1) * P],
                               xT[:, d * TT:(d + 1) * TT], start=(d == 0), stop=(d == DC - 1))
            tok_g = I("pe", mg)
            wib_free[bi] = tok_g
            W("s", tok_g)
            k1, sg1_t = fpool.get("s")
            ts1 = I("s", se.activation(sg1_t, pb[kg][:], AF.Sigmoid))
            W("v", ts1)
            W("v", ld["gnw"])
            k2, sg_t = fpool.get("v")
            tsg = I("v", ve.scalar_tensor_tensor(sg_t, pb[kg][:], gnw[:, e:e + 1], sg1_t,
                                                 ALU.mult, ALU.mult))
            pb_free[kg] = tsg
            fpool.put(k1, tsg)
            W("v", trb)
            W("v", state_toks[e])
            k3, o2_t = fpool.get("v")
            to2 = I("v", ve.tensor_mul(o2_t, hbuf[:, e * TT:(e + 1) * TT], rstd_b[:]))
            if e == 0:
                W("v", o4_free)
            W("v", to2)
            W("v", tsg)
            to4 = I("v", ve.tensor_mul(o4[:, e * TT:(e + 1) * TT], o2_t, sg_t))
            fpool.put(k2, to4)
            fpool.put(k3, to4)
        o4_tok = to4
        hbuf_free = to4
        xT_free = pr.mark("pe")

        evac_last = None
        for sweep in range(2):
            bank_tok = [None] * 8
            for d in range(DC):
                bi = d % 3
                wto = qsp.dma(wob[bi][:],
                              woB[d * P:(d + 1) * P, sweep * 1024:(sweep + 1) * 1024],
                              pre=[wob_free[bi], woB_tok[d // 4]])
                W("pe", wto)
                W("pe", o4_tok)
                for tb in range(TB):
                    for eg in range(2):
                        k = tb * 2 + eg
                        if d == 0:
                            W("pe", pb_free[k])
                        mo = te.matmul(pb[k][:],
                                       o4[:, d * TT + tb * P: d * TT + (tb + 1) * P],
                                       wob[bi][:, eg * TT:(eg + 1) * TT],
                                       start=(d == 0), stop=(d == DC - 1))
                tok_d = I("pe", mo)
                if d == DC - 1:
                    for tb in range(TB):
                        for eg in range(2):
                            bank_tok[tb * 2 + eg] = tok_d
                wob_free[bi] = tok_d
            for tb in range(TB):
                for eg in range(2):
                    k = tb * 2 + eg
                    idx = tb * 4 + sweep * 2 + eg
                    col = tb * D + sweep * 1024 + eg * TT
                    W("v", bank_tok[k])
                    W("v", xtok_tok)
                    if sweep == 0 and tb == 0 and eg == 0:
                        W("v", r1_free)
                    tv = I("v", ve.scalar_tensor_tensor(r1[:, col:col + TT], pb[k][:], 1.0,
                                                        xtok[:, col:col + TT],
                                                        ALU.mult, ALU.add,
                                                        accum_out=accm[:, idx:idx + 1]))
                    pb_free[k] = tv
                    W("s", tv)
                    ksr, scr_t = fpool.get("s")
                    tss = I("s", se.activation(scr_t, r1[:, col:col + TT], AF.Square,
                                               accum_out=accs[:, idx:idx + 1]))
                    fpool.put(ksr, tss)
                    evac_last = (tv, tss)
        o4_free = pr.mark("pe")
        W("v", evac_last[0])
        W("v", evac_last[1])
        trs = ln_finalize(16, TB, 0)
        y1_toks = []
        for tb in range(TB):
            W("v", trs)
            ty = I("v", ve.tensor_scalar(r1[:, tb * D:(tb + 1) * D], r1[:, tb * D:(tb + 1) * D],
                                         lnstat[:, 2 * TB + tb:2 * TB + tb + 1],
                                         lnstat[:, 7 * TB + tb:7 * TB + tb + 1],
                                         ALU.subtract, ALU.mult))
            y1_toks.append(ty)
        xtok_free = y1_toks[-1]
        y1f_dma = []
        for tb in range(TB):
            tf = qac.dma(y1fD[tb * P:(tb + 1) * P, :], r1[:, tb * D:(tb + 1) * D],
                         pre=[y1_toks[tb], y1f_war])
            y1f_dma.append(tf)
        r1_free = list(y1f_dma)
        y1f_tok = list(y1f_dma)
        if p + 1 < n_pass:
            xT_pre[p + 1] = load_xT(p + 1)

        if p == 1:
            emit_phase2_upgt()
        xrot_toks = []

        def emit_h128d(pz):
            d2, kx2, x1a, tok2 = pz
            W("pe", tok2)
            W("pe", ld["h128"])
            kz = bank()
            tmz = I("pe", te.matmul(pb[kz][:, 0:HALF], h128[:], x1a[:, 0:HALF],
                                    start=True, stop=True))
            bpool.put(kx2, tmz)
            W("s", tmz)
            if th == 0 and d2 == 0:
                W("s", o4_free)
                W("s", fb_free)
            if th == 1 and d2 == 0:
                W("s", xrot_toks[0])
            tev2 = I("s", se.copy(fb[:, d2 * HALF:(d2 + 1) * HALF], pb[kz][:, 0:HALF]))
            pb_free[kz] = tev2
            return tev2

        for th in range(2):
            pendD = None
            for d in range(DC):
                W("pe", y1_toks[2 * th])
                W("pe", y1_toks[2 * th + 1])
                W("pe", ld["ident"])
                kz2 = bank()
                for tbl in range(2):
                    tb2 = 2 * th + tbl
                    ttr = I("pe", te.transpose(
                        pb[kz2][:, tbl * P:(tbl + 1) * P],
                        r1[:, tb2 * D + d * P: tb2 * D + (d + 1) * P], ident[:]))
                W("s", ttr)
                W("s", ld["suug"])
                kx, x1_t = bpool.get("s")
                tx1 = I("s", se.activation(x1_t[:, 0:HALF], pb[kz2][:, 0:HALF],
                                           AF.Copy, scale=suug[:, d:d + 1]))
                pb_free[kz2] = tx1
                if pendD is not None:
                    tev = emit_h128d(pendD)
                pendD = (d, kx, x1_t, tx1)
            tev = emit_h128d(pendD)
            pendD = None
            fb_tok = tev
            src_b, dst_b = fb, fb2
            tok = fb_tok
            for li, s_ in enumerate([1, 2, 4, 8]):
                blk = s_ * HALF
                vin = src_b[:].rearrange("p (a two bt) -> p a two bt", two=2, bt=blk)
                vout = dst_b[:].rearrange("p (a two bt) -> p a two bt", two=2, bt=blk)
                W("v", tok)
                tp = I("v", ve.tensor_add(vout[:, :, 0, :], vin[:, :, 0, :], vin[:, :, 1, :]))
                tok = I("v", ve.tensor_sub(vout[:, :, 1, :], vin[:, :, 0, :], vin[:, :, 1, :]))
                src_b, dst_b = dst_b, src_b
            W("v", tok)
            if th == 0:
                W("v", xrot_free)
            vz_in = src_b[:].rearrange("p (c t) -> p c t", t=HALF)
            vz_out = xrot[:].rearrange("p (c tt) -> p c tt", tt=TT)[:, :, th * HALF:(th + 1) * HALF]
            txr = I("v", ve.tensor_copy(vz_out, vz_in))
            xrot_toks.append(txr)
        xrot_tok = xrot_toks[-1]
        fb_free = xrot_tok
        r1_free = r1_free + [pr.mark("pe")]

        pend = None
        zt_toks = [None, None]

        def emit_h128_evac(pz):
            ic2, k42, t4a, tok42 = pz
            W("pe", tok42)
            kz = bank()
            tmz = I("pe", te.matmul(pb[kz][:], h128[:], t4a, start=True, stop=True))
            bpool.put(k42, tmz)
            W("s", tmz)
            if ic2 == 0:
                W("s", zt0_free)
                W("s", fb_free)
            te1 = I("s", se.copy(zt0[:, ic2 * HALF:(ic2 + 1) * HALF], pb[kz][:, 0:HALF]))
            if ic2 == 0:
                W("s", zt1_free)
                W("s", r1_free)
            te2 = I("s", se.copy(zt1[:, ic2 * HALF:(ic2 + 1) * HALF], pb[kz][:, HALF:TT]))
            pb_free[kz] = te2
            zt_toks[0] = te1
            zt_toks[1] = te2

        for igg in range(32):
            gi = igg % 2
            pre_u = [wub_free[gi], wub_area_free] + upT_tok[igg // 8]
            if igg == 0:
                pre_u = pre_u + wib_free + wfb_free + wob_free
            up_tok = qsp.dma(wub[gi][:], wupT[igg], pre=pre_u)
            gt_tok = qsp.dma(wgb[gi][:], wgtT[igg],
                             pre=[wgb_free[gi]] + gtT_tok[igg // 8])
            for icl in range(2):
                ic = igg * 2 + icl
                W("pe", up_tok)
                W("pe", xrot_tok)
                ku = bank()
                for d in range(DC):
                    mu = te.matmul(pb[ku][:],
                                   wub[gi][:, d * HALF + icl * P: d * HALF + (icl + 1) * P],
                                   xrot[:, d * TT:(d + 1) * TT], start=(d == 0), stop=(d == DC - 1))
                tok_u = I("pe", mu)
                W("pe", gt_tok)
                kg2 = bank()
                for d in range(DC):
                    mgt = te.matmul(pb[kg2][:],
                                    wgb[gi][:, d * HALF + icl * P: d * HALF + (icl + 1) * P],
                                    xrot[:, d * TT:(d + 1) * TT], start=(d == 0), stop=(d == DC - 1))
                tok_g2 = I("pe", mgt)
                if icl == 1:
                    wub_free[gi] = tok_u
                    wgb_free[gi] = tok_g2
                if pend is not None:
                    emit_h128_evac(pend)
                    pend = None
                W("s", tok_g2)
                k1, t1_t = fpool.get("s")
                tt1 = I("s", se.activation(t1_t, pb[kg2][:], AF.Sigmoid))
                W("v", tt1)
                k2, t2_t = fpool.get("v")
                tt2 = I("v", ve.tensor_mul(t2_t, pb[kg2][:], t1_t))
                pb_free[kg2] = tt2
                fpool.put(k1, tt2)
                W("v", tok_u)
                W("v", tt2)
                k3, t3_t = fpool.get("v")
                tt3 = I("v", ve.tensor_mul(t3_t, pb[ku][:], t2_t))
                pb_free[ku] = tt3
                fpool.put(k2, tt3)
                W("v", sdp_tok)
                W("v", tt3)
                k4, t4_t = bpool.get("v")
                tt4 = I("v", ve.tensor_scalar(t4_t, t3_t, sdpc[:, ic:ic + 1], None, ALU.mult))
                fpool.put(k3, tt4)
                pend = (ic, k4, t4_t, tt4)
        emit_h128_evac(pend)
        pend = None
        xrot_free = pr.mark("pe")
        wub_area_free = pr.mark("pe")

        if p == 1:
            emit_phase2_dn()
        preA = {}
        if p + 1 < n_pass:
            for e2 in range(3):
                preA[("i", e2)] = qsp.dma(
                    wib[e2 % 3][:].rearrange("p (db q) -> p db q", q=P),
                    wiB.rearrange("eb (db pp) q -> eb pp db q", pp=P)[e2],
                    pre=[wub_area_free, wib_free[e2 % 3], wiB_tok[e2]])
                preA[("f", e2)] = qsp.dma(
                    wfb[e2 % 3][:].rearrange("p (db q) -> p db q", q=P),
                    wfB.rearrange("eb (db pp) q -> eb pp db q", pp=P)[e2],
                    pre=[wfb_free[e2 % 3], wfB_tok[e2]])

        for half in range(2):
            ztX = zt0 if half == 0 else zt1
            if half == 0:
                y1l_toks = []
                for tb in range(TB):
                    t = qsp.dma(xtok[:, tb * D:(tb + 1) * D], y1fD[tb * P:(tb + 1) * P, :],
                                pre=(y1f_tok + [xtok_free, xrot_free]) if tb == 0 else None,
                                group=(tb > 0))
                    y1l_toks.append(t)
                qsp.close()
                y1l_toks = [y1l_toks[-1]] * TB
            bank_tok = [None] * 8
            for c in range(IC):
                bi = c % 2
                wtd = qsp.dma(wdb[bi][:], wdnT[c],
                              pre=[wdb_free[bi]] + (sem_maxima(dnT_tok) if c < 2 else []))
                W("pe", wtd)
                W("pe", zt_toks[half])
                for tbl in range(2):
                    for jg in range(4):
                        k = tbl * 4 + jg
                        if c == 0:
                            W("pe", pb_free[k])
                        md = te.matmul(pb[k][:],
                                       ztX[:, c * HALF + tbl * P: c * HALF + (tbl + 1) * P],
                                       wdb[bi][:, jg * TT:(jg + 1) * TT],
                                       start=(c == 0), stop=(c == IC - 1))
                tok_c = I("pe", md)
                if c == IC - 1:
                    for k in range(8):
                        bank_tok[k] = tok_c
                wdb_free[bi] = tok_c
            ev_last = None
            for tbl in range(2):
                tb = half * 2 + tbl
                for jg in range(4):
                    k = tbl * 4 + jg
                    idx = tbl * 4 + jg
                    col = tbl * D + jg * TT
                    W("v", bank_tok[k])
                    W("v", y1l_toks[tb])
                    if tbl == 0 and jg == 0:
                        W("v", r2_free)
                    tv = I("v", ve.scalar_tensor_tensor(r2[:, col:col + TT], pb[k][:], 1.0,
                                                        xtok[:, tb * D + jg * TT: tb * D + (jg + 1) * TT],
                                                        ALU.mult, ALU.add,
                                                        accum_out=accm[:, idx:idx + 1]))
                    pb_free[k] = tv
                    W("s", tv)
                    ksr, scr_t = fpool.get("s")
                    tss = I("s", se.activation(scr_t, r2[:, col:col + TT], AF.Square,
                                               accum_out=accs[:, idx:idx + 1]))
                    fpool.put(ksr, tss)
                    ev_last = (tv, tss)
            W("v", ev_last[0])
            W("v", ev_last[1])
            trs2 = ln_finalize(8, 2, 0)
            out_row0 = (p - 1) * TT + half * HALF
            r2_free = []
            for tbl in range(2):
                W("v", trs2)
                ty2 = I("v", ve.tensor_scalar(r2[:, tbl * D:(tbl + 1) * D],
                                              r2[:, tbl * D:(tbl + 1) * D],
                                              lnstat[:, 2 * TB + tbl:2 * TB + tbl + 1],
                                              lnstat[:, 7 * TB + tbl:7 * TB + tbl + 1],
                                              ALU.subtract, ALU.mult))
                tdo = qdo.dma(outD[out_row0 + tbl * P: out_row0 + (tbl + 1) * P, :],
                              r2[:, tbl * D:(tbl + 1) * D], pre=[ty2])
                r2_free.append(tdo)
            if half == 0:
                zt0_free = pr.mark("pe")
            else:
                zt1_free = pr.mark("pe")
        hbuf_free = [hbuf_free, zt1_free]
        xtok_free = pr.mark("v")
        y1f_war = y1l_toks[-1]
        o4_free = zt0_free
        fb_free = zt0_free
        xrot_free = pr.mark("pe")

    for k, v in qdo.all_last():
        sy.wait_ge(pr.sem[k], v)
        ge.wait_ge(pr.sem[k], v)
    es.close()
    assert nc.sbuf_base <= 17408, f"auto allocs overflow reserve: {nc.sbuf_base}"
    return nc


_NC_CACHE = {}


def _get_nc(n_full=8):
    if n_full not in _NC_CACHE:
        _NC_CACHE[n_full] = build_nc(n_full)
    return _NC_CACHE[n_full]


def make_aux(full):
    return {
        "wi": np.ascontiguousarray(full["L0_attn_Wi"], dtype=np.float32),
        "wf": np.ascontiguousarray(full["L0_attn_Wf"], dtype=np.float32),
        "wg": np.ascontiguousarray(full["L0_attn_Wg"], dtype=np.float32),
        "wo": np.ascontiguousarray(full["L0_attn_Wo"], dtype=np.float32),
        "wup": np.ascontiguousarray(full["L0_ffn_W_up"], dtype=np.float32),
        "wgt": np.ascontiguousarray(full["L0_ffn_W_gate"], dtype=np.float32),
        "wdn": np.ascontiguousarray(full["L0_ffn_W_down"], dtype=np.float32),
        "gnw": np.ascontiguousarray(np.asarray(full["L0_attn_g_norm_weight"], dtype=np.float32).reshape(DC, P).T),
        "suug": np.ascontiguousarray(np.asarray(full["L0_ffn_SU_ug"], dtype=np.float32).reshape(DC, P).T),
        "sudp": np.ascontiguousarray(np.asarray(full["L0_ffn_SU_dp"], dtype=np.float32).reshape(IC, P).T),
        "wsu": np.ascontiguousarray(np.asarray(full["L0_ffn_Wscale_ug"], dtype=np.float32)[:INTER].reshape(IC, P).T),
        "wsg": np.ascontiguousarray(np.asarray(full["L0_ffn_Wscale_ug"], dtype=np.float32)[INTER:].reshape(IC, P).T),
        "wsd": np.ascontiguousarray(np.asarray(full["L0_ffn_Wscale_dp"], dtype=np.float32).reshape(DC, P).T),
        "h128": _hadamard128(),
        "ident": np.eye(P, dtype=np.float32),
    }


def kernel(**inputs):
    n_full = 8
    ct = n_full * TT
    rows = (n_full + 1) * TT
    nc = _get_nc(n_full)
    x = np.asarray(inputs["L0_attn_hidden_states"])
    aux = make_aux(inputs)
    in_maps = []
    for c in range(8):
        b, q = c // 4, c % 4
        xs = np.zeros((rows, D), np.float32)
        t0 = q * ct
        w0 = max(0, t0 - NWARM)
        nw = t0 - w0
        if nw > 0:
            xs[NWARM - nw:NWARM] = x[b, w0:t0]
        xs[NWARM:] = x[b, t0:t0 + ct]
        m = dict(aux)
        m["x"] = xs
        in_maps.append(m)
    res = run_bass_kernel_spmd(nc, in_maps, core_ids=list(range(8)))
    out = np.empty((2, 16384, D), np.float32)
    for c in range(8):
        b, q = c // 4, c % 4
        out[b, q * ct:(q + 1) * ct] = res.results[c]["out"]
    return out


# version 10
# speedup vs baseline: 1.1542x; 1.1542x over previous
import numpy as np
import ml_dtypes

import concourse.bass as bass
import concourse.mybir as mybir
from concourse.bass_utils import run_bass_kernel_spmd

F32 = mybir.dt.float32
BF16 = mybir.dt.bfloat16
AF = mybir.ActivationFunctionType
ALU = mybir.AluOpType
BF = ml_dtypes.bfloat16

P = 128
D = 2048
DC = 16
INTER = 8192
IC = 64
TT = 512
TB = 4
HALF = 256
NWARM = 512
S0 = 1.0 / (64.0 * np.sqrt(2048.0))
EPS = 1e-5
ENG = ("pe", "v", "s", "g", "sp")


def _hadamard128():
    i = np.arange(128)
    pc = np.zeros((128, 128), np.int64)
    for b in range(7):
        pc += ((i[:, None] >> b) & 1) * ((i[None, :] >> b) & 1)
    return np.where(pc % 2 == 0, 1.0, -1.0).astype(BF)


class Prog:
    def __init__(self, nc, es):
        self.nc = nc
        self.sem = {}
        self.cnt = {}
        for name in ("pe", "v", "s", "g"):
            self.sem[name] = es.enter_context(nc.semaphore(name))
            self.cnt[name] = 0
        self.wm = {e: {} for e in ENG}
        self.eng = {"pe": nc.tensor, "v": nc.vector, "s": nc.scalar,
                    "g": nc.gpsimd, "sp": nc.sync}

    def wait(self, eng, tok):
        if tok is None:
            return
        if isinstance(tok, list):
            for t in tok:
                self.wait(eng, t)
            return
        sname, val = tok
        if self.wm[eng].get(sname, 0) >= val:
            return
        self.eng[eng].wait_ge(self.sem[sname], val)
        self.wm[eng][sname] = val

    def inc(self, eng, inst):
        self.cnt[eng] += 1
        inst.then_inc(self.sem[eng], 1)
        return (eng, self.cnt[eng])

    def mark(self, eng):
        return (eng, self.cnt[eng])


class DQ:

    def __init__(self, pr, eng_name, basename, es, nsem):
        self.pr = pr
        self.eng_name = eng_name
        self.eng = pr.eng[eng_name]
        self.names = []
        for i in range(nsem):
            name = f"{basename}{i}"
            pr.sem[name] = es.enter_context(pr.nc.semaphore(name))
            pr.cnt[name] = 0
            self.names.append(name)
        self.i = 0
        self.cur = None

    def dma(self, out, in_, pre=None, group=False, **kw):
        pr = self.pr
        if group and self.cur is not None:
            k = self.cur
        else:
            k = self.names[self.i]
            self.i = (self.i + 1) % len(self.names)
            if pr.cnt[k] > 0:
                pr.wait(self.eng_name, (k, pr.cnt[k]))
            self.cur = k
        pr.wait(self.eng_name, pre)
        inst = self.eng.dma_start(out=out, in_=in_, **kw)
        pr.cnt[k] += 16
        inst.then_inc(pr.sem[k], 16)
        return (k, pr.cnt[k])

    def close(self):
        self.cur = None

    def all_last(self):
        return [(k, self.pr.cnt[k]) for k in self.names if self.pr.cnt[k] > 0]


class Pool:

    def __init__(self, pr, tiles):
        self.pr = pr
        self.tiles = tiles
        self.free = [None] * len(tiles)
        self.i = 0

    def get(self, eng):
        k = self.i
        self.i = (self.i + 1) % len(self.tiles)
        self.pr.wait(eng, self.free[k])
        return k, self.tiles[k]

    def put(self, k, tok):
        self.free[k] = tok


def sem_maxima(toks):
    seen = {}
    for t_ in toks:
        if t_ is None:
            continue
        seen[t_[0]] = max(seen.get(t_[0], 0), t_[1])
    return [(k, v) for k, v in seen.items()]


def build_nc(n_full=8):
    nc = bass.Bass("TRN2", target_bir_lowering=False)
    n_pass = n_full + 1
    rows = n_pass * TT

    xD = nc.dram_tensor("x", [rows, D], F32, kind="ExternalInput").ap()
    wiD = nc.dram_tensor("wi", [D, D], F32, kind="ExternalInput").ap()
    wfD = nc.dram_tensor("wf", [D, D], F32, kind="ExternalInput").ap()
    wgD = nc.dram_tensor("wg", [D, D], F32, kind="ExternalInput").ap()
    woD = nc.dram_tensor("wo", [D, D], F32, kind="ExternalInput").ap()
    wupD = nc.dram_tensor("wup", [INTER, D], F32, kind="ExternalInput").ap()
    wgtD = nc.dram_tensor("wgt", [INTER, D], F32, kind="ExternalInput").ap()
    wdnD = nc.dram_tensor("wdn", [D, INTER], F32, kind="ExternalInput").ap()
    gnwD = nc.dram_tensor("gnw", [P, DC], F32, kind="ExternalInput").ap()
    suugD = nc.dram_tensor("suug", [P, DC], F32, kind="ExternalInput").ap()
    sudpD = nc.dram_tensor("sudp", [P, IC], F32, kind="ExternalInput").ap()
    wsuD = nc.dram_tensor("wsu", [P, IC], F32, kind="ExternalInput").ap()
    wsgD = nc.dram_tensor("wsg", [P, IC], F32, kind="ExternalInput").ap()
    wsdD = nc.dram_tensor("wsd", [P, DC], F32, kind="ExternalInput").ap()
    h128D = nc.dram_tensor("h128", [P, P], BF16, kind="ExternalInput").ap()
    identD = nc.dram_tensor("ident", [P, P], F32, kind="ExternalInput").ap()
    outD = nc.dram_tensor("out", [n_full * TT, D], F32, kind="ExternalOutput").ap()

    xbD = nc.dram_tensor("xb", [rows, D], BF16, kind="Internal").ap()
    wiB = nc.dram_tensor("wiB", [DC, D, P], BF16, kind="Internal").ap()
    wfB = nc.dram_tensor("wfB", [DC, D, P], BF16, kind="Internal").ap()
    wgB = nc.dram_tensor("wgB", [DC, D, P], BF16, kind="Internal").ap()
    woB = nc.dram_tensor("woB", [D, D], BF16, kind="Internal").ap()
    wupB = nc.dram_tensor("wupB", [INTER, D], BF16, kind="Internal").ap()
    wgtB = nc.dram_tensor("wgtB", [INTER, D], BF16, kind="Internal").ap()
    wdnB = nc.dram_tensor("wdnB", [D, INTER], BF16, kind="Internal").ap()
    wdnF = nc.dram_tensor("wdnF", [D, INTER], F32, kind="Internal").ap()
    wupT = nc.dram_tensor("wupT", [32, P, DC * HALF], BF16, kind="Internal").ap()
    wgtT = nc.dram_tensor("wgtT", [32, P, DC * HALF], BF16, kind="Internal").ap()
    wdnT = nc.dram_tensor("wdnT", [IC, P, 2048], BF16, kind="Internal").ap()

    from contextlib import ExitStack
    es = ExitStack()
    pr = Prog(nc, es)
    W, I = pr.wait, pr.inc
    te, ve, se, ge, sy = nc.tensor, nc.vector, nc.scalar, nc.gpsimd, nc.sync
    qsp = DQ(pr, "sp", "qw", es, 8)
    qac = DQ(pr, "s", "qa", es, 8)
    qgp = DQ(pr, "g", "qg", es, 6)
    qdo = DQ(pr, "sp", "qo", es, 2)

    off = [17408]

    def at(name, shape, dtype, o=None):
        nonlocal off
        nbytes = int(np.prod(shape[1:])) * mybir.dt.size(dtype)
        if o is None:
            o = off[0]
            off[0] += (nbytes + 31) // 32 * 32
        return nc.alloc_sbuf_tensor_at(name, shape, dtype, offset=o)

    h128 = at("h128", [P, P], BF16)
    ident = at("ident", [P, P], F32)
    gnw = at("gnw", [P, DC], F32)
    suug = at("suug", [P, DC], F32)
    sdpc = at("sdpc", [P, IC], F32)
    wsu = at("wsu", [P, IC], F32)
    wsg = at("wsg", [P, IC], F32)
    wsd = at("wsd", [P, DC], F32)
    ones_c = at("ones_c", [P, 1], BF16)
    ones_r = at("ones_r", [1, P], F32)
    state = at("state", [P, DC], F32)
    rmsrow = at("rmsrow", [1, TT], F32)
    rstd_b = at("rstd_b", [P, TT], F32)
    accm = at("accm", [P, 16], F32)
    accs = at("accs", [P, 16], F32)
    lnstat = at("lnstat", [P, 10 * TB], F32)
    wdb = [at(f"wdb{i}", [P, 2048], BF16) for i in range(2)]
    fpool_t = [at(f"fp{i}", [P, TT], F32) for i in range(7)]
    bpool_t = [at(f"bp{i}", [P, TT], BF16) for i in range(4)]
    prepb = [at(f"prepb{i}", [P, 1024], BF16) for i in range(2)]
    base_small = off[0]
    assert base_small <= 65504, base_small

    OFF_W = 65504
    OFF_X = 98272
    OFF_H = 131040
    OFF_Z = 163808
    OFF_WS = 196576
    OFF_XT = 212960

    wib = [at(f"wib{i}", [P, D], BF16, OFF_W + i * 4096) for i in range(3)]
    wfb = [at(f"wfb{i}", [P, D], BF16, OFF_W + 12288 + i * 4096) for i in range(3)]
    wob = [at(f"wob{i}", [P, 1024], BF16, OFF_W + 24576 + i * 2048) for i in range(3)]
    wub = [at(f"wub{i}", [P, DC * HALF], BF16, OFF_W + i * 8192) for i in range(2)]
    wgb = [at(f"wgb{i}", [P, DC * HALF], BF16, OFF_W + 16384 + i * 8192) for i in range(2)]

    xtok = at("xtok", [P, TB * D], F32, OFF_X)
    zt1 = at("zt1", [P, IC * HALF], BF16, OFF_X)

    hbuf = at("hbuf", [P, DC * TT], F32, OFF_H)
    r1 = at("r1", [P, TB * D], F32, OFF_H)

    o4 = at("o4", [P, DC * TT], BF16, OFF_Z)
    fb = at("fb", [P, DC * HALF], F32, OFF_Z)
    fb2 = at("fb2", [P, DC * HALF], F32, OFF_Z + 16 * 1024)
    zt0 = at("zt0", [P, IC * HALF], BF16, OFF_Z)

    foldf = [at(f"foldf{i}", [P, 1024], F32, OFF_WS + i * 4096) for i in range(2)]
    foldf2 = [at(f"foldf2{i}", [P, 1024], F32, OFF_WS + 8192 + i * 4096) for i in range(2)]
    wdbg = [at(f"wdbg{i}", [P, 1024], BF16, OFF_WS + i * 2048) for i in range(6)]

    xT = at("xT", [P, DC * TT], BF16, OFF_XT)
    xrot = at("xrot", [P, DC * TT], BF16, OFF_XT)

    pb = [nc.alloc_psum_tensor(f"pb{i}", [P, TT], F32) for i in range(8)]
    pb_free = [None] * 8
    rr = [0]

    def bank(eng="pe", k=None):
        if k is None:
            k = rr[0]
            rr[0] = (rr[0] + 1) % 6
        W(eng, pb_free[k])
        return k

    fpool = Pool(pr, [t[:] for t in fpool_t])
    bpool = Pool(pr, [t[:] for t in bpool_t])

    ld = {}
    ldt = None
    for name, dst, srcT in [("gnw", gnw, gnwD), ("suug", suug, suugD), ("sdp", sdpc, sudpD),
                            ("wsu", wsu, wsuD), ("wsg", wsg, wsgD), ("wsd", wsd, wsdD),
                            ("h128", h128, h128D), ("ident", ident, identD)]:
        ldt = qsp.dma(dst[:], srcT, group=(ldt is not None))
    for name in ("gnw", "suug", "sdp", "wsu", "wsg", "wsd", "h128", "ident"):
        ld[name] = ldt
    qsp.close()

    ve.memset(ones_c[:], 1.0)
    ve.memset(ones_r[:], 1.0)
    tk0 = I("v", ve.memset(state[:], 0.0))
    W("v", ld["sdp"])
    sdp_tok = I("v", ve.tensor_scalar(sdpc[:], sdpc[:], S0 / 2.0, None, ALU.mult))
    W("v", ld["wsu"])
    wsu_tok = I("v", ve.tensor_scalar(wsu[:], wsu[:], 64.0 * S0, None, ALU.mult))
    W("v", ld["wsg"])
    wsg_tok = I("v", ve.tensor_scalar(wsg[:], wsg[:], 64.0 * S0, None, ALU.mult))
    W("v", ld["wsd"])
    wsd_tok = I("v", ve.tensor_scalar(wsd[:], wsd[:], 64.0, None, ALU.mult))

    xb_tok = [None] * n_pass
    for p in range(3):
        xb_tok[p] = qgp.dma(xbD[p * TT:(p + 1) * TT, :], xD[p * TT:(p + 1) * TT, :])
    wiB_tok, wfB_tok, wgB_tok = [], [], []
    for eb in range(DC):
        s_ = wiD.rearrange("d (eb q) -> eb d q", q=P)[eb]
        wiB_tok.append(qgp.dma(wiB[eb], s_))
        s_ = wfD.rearrange("d (eb q) -> eb d q", q=P)[eb]
        wfB_tok.append(qgp.dma(wfB[eb], s_))
    for eb in range(DC):
        s_ = wgD.rearrange("d (eb q) -> eb d q", q=P)[eb]
        wgB_tok.append(qgp.dma(wgB[eb], s_))
    woB_tok = []
    for b in range(4):
        woB_tok.append(qgp.dma(woB[b * TT:(b + 1) * TT, :], woD[b * TT:(b + 1) * TT, :]))

    cast_jobs = []
    upB_tok = [[] for _ in range(4)]
    gtB_tok = [[] for _ in range(4)]
    for ib in range(4):
        for j in range(4):
            r0 = ib * 2048 + j * 512
            cast_jobs.append(("up", ib, r0))
        for j in range(4):
            r0 = ib * 2048 + j * 512
            cast_jobs.append(("gt", ib, r0))
    for p in range(3, n_pass):
        cast_jobs.append(("x", p, p * TT))
    cj_i = [0]

    def pump_cast(n):
        while n > 0 and cj_i[0] < len(cast_jobs):
            kind, a, r0 = cast_jobs[cj_i[0]]
            cj_i[0] += 1
            n -= 1
            if kind == "up":
                upB_tok[a].append(qgp.dma(wupB[r0:r0 + 512, :], wupD[r0:r0 + 512, :]))
            elif kind == "gt":
                gtB_tok[a].append(qgp.dma(wgtB[r0:r0 + 512, :], wgtD[r0:r0 + 512, :]))
            else:
                xb_tok[a] = qgp.dma(xbD[r0:r0 + TT, :], xD[r0:r0 + TT, :])

    fold1_tok = [[] for _ in range(DC)]
    fold2_tok = [[] for _ in range(DC)]
    fold_free = [None, None]
    for sweep in range(2):
        for jb in range(DC):
            for grp in range(8):
                pi3 = (jb * 8 + grp) % 2
                if sweep == 0:
                    src_ap = wdnD[jb * P:(jb + 1) * P, grp * 1024:(grp + 1) * 1024]
                    dst_ap = wdnF[jb * P:(jb + 1) * P, grp * 1024:(grp + 1) * 1024]
                    pre = [fold_free[pi3]]
                else:
                    gath = wdnF[jb * P:(jb + 1) * P, :].rearrange(
                        "p (g sub q) -> p sub g q", g=8, q=P)[:, grp, :, :]
                    src_ap = gath
                    dst_ap = gath
                    pre = [fold_free[pi3]] + (fold1_tok[jb] if grp == 0 else [])
                lt3 = qgp.dma(foldf[pi3][:], src_ap, pre=pre)
                fa, fz = foldf[pi3], foldf2[pi3]
                tokf = lt3
                for s_ in (1, 2, 4):
                    blklen = s_ * P
                    vin = fa[:].rearrange("p (a two bt) -> p a two bt", two=2, bt=blklen)
                    vout = fz[:].rearrange("p (a two bt) -> p a two bt", two=2, bt=blklen)
                    W("g", tokf)
                    I("g", ge.tensor_add(vout[:, :, 0, :], vin[:, :, 0, :], vin[:, :, 1, :]))
                    tokf = I("g", ge.tensor_sub(vout[:, :, 1, :], vin[:, :, 0, :], vin[:, :, 1, :]))
                    fa, fz = fz, fa
                st3 = qgp.dma(dst_ap, fa[:], pre=[tokf])
                fold_free[pi3] = st3
                (fold1_tok if sweep == 0 else fold2_tok)[jb].append(st3)
                pump_cast(1)
    pump_cast(len(cast_jobs))

    wdn_toks = []
    dnsc_free = [None, None]
    for blk in range(DC):
        for piece in range(8):
            pi = (blk * 8 + piece) % 2
            lt = qgp.dma(foldf[pi][:], wdnF[blk * P:(blk + 1) * P, piece * 1024:(piece + 1) * 1024],
                         pre=[dnsc_free[pi], fold_free[pi]] + (fold2_tok[blk] if piece == 0 else []))
            W("g", lt)
            W("g", wsd_tok)
            gt_ = I("g", ge.tensor_scalar(prepb[pi][:], foldf[pi][:], wsd[:, blk:blk + 1],
                                          None, ALU.mult))
            st = qgp.dma(wdnB[blk * P:(blk + 1) * P, piece * 1024:(piece + 1) * 1024],
                         prepb[pi][:], pre=[gt_])
            dnsc_free[pi] = st
            wdn_toks.append(st)
    wdn_last = sem_maxima(wdn_toks)

    wdb_tok = [None, None]
    upT_tok = [[] for _ in range(4)]
    gtT_tok = [[] for _ in range(4)]
    retile_jobs = []
    for ib in range(4):
        for mat in ("up", "gt"):
            for dch in range(DC):
                retile_jobs.append((mat, ib, dch))
    rj_i = [0]

    def pump_retile(n):
        while n > 0 and rj_i[0] < len(retile_jobs):
            mat, ib, dch = retile_jobs[rj_i[0]]
            rj_i[0] += 1
            n -= 1
            pi2 = rj_i[0] % 2
            if mat == "up":
                srcB, dstT, ctoks, toks_out = wupB, wupT, upB_tok[ib], upT_tok
            else:
                srcB, dstT, ctoks, toks_out = wgtB, wgtT, gtB_tok[ib], gtT_tok
            tt = qac.dma(wdb[pi2][:],
                         srcB[ib * 2048:(ib + 1) * 2048, dch * P:(dch + 1) * P],
                         pre=[wdb_tok[pi2]] + ctoks, transpose=True)
            st2 = qac.dma(dstT[ib * 8:(ib + 1) * 8, :, dch * HALF:(dch + 1) * HALF]
                          .rearrange("g p t -> p g t"),
                          wdb[pi2][:].rearrange("p (g t) -> p g t", t=HALF),
                          pre=[tt])
            wdb_tok[pi2] = st2
            toks_out[ib].append(st2)

    xT_free = None
    xtok_war = None
    hbuf_war = None
    o4_war = None
    o4_free = None
    fb_free = None
    xrot_free = None
    state_toks = [tk0] * DC
    wib_free = [None, None, None]
    wfb_free = [None, None, None]
    wob_free = [None, None, None]
    wub_free = [None, None]
    wgb_free = [None, None]
    wdbg_free = [None] * 6
    grr = [0]
    dnT_tok = []
    dnT_last = []
    upT_fin = None
    gtT_fin = None

    def ln_finalize(n_idx, ntb):
        tm = I("v", ve.tensor_reduce(lnstat[:, 0:ntb],
                                     accm[:, 0:n_idx].rearrange("p (tb s) -> p tb s", s=4),
                                     mybir.AxisListType.X, ALU.add))
        tq = I("v", ve.tensor_reduce(lnstat[:, TB:TB + ntb],
                                     accs[:, 0:n_idx].rearrange("p (tb s) -> p tb s", s=4),
                                     mybir.AxisListType.X, ALU.add))
        W("v", tm)
        tmn = I("v", ve.tensor_scalar(lnstat[:, 2 * TB:2 * TB + ntb], lnstat[:, 0:ntb],
                                      1.0 / D, None, ALU.mult))
        W("v", tmn)
        tms = I("v", ve.tensor_mul(lnstat[:, 3 * TB:3 * TB + ntb],
                                   lnstat[:, 2 * TB:2 * TB + ntb],
                                   lnstat[:, 2 * TB:2 * TB + ntb]))
        W("v", tq)
        tv1 = I("v", ve.tensor_scalar(lnstat[:, 4 * TB:4 * TB + ntb], lnstat[:, TB:TB + ntb],
                                      1.0 / D, EPS, ALU.mult, ALU.add))
        W("v", tms)
        W("v", tv1)
        tvar = I("v", ve.tensor_sub(lnstat[:, 5 * TB:5 * TB + ntb],
                                    lnstat[:, 4 * TB:4 * TB + ntb],
                                    lnstat[:, 3 * TB:3 * TB + ntb]))
        W("s", tvar)
        tsd = I("s", se.activation(lnstat[:, 6 * TB:6 * TB + ntb],
                                   lnstat[:, 5 * TB:5 * TB + ntb], AF.Sqrt))
        W("v", tsd)
        trs = I("v", ve.reciprocal(lnstat[:, 7 * TB:7 * TB + ntb],
                                   lnstat[:, 6 * TB:6 * TB + ntb]))
        return trs

    def load_xT(p2, extra_pre=None):
        r2_ = p2 * TT
        for d in range(DC):
            pre0 = None
            if d == 0:
                pre0 = [xT_free, xb_tok[p2]] + (extra_pre or [])
            txT = qac.dma(xT[:, d * TT:(d + 1) * TT],
                          xbD[r2_:r2_ + TT, d * P:(d + 1) * P],
                          pre=pre0, group=(d > 0), transpose=True)
        qac.close()
        return txT

    xT_pre = [None] * n_pass
    preA = {}

    for p in range(n_pass):
        warm = (p == 0)
        r0 = p * TT

        if xT_pre[p] is None:
            xT_pre[p] = load_xT(p)
        xT_tok = xT_pre[p]

        pend_rms = None
        preA_cur = preA if not warm else {}

        def emit_rms(pz):
            e2, kq2, hqa, tok2 = pz
            W("pe", tok2)
            mq = te.matmul(pb[7][0:1, :], ones_c[:], hqa,
                           start=(e2 == 0), stop=(e2 == DC - 1))
            tq2 = I("pe", mq)
            bpool.put(kq2, tq2)
            return tq2

        if not warm:
            W("pe", pb_free[7])
        for e in range(DC):
            bi = e % 3
            if ("i", e) in preA_cur:
                wti = preA_cur[("i", e)]
                wtf = preA_cur[("f", e)]
            else:
                wti = qsp.dma(wib[bi][:].rearrange("p (db q) -> p db q", q=P),
                              wiB.rearrange("eb (db pp) q -> eb pp db q", pp=P)[e],
                              pre=[wib_free[bi], wiB_tok[e]])
                wtf = qac.dma(wfb[bi][:].rearrange("p (db q) -> p db q", q=P),
                              wfB.rearrange("eb (db pp) q -> eb pp db q", pp=P)[e],
                              pre=[wfb_free[bi], wfB_tok[e]])
            W("pe", wti)
            W("pe", xT_tok)
            if pend_rms is not None:
                rms_tok = emit_rms(pend_rms)
                pend_rms = None
            ki = bank()
            for d in range(DC):
                mi = te.matmul(pb[ki][:], wib[bi][:, d * P:(d + 1) * P],
                               xT[:, d * TT:(d + 1) * TT], start=(d == 0), stop=(d == DC - 1))
            tok_i = I("pe", mi)
            wib_free[bi] = tok_i
            W("pe", wtf)
            kf = bank()
            for d in range(DC):
                mf = te.matmul(pb[kf][:], wfb[bi][:, d * P:(d + 1) * P],
                               xT[:, d * TT:(d + 1) * TT], start=(d == 0), stop=(d == DC - 1))
            tok_f = I("pe", mf)
            wfb_free[bi] = tok_f
            W("s", tok_f)
            ka, a_t = fpool.get("s")
            ta = I("s", se.activation(a_t, pb[kf][:], AF.Sigmoid))
            kn, sn_t = fpool.get("s")
            tsn = I("s", se.activation(sn_t, pb[kf][:], AF.Sigmoid, scale=-1.0))
            pb_free[kf] = tsn
            W("s", tok_i)
            ks, si_t = fpool.get("s")
            tsi = I("s", se.activation(si_t, pb[ki][:], AF.Sigmoid))
            W("v", tsi)
            k1, i1_t = fpool.get("v")
            t1 = I("v", ve.tensor_mul(i1_t, pb[ki][:], si_t))
            pb_free[ki] = t1
            fpool.put(ks, t1)
            W("v", tsn)
            W("v", t1)
            k2, i2_t = fpool.get("v")
            t2 = I("v", ve.tensor_mul(i2_t, i1_t, sn_t))
            fpool.put(k1, t2)
            fpool.put(kn, t2)
            W("v", ta)
            W("v", t2)
            W("v", state_toks[e])
            if warm:
                kh, h_dst = fpool.get("v")
                hview = h_dst
            else:
                if e == 0:
                    W("v", hbuf_war)
                hview = hbuf[:, e * TT:(e + 1) * TT]
            tscan = I("v", ve.tensor_tensor_scan(hview, a_t, i2_t, state[:, e:e + 1],
                                                 ALU.mult, ALU.add))
            fpool.put(ka, tscan)
            fpool.put(k2, tscan)
            W("v", tscan)
            tst = I("v", ve.tensor_copy(state[:, e:e + 1], hview[:, TT - 1:TT]))
            state_toks[e] = tst
            if warm:
                fpool.put(kh, tst)
            else:
                kq, hq_t = bpool.get("v")
                W("v", tscan)
                thq = I("v", ve.tensor_mul(hq_t, hview, hview))
                pend_rms = (e, kq, hq_t, thq)
        if warm:
            xT_free = pr.mark("pe")
            continue
        rms_tok = emit_rms(pend_rms)
        pend_rms = None

        W("v", rms_tok)
        tr = I("v", ve.tensor_scalar(rmsrow[0:1, :], pb[7][0:1, :], 1.0 / D, EPS,
                                     ALU.mult, ALU.add))
        pb_free[7] = tr
        W("s", tr)
        tsq = I("s", se.activation(rstd_b[0:1, :], rmsrow[0:1, :], AF.Sqrt))
        W("v", tsq)
        trc = I("v", ve.reciprocal(rmsrow[0:1, :], rstd_b[0:1, :]))
        W("pe", trc)
        kb = bank()
        tbc = I("pe", te.matmul(pb[kb][:], ones_r[:], rmsrow[0:1, :], start=True, stop=True))
        W("s", tbc)
        trb = I("s", se.copy(rstd_b[:], pb[kb][:]))
        pb_free[kb] = trb

        for tb in range(TB):
            txk = qsp.dma(xtok[:, tb * D:(tb + 1) * D],
                          xD[r0 + tb * P:r0 + (tb + 1) * P, :],
                          pre=[xtok_war] if tb == 0 else None, group=(tb > 0))
        qsp.close()
        xtok_tok = txk
        for e in range(DC):
            bi = e % 3
            wtg = qsp.dma(wib[bi][:].rearrange("p (db q) -> p db q", q=P),
                          wgB.rearrange("eb (db pp) q -> eb pp db q", pp=P)[e],
                          pre=[wib_free[bi], wgB_tok[e]])
            if p == 1:
                pump_retile(2)
            W("pe", wtg)
            kg = bank()
            for d in range(DC):
                mg = te.matmul(pb[kg][:], wib[bi][:, d * P:(d + 1) * P],
                               xT[:, d * TT:(d + 1) * TT], start=(d == 0), stop=(d == DC - 1))
            tok_g = I("pe", mg)
            wib_free[bi] = tok_g
            W("s", tok_g)
            k1, sg1_t = fpool.get("s")
            ts1 = I("s", se.activation(sg1_t, pb[kg][:], AF.Sigmoid))
            W("v", ts1)
            W("v", ld["gnw"])
            k2, sg_t = fpool.get("v")
            tsg = I("v", ve.scalar_tensor_tensor(sg_t, pb[kg][:], gnw[:, e:e + 1], sg1_t,
                                                 ALU.mult, ALU.mult))
            pb_free[kg] = tsg
            fpool.put(k1, tsg)
            W("v", trb)
            W("v", state_toks[e])
            k3, o2_t = fpool.get("v")
            to2 = I("v", ve.tensor_mul(o2_t, hbuf[:, e * TT:(e + 1) * TT], rstd_b[:]))
            if e == 0:
                W("v", o4_war)
            W("v", to2)
            W("v", tsg)
            to4 = I("v", ve.tensor_mul(o4[:, e * TT:(e + 1) * TT], o2_t, sg_t))
            fpool.put(k2, to4)
            fpool.put(k3, to4)
        o4_tok = to4
        xT_free = pr.mark("pe")

        evac_last = None
        for sweep in range(2):
            bank_tok = [None] * 8
            for d in range(DC):
                bi = d % 3
                wto = qsp.dma(wob[bi][:],
                              woB[d * P:(d + 1) * P, sweep * 1024:(sweep + 1) * 1024],
                              pre=[wob_free[bi], woB_tok[d // 4]])
                if p == 1:
                    pump_retile(2)
                W("pe", wto)
                W("pe", o4_tok)
                for tb in range(TB):
                    for eg in range(2):
                        k = tb * 2 + eg
                        if d == 0:
                            W("pe", pb_free[k])
                        mo = te.matmul(pb[k][:],
                                       o4[:, d * TT + tb * P: d * TT + (tb + 1) * P],
                                       wob[bi][:, eg * TT:(eg + 1) * TT],
                                       start=(d == 0), stop=(d == DC - 1))
                tok_d = I("pe", mo)
                if d == DC - 1:
                    for tb in range(TB):
                        for eg in range(2):
                            bank_tok[tb * 2 + eg] = tok_d
                wob_free[bi] = tok_d
            for tb in range(TB):
                for eg in range(2):
                    k = tb * 2 + eg
                    idx = tb * 4 + sweep * 2 + eg
                    col = tb * D + sweep * 1024 + eg * TT
                    W("v", bank_tok[k])
                    W("v", xtok_tok)
                    if sweep == 0 and tb == 0 and eg == 0:
                        W("v", hbuf_war)
                    tv = I("v", ve.scalar_tensor_tensor(r1[:, col:col + TT], pb[k][:], 1.0,
                                                        xtok[:, col:col + TT],
                                                        ALU.mult, ALU.add,
                                                        accum_out=accm[:, idx:idx + 1]))
                    pb_free[k] = tv
                    W("s", tv)
                    ksr, scr_t = fpool.get("s")
                    tss = I("s", se.activation(scr_t, r1[:, col:col + TT], AF.Square,
                                               accum_out=accs[:, idx:idx + 1]))
                    fpool.put(ksr, tss)
                    evac_last = (tv, tss)
        o4_free = pr.mark("pe")
        W("v", evac_last[0])
        W("v", evac_last[1])
        trs = ln_finalize(16, TB)
        y1_toks = []
        for tb in range(TB):
            W("v", trs)
            ty = I("v", ve.tensor_scalar(r1[:, tb * D:(tb + 1) * D], r1[:, tb * D:(tb + 1) * D],
                                         lnstat[:, 2 * TB + tb:2 * TB + tb + 1],
                                         lnstat[:, 7 * TB + tb:7 * TB + tb + 1],
                                         ALU.subtract, ALU.mult))
            y1_toks.append(ty)
        xtok_v_done = pr.mark("v")

        xrot_toks = []

        def emit_h128d(pz):
            d2, kx2, x1a, tok2 = pz
            W("pe", tok2)
            W("pe", ld["h128"])
            kz = bank()
            tmz = I("pe", te.matmul(pb[kz][:, 0:HALF], h128[:], x1a[:, 0:HALF],
                                    start=True, stop=True))
            bpool.put(kx2, tmz)
            W("s", tmz)
            if th == 0 and d2 == 0:
                W("s", o4_free)
                W("s", o4_war)
                W("s", fb_free)
            if th == 1 and d2 == 0:
                W("s", xrot_toks[0])
            tev2 = I("s", se.copy(fb[:, d2 * HALF:(d2 + 1) * HALF], pb[kz][:, 0:HALF]))
            pb_free[kz] = tev2
            return tev2

        for th in range(2):
            pendD = None
            for d in range(DC):
                if p == 1:
                    pump_retile(1)
                W("pe", y1_toks[2 * th])
                W("pe", y1_toks[2 * th + 1])
                W("pe", ld["ident"])
                kz2 = bank()
                for tbl in range(2):
                    tb2 = 2 * th + tbl
                    ttr = I("pe", te.transpose(
                        pb[kz2][:, tbl * P:(tbl + 1) * P],
                        r1[:, tb2 * D + d * P: tb2 * D + (d + 1) * P], ident[:]))
                W("s", ttr)
                W("s", ld["suug"])
                kx, x1_t = bpool.get("s")
                tx1 = I("s", se.activation(x1_t[:, 0:HALF], pb[kz2][:, 0:HALF],
                                           AF.Copy, scale=suug[:, d:d + 1]))
                pb_free[kz2] = tx1
                if pendD is not None:
                    tev = emit_h128d(pendD)
                pendD = (d, kx, x1_t, tx1)
            tev = emit_h128d(pendD)
            pendD = None
            fb_tok = tev
            src_b, dst_b = fb, fb2
            tok = fb_tok
            for li, s_ in enumerate([1, 2, 4, 8]):
                blk = s_ * HALF
                vin = src_b[:].rearrange("p (a two bt) -> p a two bt", two=2, bt=blk)
                vout = dst_b[:].rearrange("p (a two bt) -> p a two bt", two=2, bt=blk)
                W("v", tok)
                tp = I("v", ve.tensor_add(vout[:, :, 0, :], vin[:, :, 0, :], vin[:, :, 1, :]))
                tok = I("v", ve.tensor_sub(vout[:, :, 1, :], vin[:, :, 0, :], vin[:, :, 1, :]))
                src_b, dst_b = dst_b, src_b
            W("v", tok)
            if th == 0:
                W("v", xT_free)
            vz_in = src_b[:].rearrange("p (c t) -> p c t", t=HALF)
            vz_out = xrot[:].rearrange("p (c tt) -> p c tt", tt=TT)[:, :, th * HALF:(th + 1) * HALF]
            txr = I("v", ve.tensor_copy(vz_out, vz_in))
            xrot_toks.append(txr)
        xrot_tok = xrot_toks[-1]
        fb_free = xrot_tok
        r1_pe_done = pr.mark("pe")

        if p == 1:
            pump_retile(len(retile_jobs))
            upT_fin = [sem_maxima(upT_tok[ib]) for ib in range(4)]
            gtT_fin = [sem_maxima(gtT_tok[ib]) for ib in range(4)]
        pend = None
        zt_toks = [None, None]

        def emit_h128_evac(pz):
            ic2, k42, t4a, tok42 = pz
            W("pe", tok42)
            kz = bank()
            tmz = I("pe", te.matmul(pb[kz][:], h128[:], t4a, start=True, stop=True))
            bpool.put(k42, tmz)
            W("s", tmz)
            if ic2 == 0:
                W("s", o4_free)
                W("s", fb_free)
            te1 = I("s", se.copy(zt0[:, ic2 * HALF:(ic2 + 1) * HALF], pb[kz][:, 0:HALF]))
            if ic2 == 0:
                W("s", xtok_v_done)
            te2 = I("s", se.copy(zt1[:, ic2 * HALF:(ic2 + 1) * HALF], pb[kz][:, HALF:TT]))
            pb_free[kz] = te2
            zt_toks[0] = te1
            zt_toks[1] = te2

        W("s", wsg_tok)
        for igg in range(32):
            gi = igg % 2
            pre_u = [wub_free[gi]] + upT_fin[igg // 8]
            if igg == 0:
                pre_u = pre_u + wib_free + wfb_free + wob_free
            up_tok = qsp.dma(wub[gi][:], wupT[igg], pre=pre_u)
            gt_tok = qac.dma(wgb[gi][:], wgtT[igg],
                             pre=[wgb_free[gi]] + gtT_fin[igg // 8])
            for icl in range(2):
                ic = igg * 2 + icl
                W("pe", up_tok)
                W("pe", xrot_tok)
                ku = bank()
                for d in range(DC):
                    mu = te.matmul(pb[ku][:],
                                   wub[gi][:, d * HALF + icl * P: d * HALF + (icl + 1) * P],
                                   xrot[:, d * TT:(d + 1) * TT], start=(d == 0), stop=(d == DC - 1))
                tok_u = I("pe", mu)
                W("pe", gt_tok)
                kg2 = bank()
                for d in range(DC):
                    mgt = te.matmul(pb[kg2][:],
                                    wgb[gi][:, d * HALF + icl * P: d * HALF + (icl + 1) * P],
                                    xrot[:, d * TT:(d + 1) * TT], start=(d == 0), stop=(d == DC - 1))
                tok_g2 = I("pe", mgt)
                if icl == 1:
                    wub_free[gi] = tok_u
                    wgb_free[gi] = tok_g2
                if pend is not None:
                    emit_h128_evac(pend)
                    pend = None
                W("s", tok_g2)
                k1, t1_t = fpool.get("s")
                tt1 = I("s", se.activation(t1_t, pb[kg2][:], AF.Sigmoid,
                                           scale=wsg[:, ic:ic + 1]))
                W("v", tt1)
                k2, t2_t = fpool.get("v")
                tt2 = I("v", ve.scalar_tensor_tensor(t2_t, pb[kg2][:], wsg[:, ic:ic + 1],
                                                     t1_t, ALU.mult, ALU.mult))
                pb_free[kg2] = tt2
                fpool.put(k1, tt2)
                W("v", tok_u)
                W("v", tt2)
                k3, t3_t = fpool.get("v")
                tt3 = I("v", ve.scalar_tensor_tensor(t3_t, pb[ku][:], wsu[:, ic:ic + 1],
                                                     t2_t, ALU.mult, ALU.mult))
                pb_free[ku] = tt3
                fpool.put(k2, tt3)
                W("v", sdp_tok)
                W("v", tt3)
                k4, t4_t = bpool.get("v")
                tt4 = I("v", ve.tensor_scalar(t4_t, t3_t, sdpc[:, ic:ic + 1], None, ALU.mult))
                fpool.put(k3, tt4)
                pend = (ic, k4, t4_t, tt4)
        emit_h128_evac(pend)
        pend = None
        xrot_free = pr.mark("pe")

        preA = {}
        if p + 1 < n_pass:
            for e2 in range(3):
                preA[("i", e2)] = qsp.dma(
                    wib[e2 % 3][:].rearrange("p (db q) -> p db q", q=P),
                    wiB.rearrange("eb (db pp) q -> eb pp db q", pp=P)[e2],
                    pre=[xrot_free, wib_free[e2 % 3], wiB_tok[e2]])
                preA[("f", e2)] = qac.dma(
                    wfb[e2 % 3][:].rearrange("p (db q) -> p db q", q=P),
                    wfB.rearrange("eb (db pp) q -> eb pp db q", pp=P)[e2],
                    pre=[wfb_free[e2 % 3], wfB_tok[e2]])

        if p + 1 < n_pass:
            xT_pre[p + 1] = load_xT(p + 1, extra_pre=[xrot_free])
        ev_last = None
        for sweep in range(2):
            bank_tok = [None] * 8
            for c in range(IC):
                if p == 1 and sweep == 0:
                    pi2 = c % 2
                    ttw = qac.dma(wdb[pi2][:], wdnB[0:D, c * P:(c + 1) * P],
                                  pre=[wdb_tok[pi2]] + (wdn_last if c < 2 else []),
                                  transpose=True)
                    st2 = qac.dma(wdnT[c], wdb[pi2][:], pre=[ttw])
                    wdb_tok[pi2] = st2
                    dnT_tok.append(st2)
                gslot = grr[0]
                grr[0] = (grr[0] + 1) % len(wdbg)
                pre_w = [wdbg_free[gslot]]
                if p == 1:
                    pre_w.append(dnT_tok[c])
                    if sweep == 0 and c == 0:
                        pre_w += wdn_last
                elif p == 2 and sweep == 0 and c < len(wdbg):
                    pre_w += dnT_last
                wtd = qsp.dma(wdbg[gslot][:], wdnT[c][:, sweep * 1024:(sweep + 1) * 1024],
                              pre=pre_w)
                W("pe", wtd)
                W("pe", zt_toks[0])
                W("pe", zt_toks[1])
                for tbl in range(4):
                    ztX = zt0 if tbl < 2 else zt1
                    tl = tbl % 2
                    for jg in range(2):
                        k = tbl * 2 + jg
                        if c == 0:
                            W("pe", pb_free[k])
                        md = te.matmul(pb[k][:],
                                       ztX[:, c * HALF + tl * P: c * HALF + (tl + 1) * P],
                                       wdbg[gslot][:, jg * TT:(jg + 1) * TT],
                                       start=(c == 0), stop=(c == IC - 1))
                tok_c = I("pe", md)
                wdbg_free[gslot] = tok_c
                if c == IC - 1:
                    for k in range(8):
                        bank_tok[k] = tok_c
            for tbl in range(4):
                for jg in range(2):
                    k = tbl * 2 + jg
                    idx = tbl * 4 + sweep * 2 + jg
                    col = tbl * D + sweep * 1024 + jg * TT
                    W("v", bank_tok[k])
                    tv = I("v", ve.scalar_tensor_tensor(r1[:, col:col + TT], pb[k][:], 1.0,
                                                        r1[:, col:col + TT],
                                                        ALU.mult, ALU.add,
                                                        accum_out=accm[:, idx:idx + 1]))
                    pb_free[k] = tv
                    W("s", tv)
                    ksr, scr_t = fpool.get("s")
                    tss = I("s", se.activation(scr_t, r1[:, col:col + TT], AF.Square,
                                               accum_out=accs[:, idx:idx + 1]))
                    fpool.put(ksr, tss)
                    ev_last = (tv, tss)
        if p == 1:
            dnT_last = sem_maxima(dnT_tok)
        W("v", ev_last[0])
        W("v", ev_last[1])
        trs2 = ln_finalize(16, TB)
        out_toks = []
        for tb in range(TB):
            W("v", trs2)
            ty2 = I("v", ve.tensor_scalar(r1[:, tb * D:(tb + 1) * D],
                                          r1[:, tb * D:(tb + 1) * D],
                                          lnstat[:, 2 * TB + tb:2 * TB + tb + 1],
                                          lnstat[:, 7 * TB + tb:7 * TB + tb + 1],
                                          ALU.subtract, ALU.mult))
            tdo = qdo.dma(outD[(p - 1) * TT + tb * P: (p - 1) * TT + (tb + 1) * P, :],
                          r1[:, tb * D:(tb + 1) * D], pre=[ty2])
            out_toks.append(tdo)
        g_pe_done = pr.mark("pe")
        xtok_war = g_pe_done
        o4_war = g_pe_done
        hbuf_war = list(out_toks)

    for k, v in qdo.all_last():
        sy.wait_ge(pr.sem[k], v)
        ge.wait_ge(pr.sem[k], v)
    es.close()
    assert nc.sbuf_base <= 17408, f"auto allocs overflow reserve: {nc.sbuf_base}"
    return nc


_NC_CACHE = {}


def _get_nc(n_full=8):
    if n_full not in _NC_CACHE:
        _NC_CACHE[n_full] = build_nc(n_full)
    return _NC_CACHE[n_full]


def make_aux(full):
    return {
        "wi": np.ascontiguousarray(full["L0_attn_Wi"], dtype=np.float32),
        "wf": np.ascontiguousarray(full["L0_attn_Wf"], dtype=np.float32),
        "wg": np.ascontiguousarray(full["L0_attn_Wg"], dtype=np.float32),
        "wo": np.ascontiguousarray(full["L0_attn_Wo"], dtype=np.float32),
        "wup": np.ascontiguousarray(full["L0_ffn_W_up"], dtype=np.float32),
        "wgt": np.ascontiguousarray(full["L0_ffn_W_gate"], dtype=np.float32),
        "wdn": np.ascontiguousarray(full["L0_ffn_W_down"], dtype=np.float32),
        "gnw": np.ascontiguousarray(np.asarray(full["L0_attn_g_norm_weight"], dtype=np.float32).reshape(DC, P).T),
        "suug": np.ascontiguousarray(np.asarray(full["L0_ffn_SU_ug"], dtype=np.float32).reshape(DC, P).T),
        "sudp": np.ascontiguousarray(np.asarray(full["L0_ffn_SU_dp"], dtype=np.float32).reshape(IC, P).T),
        "wsu": np.ascontiguousarray(np.asarray(full["L0_ffn_Wscale_ug"], dtype=np.float32)[:INTER].reshape(IC, P).T),
        "wsg": np.ascontiguousarray(np.asarray(full["L0_ffn_Wscale_ug"], dtype=np.float32)[INTER:].reshape(IC, P).T),
        "wsd": np.ascontiguousarray(np.asarray(full["L0_ffn_Wscale_dp"], dtype=np.float32).reshape(DC, P).T),
        "h128": _hadamard128(),
        "ident": np.eye(P, dtype=np.float32),
    }


def kernel(**inputs):
    n_full = 8
    ct = n_full * TT
    rows = (n_full + 1) * TT
    nc = _get_nc(n_full)
    x = np.asarray(inputs["L0_attn_hidden_states"])
    aux = make_aux(inputs)
    in_maps = []
    for c in range(8):
        b, q = c // 4, c % 4
        xs = np.zeros((rows, D), np.float32)
        t0 = q * ct
        w0 = max(0, t0 - NWARM)
        nw = t0 - w0
        if nw > 0:
            xs[NWARM - nw:NWARM] = x[b, w0:t0]
        xs[NWARM:] = x[b, t0:t0 + ct]
        m = dict(aux)
        m["x"] = xs
        in_maps.append(m)
    res = run_bass_kernel_spmd(nc, in_maps, core_ids=list(range(8)))
    out = np.empty((2, 16384, D), np.float32)
    for c in range(8):
        b, q = c // 4, c % 4
        out[b, q * ct:(q + 1) * ct] = res.results[c]["out"]
    return out


# version 11
# speedup vs baseline: 1.3039x; 1.1297x over previous
import numpy as np
import ml_dtypes

import concourse.bass as bass
import concourse.mybir as mybir
from concourse.bass_utils import run_bass_kernel_spmd

F32 = mybir.dt.float32
BF16 = mybir.dt.bfloat16
AF = mybir.ActivationFunctionType
ALU = mybir.AluOpType
BF = ml_dtypes.bfloat16

P = 128
D = 2048
DC = 16
INTER = 8192
IC = 64
TT = 512
TB = 4
HALF = 256
NWARM = 512
S0 = 1.0 / (64.0 * np.sqrt(2048.0))
EPS = 1e-5
ENG = ("pe", "v", "s", "g", "sp")


def _hadamard128():
    i = np.arange(128)
    pc = np.zeros((128, 128), np.int64)
    for b in range(7):
        pc += ((i[:, None] >> b) & 1) * ((i[None, :] >> b) & 1)
    return np.where(pc % 2 == 0, 1.0, -1.0).astype(BF)


class Prog:
    def __init__(self, nc, es):
        self.nc = nc
        self.sem = {}
        self.cnt = {}
        for name in ("pe", "v", "s", "g"):
            self.sem[name] = es.enter_context(nc.semaphore(name))
            self.cnt[name] = 0
        self.wm = {e: {} for e in ENG}
        self.eng = {"pe": nc.tensor, "v": nc.vector, "s": nc.scalar,
                    "g": nc.gpsimd, "sp": nc.sync}

    def wait(self, eng, tok):
        if tok is None:
            return
        if isinstance(tok, list):
            for t in tok:
                self.wait(eng, t)
            return
        sname, val = tok
        if self.wm[eng].get(sname, 0) >= val:
            return
        self.eng[eng].wait_ge(self.sem[sname], val)
        self.wm[eng][sname] = val

    def inc(self, eng, inst):
        self.cnt[eng] += 1
        inst.then_inc(self.sem[eng], 1)
        return (eng, self.cnt[eng])

    def mark(self, eng):
        return (eng, self.cnt[eng])


class DQ:

    def __init__(self, pr, eng_name, basename, es, nsem):
        self.pr = pr
        self.eng_name = eng_name
        self.eng = pr.eng[eng_name]
        self.names = []
        for i in range(nsem):
            name = f"{basename}{i}"
            pr.sem[name] = es.enter_context(pr.nc.semaphore(name))
            pr.cnt[name] = 0
            self.names.append(name)
        self.i = 0
        self.cur = None

    def dma(self, out, in_, pre=None, group=False, **kw):
        pr = self.pr
        if group and self.cur is not None:
            k = self.cur
        else:
            k = self.names[self.i]
            self.i = (self.i + 1) % len(self.names)
            if pr.cnt[k] > 0:
                pr.wait(self.eng_name, (k, pr.cnt[k]))
            self.cur = k
        pr.wait(self.eng_name, pre)
        inst = self.eng.dma_start(out=out, in_=in_, **kw)
        pr.cnt[k] += 16
        inst.then_inc(pr.sem[k], 16)
        return (k, pr.cnt[k])

    def close(self):
        self.cur = None

    def all_last(self):
        return [(k, self.pr.cnt[k]) for k in self.names if self.pr.cnt[k] > 0]


class Pool:

    def __init__(self, pr, tiles):
        self.pr = pr
        self.tiles = tiles
        self.free = [None] * len(tiles)
        self.i = 0

    def get(self, eng):
        k = self.i
        self.i = (self.i + 1) % len(self.tiles)
        self.pr.wait(eng, self.free[k])
        return k, self.tiles[k]

    def put(self, k, tok):
        self.free[k] = tok


def sem_maxima(toks):
    seen = {}
    for t_ in toks:
        if t_ is None:
            continue
        seen[t_[0]] = max(seen.get(t_[0], 0), t_[1])
    return [(k, v) for k, v in seen.items()]


def build_nc(n_full=8):
    nc = bass.Bass("TRN2", target_bir_lowering=False)
    n_pass = n_full + 1
    rows = n_pass * TT

    xD = nc.dram_tensor("x", [rows, D], F32, kind="ExternalInput").ap()
    wiD = nc.dram_tensor("wi", [D, D], F32, kind="ExternalInput").ap()
    wfD = nc.dram_tensor("wf", [D, D], F32, kind="ExternalInput").ap()
    wgD = nc.dram_tensor("wg", [D, D], F32, kind="ExternalInput").ap()
    woD = nc.dram_tensor("wo", [D, D], F32, kind="ExternalInput").ap()
    wupD = nc.dram_tensor("wup", [INTER, D], F32, kind="ExternalInput").ap()
    wgtD = nc.dram_tensor("wgt", [INTER, D], F32, kind="ExternalInput").ap()
    wdnD = nc.dram_tensor("wdn", [D, INTER], F32, kind="ExternalInput").ap()
    gnwD = nc.dram_tensor("gnw", [P, DC], F32, kind="ExternalInput").ap()
    suugD = nc.dram_tensor("suug", [P, DC], F32, kind="ExternalInput").ap()
    sudpD = nc.dram_tensor("sudp", [P, IC], F32, kind="ExternalInput").ap()
    wsuD = nc.dram_tensor("wsu", [P, IC], F32, kind="ExternalInput").ap()
    wsgD = nc.dram_tensor("wsg", [P, IC], F32, kind="ExternalInput").ap()
    wsdD = nc.dram_tensor("wsd", [P, DC], F32, kind="ExternalInput").ap()
    h128D = nc.dram_tensor("h128", [P, P], BF16, kind="ExternalInput").ap()
    identD = nc.dram_tensor("ident", [P, P], F32, kind="ExternalInput").ap()
    outD = nc.dram_tensor("out", [n_full * TT, D], F32, kind="ExternalOutput").ap()

    xbD = nc.dram_tensor("xb", [rows, D], BF16, kind="Internal").ap()
    wiB = nc.dram_tensor("wiB", [DC, D, P], BF16, kind="Internal").ap()
    wfB = nc.dram_tensor("wfB", [DC, D, P], BF16, kind="Internal").ap()
    wgB = nc.dram_tensor("wgB", [DC, D, P], BF16, kind="Internal").ap()
    woB = nc.dram_tensor("woB", [D, D], BF16, kind="Internal").ap()
    wupB = nc.dram_tensor("wupB", [INTER, D], BF16, kind="Internal").ap()
    wgtB = nc.dram_tensor("wgtB", [INTER, D], BF16, kind="Internal").ap()
    wdnB = nc.dram_tensor("wdnB", [D, INTER], BF16, kind="Internal").ap()
    wdnF = nc.dram_tensor("wdnF", [D, INTER], F32, kind="Internal").ap()
    wupT = nc.dram_tensor("wupT", [32, P, DC * HALF], BF16, kind="Internal").ap()
    wgtT = nc.dram_tensor("wgtT", [32, P, DC * HALF], BF16, kind="Internal").ap()
    wdnT = nc.dram_tensor("wdnT", [IC, P, 2048], BF16, kind="Internal").ap()

    from contextlib import ExitStack
    es = ExitStack()
    pr = Prog(nc, es)
    W, I = pr.wait, pr.inc
    te, ve, se, ge, sy = nc.tensor, nc.vector, nc.scalar, nc.gpsimd, nc.sync
    qsp = DQ(pr, "sp", "qw", es, 8)
    qac = DQ(pr, "s", "qa", es, 8)
    qgp = DQ(pr, "g", "qg", es, 6)
    qdo = DQ(pr, "sp", "qo", es, 2)

    off = [17408]

    def at(name, shape, dtype, o=None):
        nonlocal off
        nbytes = int(np.prod(shape[1:])) * mybir.dt.size(dtype)
        if o is None:
            o = off[0]
            off[0] += (nbytes + 31) // 32 * 32
        return nc.alloc_sbuf_tensor_at(name, shape, dtype, offset=o)

    h128 = at("h128", [P, P], BF16)
    ident = at("ident", [P, P], F32)
    gnw = at("gnw", [P, DC], F32)
    suug = at("suug", [P, DC], F32)
    sdpc = at("sdpc", [P, IC], F32)
    wsu = at("wsu", [P, IC], F32)
    wsg = at("wsg", [P, IC], F32)
    wsd = at("wsd", [P, DC], F32)
    ones_c = at("ones_c", [P, 1], BF16)
    ones_r = at("ones_r", [1, P], F32)
    state = at("state", [P, DC], F32)
    rmsrow = at("rmsrow", [1, TT], F32)
    rstd_b = at("rstd_b", [P, TT], F32)
    accm = at("accm", [P, 16], F32)
    accs = at("accs", [P, 16], F32)
    lnstat = at("lnstat", [P, 10 * TB], F32)
    wdb = [at(f"wdb{i}", [P, 2048], BF16) for i in range(2)]
    fpool_t = [at(f"fp{i}", [P, TT], F32) for i in range(7)]
    bpool_t = [at(f"bp{i}", [P, TT], BF16) for i in range(4)]
    prepb = [at(f"prepb{i}", [P, 1024], BF16) for i in range(2)]
    base_small = off[0]
    assert base_small <= 65504, base_small

    OFF_W = 65504
    OFF_X = 98272
    OFF_H = 131040
    OFF_Z = 163808
    OFF_WS = 196576
    OFF_XT = 212960

    wib = [at(f"wib{i}", [P, D], BF16, OFF_W + i * 4096) for i in range(3)]
    wfb = [at(f"wfb{i}", [P, D], BF16, OFF_W + 12288 + i * 4096) for i in range(3)]
    wob = [at(f"wob{i}", [P, 1024], BF16, OFF_W + 24576 + i * 2048) for i in range(3)]
    wub = [at(f"wub{i}", [P, DC * HALF], BF16, OFF_W + i * 8192) for i in range(2)]
    wgb = [at(f"wgb{i}", [P, DC * HALF], BF16, OFF_W + 16384 + i * 8192) for i in range(2)]

    xtok = at("xtok", [P, TB * D], F32, OFF_X)
    zt1 = at("zt1", [P, IC * HALF], BF16, OFF_X)

    hbuf = at("hbuf", [P, DC * TT], F32, OFF_H)
    r1 = at("r1", [P, TB * D], F32, OFF_H)

    o4 = at("o4", [P, DC * TT], BF16, OFF_Z)
    fb = at("fb", [P, DC * HALF], F32, OFF_Z)
    fb2 = at("fb2", [P, DC * HALF], F32, OFF_Z + 16 * 1024)
    zt0 = at("zt0", [P, IC * HALF], BF16, OFF_Z)

    fold3a = at("fold3a", [P, 1024], F32)
    fold3b = at("fold3b", [P, 1024], F32)
    assert off[0] <= 65504, off[0]
    foldf = [at(f"foldf{i}", [P, 1024], F32, OFF_WS + i * 4096) for i in range(2)] + [fold3a]
    foldf2 = [at(f"foldf2{i}", [P, 1024], F32, OFF_WS + 8192 + i * 4096) for i in range(2)] + [fold3b]
    wdbg = [at(f"wdbg{i}", [P, 1024], BF16, OFF_WS + i * 2048) for i in range(6)]

    xT = at("xT", [P, DC * TT], BF16, OFF_XT)
    xrot = at("xrot", [P, DC * TT], BF16, OFF_XT)

    pb = [nc.alloc_psum_tensor(f"pb{i}", [P, TT], F32) for i in range(8)]
    pb_free = [None] * 8
    rr = [0]

    def bank(eng="pe", k=None):
        if k is None:
            k = rr[0]
            rr[0] = (rr[0] + 1) % 6
        W(eng, pb_free[k])
        return k

    fpool = Pool(pr, [t[:] for t in fpool_t])
    bpool = Pool(pr, [t[:] for t in bpool_t])

    ld = {}
    ldt = None
    for name, dst, srcT in [("gnw", gnw, gnwD), ("suug", suug, suugD), ("sdp", sdpc, sudpD),
                            ("wsu", wsu, wsuD), ("wsg", wsg, wsgD), ("wsd", wsd, wsdD),
                            ("h128", h128, h128D), ("ident", ident, identD)]:
        ldt = qsp.dma(dst[:], srcT, group=(ldt is not None))
    for name in ("gnw", "suug", "sdp", "wsu", "wsg", "wsd", "h128", "ident"):
        ld[name] = ldt
    qsp.close()

    ve.memset(ones_c[:], 1.0)
    ve.memset(ones_r[:], 1.0)
    tk0 = I("v", ve.memset(state[:], 0.0))
    W("v", ld["sdp"])
    sdp_tok = I("v", ve.tensor_scalar(sdpc[:], sdpc[:], S0 / 2.0, None, ALU.mult))
    W("v", ld["wsu"])
    wsu_tok = I("v", ve.tensor_scalar(wsu[:], wsu[:], 64.0 * S0, None, ALU.mult))
    W("v", ld["wsg"])
    wsg_tok = I("v", ve.tensor_scalar(wsg[:], wsg[:], 64.0 * S0, None, ALU.mult))
    W("v", ld["wsd"])
    wsd_tok = I("v", ve.tensor_scalar(wsd[:], wsd[:], 64.0, None, ALU.mult))

    xb_tok = [None] * n_pass
    for p in range(3):
        xb_tok[p] = qgp.dma(xbD[p * TT:(p + 1) * TT, :], xD[p * TT:(p + 1) * TT, :])
    wiB_tok, wfB_tok, wgB_tok = [], [], []
    for eb in range(DC):
        s_ = wiD.rearrange("d (eb q) -> eb d q", q=P)[eb]
        wiB_tok.append(qgp.dma(wiB[eb], s_))
        s_ = wfD.rearrange("d (eb q) -> eb d q", q=P)[eb]
        wfB_tok.append(qgp.dma(wfB[eb], s_))
    for eb in range(DC):
        s_ = wgD.rearrange("d (eb q) -> eb d q", q=P)[eb]
        wgB_tok.append(qgp.dma(wgB[eb], s_))
    woB_tok = []
    for b in range(4):
        woB_tok.append(qgp.dma(woB[b * TT:(b + 1) * TT, :], woD[b * TT:(b + 1) * TT, :]))

    cast_jobs = []
    upB_tok = [[] for _ in range(4)]
    gtB_tok = [[] for _ in range(4)]
    for ib in range(4):
        for j in range(4):
            r0 = ib * 2048 + j * 512
            cast_jobs.append(("up", ib, r0))
        for j in range(4):
            r0 = ib * 2048 + j * 512
            cast_jobs.append(("gt", ib, r0))
    for p in range(3, n_pass):
        cast_jobs.append(("x", p, p * TT))
    cj_i = [0]

    def pump_cast(n):
        while n > 0 and cj_i[0] < len(cast_jobs):
            kind, a, r0 = cast_jobs[cj_i[0]]
            cj_i[0] += 1
            n -= 1
            if kind == "up":
                upB_tok[a].append(qgp.dma(wupB[r0:r0 + 512, :], wupD[r0:r0 + 512, :]))
            elif kind == "gt":
                gtB_tok[a].append(qgp.dma(wgtB[r0:r0 + 512, :], wgtD[r0:r0 + 512, :]))
            else:
                xb_tok[a] = qgp.dma(xbD[r0:r0 + TT, :], xD[r0:r0 + TT, :])

    fold1_tok = [[] for _ in range(DC)]
    fold2_tok = [[] for _ in range(DC)]
    fold_free = [None, None, None]
    fold_all_toks = []
    for sweep in range(2):
        for jb in range(DC):
            for grp in range(8):
                jidx = sweep * 128 + jb * 8 + grp
                pi3 = jidx % 3
                if sweep == 0:
                    src_ap = wdnD[jb * P:(jb + 1) * P, grp * 1024:(grp + 1) * 1024]
                    dst_ap = wdnF[jb * P:(jb + 1) * P, grp * 1024:(grp + 1) * 1024]
                    pre = [fold_free[pi3]]
                else:
                    gath = wdnF[jb * P:(jb + 1) * P, :].rearrange(
                        "p (g sub q) -> p sub g q", g=8, q=P)[:, grp, :, :]
                    src_ap = gath
                    dst_ap = gath
                    pre = [fold_free[pi3]] + (fold1_tok[jb] if grp == 0 else [])
                lt3 = qgp.dma(foldf[pi3][:], src_ap, pre=pre)
                fa, fz = foldf[pi3], foldf2[pi3]
                tokf = lt3
                for s_ in (1, 2, 4):
                    blklen = s_ * P
                    vin = fa[:].rearrange("p (a two bt) -> p a two bt", two=2, bt=blklen)
                    vout = fz[:].rearrange("p (a two bt) -> p a two bt", two=2, bt=blklen)
                    W("g", tokf)
                    I("g", ge.tensor_add(vout[:, :, 0, :], vin[:, :, 0, :], vin[:, :, 1, :]))
                    tokf = I("g", ge.tensor_sub(vout[:, :, 1, :], vin[:, :, 0, :], vin[:, :, 1, :]))
                    fa, fz = fz, fa
                st3 = qgp.dma(dst_ap, fa[:], pre=[tokf])
                fold_free[pi3] = st3
                fold_all_toks.append(st3)
                (fold1_tok if sweep == 0 else fold2_tok)[jb].append(st3)
                if jidx >= 8 and jidx % 2 == 0:
                    pump_cast(1)
    pump_cast(len(cast_jobs))
    fold_all = sem_maxima(fold_all_toks)

    wdb_tok = [None, None]
    upT_tok = [[] for _ in range(4)]
    gtT_tok = [[] for _ in range(4)]

    def emit_retile():
        n = 0
        for ib in range(4):
            for mat in ("up", "gt"):
                for dch in range(DC):
                    n += 1
                    pi2 = n % 2
                    if mat == "up":
                        srcB, dstT, ctoks, toks_out = wupB, wupT, upB_tok[ib], upT_tok
                    else:
                        srcB, dstT, ctoks, toks_out = wgtB, wgtT, gtB_tok[ib], gtT_tok
                    tt = qsp.dma(wdb[pi2][:],
                                 srcB[ib * 2048:(ib + 1) * 2048, dch * P:(dch + 1) * P],
                                 pre=[wdb_tok[pi2]] + ctoks, transpose=True)
                    st2 = qsp.dma(dstT[ib * 8:(ib + 1) * 8, :, dch * HALF:(dch + 1) * HALF]
                                  .rearrange("g p t -> p g t"),
                                  wdb[pi2][:].rearrange("p (g t) -> p g t", t=HALF),
                                  pre=[tt])
                    wdb_tok[pi2] = st2
                    toks_out[ib].append(st2)

    xT_free = None
    xtok_war = None
    hbuf_war = None
    o4_war = None
    o4_free = None
    fb_free = None
    xrot_free = None
    state_toks = [tk0] * DC
    wib_free = [None, None, None]
    wfb_free = [None, None, None]
    wob_free = [None, None, None]
    wub_free = [None, None]
    wgb_free = [None, None]
    wdbg_free = [None] * 6
    grr = [0]
    dnT_tok = []
    dnT_last = []
    wdnp_tok = [[] for _ in range(8)]
    upT_fin = None
    gtT_fin = None

    def ln_finalize(n_idx, ntb):
        tm = I("v", ve.tensor_reduce(lnstat[:, 0:ntb],
                                     accm[:, 0:n_idx].rearrange("p (tb s) -> p tb s", s=4),
                                     mybir.AxisListType.X, ALU.add))
        tq = I("v", ve.tensor_reduce(lnstat[:, TB:TB + ntb],
                                     accs[:, 0:n_idx].rearrange("p (tb s) -> p tb s", s=4),
                                     mybir.AxisListType.X, ALU.add))
        W("v", tm)
        tmn = I("v", ve.tensor_scalar(lnstat[:, 2 * TB:2 * TB + ntb], lnstat[:, 0:ntb],
                                      1.0 / D, None, ALU.mult))
        W("v", tmn)
        tms = I("v", ve.tensor_mul(lnstat[:, 3 * TB:3 * TB + ntb],
                                   lnstat[:, 2 * TB:2 * TB + ntb],
                                   lnstat[:, 2 * TB:2 * TB + ntb]))
        W("v", tq)
        tv1 = I("v", ve.tensor_scalar(lnstat[:, 4 * TB:4 * TB + ntb], lnstat[:, TB:TB + ntb],
                                      1.0 / D, EPS, ALU.mult, ALU.add))
        W("v", tms)
        W("v", tv1)
        tvar = I("v", ve.tensor_sub(lnstat[:, 5 * TB:5 * TB + ntb],
                                    lnstat[:, 4 * TB:4 * TB + ntb],
                                    lnstat[:, 3 * TB:3 * TB + ntb]))
        W("s", tvar)
        tsd = I("s", se.activation(lnstat[:, 6 * TB:6 * TB + ntb],
                                   lnstat[:, 5 * TB:5 * TB + ntb], AF.Sqrt))
        W("v", tsd)
        trs = I("v", ve.reciprocal(lnstat[:, 7 * TB:7 * TB + ntb],
                                   lnstat[:, 6 * TB:6 * TB + ntb]))
        return trs

    def load_xT(p2, extra_pre=None):
        r2_ = p2 * TT
        for d in range(DC):
            pre0 = None
            if d == 0:
                pre0 = [xT_free, xb_tok[p2]] + (extra_pre or [])
            txT = qac.dma(xT[:, d * TT:(d + 1) * TT],
                          xbD[r2_:r2_ + TT, d * P:(d + 1) * P],
                          pre=pre0, group=(d > 0), transpose=True)
        qac.close()
        return txT

    xT_pre = [None] * n_pass
    preA = {}

    for p in range(n_pass):
        warm = (p == 0)
        r0 = p * TT

        if xT_pre[p] is None:
            xT_pre[p] = load_xT(p)
        xT_tok = xT_pre[p]

        pend_rms = None
        preA_cur = preA if not warm else {}

        def emit_rms(pz):
            e2, kq2, hqa, tok2 = pz
            W("pe", tok2)
            mq = te.matmul(pb[7][0:1, :], ones_c[:], hqa,
                           start=(e2 == 0), stop=(e2 == DC - 1))
            tq2 = I("pe", mq)
            bpool.put(kq2, tq2)
            return tq2

        if not warm:
            W("pe", pb_free[7])
        for e in range(DC):
            bi = e % 3
            if ("i", e) in preA_cur:
                wti = preA_cur[("i", e)]
                wtf = preA_cur[("f", e)]
            else:
                wti = qsp.dma(wib[bi][:].rearrange("p (db q) -> p db q", q=P),
                              wiB.rearrange("eb (db pp) q -> eb pp db q", pp=P)[e],
                              pre=[wib_free[bi], wiB_tok[e]])
                wtf = qac.dma(wfb[bi][:].rearrange("p (db q) -> p db q", q=P),
                              wfB.rearrange("eb (db pp) q -> eb pp db q", pp=P)[e],
                              pre=[wfb_free[bi], wfB_tok[e]])
            W("pe", wti)
            W("pe", xT_tok)
            if pend_rms is not None:
                rms_tok = emit_rms(pend_rms)
                pend_rms = None
            ki = bank()
            for d in range(DC):
                mi = te.matmul(pb[ki][:], wib[bi][:, d * P:(d + 1) * P],
                               xT[:, d * TT:(d + 1) * TT], start=(d == 0), stop=(d == DC - 1))
            tok_i = I("pe", mi)
            wib_free[bi] = tok_i
            W("pe", wtf)
            kf = bank()
            for d in range(DC):
                mf = te.matmul(pb[kf][:], wfb[bi][:, d * P:(d + 1) * P],
                               xT[:, d * TT:(d + 1) * TT], start=(d == 0), stop=(d == DC - 1))
            tok_f = I("pe", mf)
            wfb_free[bi] = tok_f
            W("s", tok_f)
            ka, a_t = fpool.get("s")
            ta = I("s", se.activation(a_t, pb[kf][:], AF.Sigmoid))
            kn, sn_t = fpool.get("s")
            tsn = I("s", se.activation(sn_t, pb[kf][:], AF.Sigmoid, scale=-1.0))
            pb_free[kf] = tsn
            W("s", tok_i)
            ks, si_t = fpool.get("s")
            tsi = I("s", se.activation(si_t, pb[ki][:], AF.Sigmoid))
            W("v", tsi)
            k1, i1_t = fpool.get("v")
            t1 = I("v", ve.tensor_mul(i1_t, pb[ki][:], si_t))
            pb_free[ki] = t1
            fpool.put(ks, t1)
            W("v", tsn)
            W("v", t1)
            k2, i2_t = fpool.get("v")
            t2 = I("v", ve.tensor_mul(i2_t, i1_t, sn_t))
            fpool.put(k1, t2)
            fpool.put(kn, t2)
            W("v", ta)
            W("v", t2)
            W("v", state_toks[e])
            if warm:
                kh, h_dst = fpool.get("v")
                hview = h_dst
            else:
                if e == 0:
                    W("v", hbuf_war)
                hview = hbuf[:, e * TT:(e + 1) * TT]
            tscan = I("v", ve.tensor_tensor_scan(hview, a_t, i2_t, state[:, e:e + 1],
                                                 ALU.mult, ALU.add))
            fpool.put(ka, tscan)
            fpool.put(k2, tscan)
            W("v", tscan)
            tst = I("v", ve.tensor_copy(state[:, e:e + 1], hview[:, TT - 1:TT]))
            state_toks[e] = tst
            if warm:
                fpool.put(kh, tst)
            else:
                kq, hq_t = bpool.get("v")
                W("v", tscan)
                thq = I("v", ve.tensor_mul(hq_t, hview, hview))
                pend_rms = (e, kq, hq_t, thq)
        if warm:
            xT_free = pr.mark("pe")
            continue
        rms_tok = emit_rms(pend_rms)
        pend_rms = None

        W("v", rms_tok)
        tr = I("v", ve.tensor_scalar(rmsrow[0:1, :], pb[7][0:1, :], 1.0 / D, EPS,
                                     ALU.mult, ALU.add))
        pb_free[7] = tr
        W("s", tr)
        tsq = I("s", se.activation(rstd_b[0:1, :], rmsrow[0:1, :], AF.Sqrt))
        W("v", tsq)
        trc = I("v", ve.reciprocal(rmsrow[0:1, :], rstd_b[0:1, :]))
        W("pe", trc)
        kb = bank()
        tbc = I("pe", te.matmul(pb[kb][:], ones_r[:], rmsrow[0:1, :], start=True, stop=True))
        W("s", tbc)
        trb = I("s", se.copy(rstd_b[:], pb[kb][:]))
        pb_free[kb] = trb

        for tb in range(TB):
            txk = qac.dma(xtok[:, tb * D:(tb + 1) * D],
                          xD[r0 + tb * P:r0 + (tb + 1) * P, :],
                          pre=[xtok_war] if tb == 0 else None, group=(tb > 0))
        qac.close()
        xtok_tok = txk
        for e in range(DC):
            bi = e % 3
            q_ = qsp if e % 2 == 0 else qac
            wtg = q_.dma(wib[bi][:].rearrange("p (db q) -> p db q", q=P),
                         wgB.rearrange("eb (db pp) q -> eb pp db q", pp=P)[e],
                         pre=[wib_free[bi], wgB_tok[e]])
            W("pe", wtg)
            kg = bank()
            for d in range(DC):
                mg = te.matmul(pb[kg][:], wib[bi][:, d * P:(d + 1) * P],
                               xT[:, d * TT:(d + 1) * TT], start=(d == 0), stop=(d == DC - 1))
            tok_g = I("pe", mg)
            wib_free[bi] = tok_g
            W("s", tok_g)
            k1, sg1_t = fpool.get("s")
            ts1 = I("s", se.activation(sg1_t, pb[kg][:], AF.Sigmoid))
            W("v", ts1)
            W("v", ld["gnw"])
            k2, sg_t = fpool.get("v")
            tsg = I("v", ve.scalar_tensor_tensor(sg_t, pb[kg][:], gnw[:, e:e + 1], sg1_t,
                                                 ALU.mult, ALU.mult))
            pb_free[kg] = tsg
            fpool.put(k1, tsg)
            W("v", trb)
            W("v", state_toks[e])
            k3, o2_t = fpool.get("v")
            to2 = I("v", ve.tensor_mul(o2_t, hbuf[:, e * TT:(e + 1) * TT], rstd_b[:]))
            if e == 0:
                W("v", o4_war)
            W("v", to2)
            W("v", tsg)
            to4 = I("v", ve.tensor_mul(o4[:, e * TT:(e + 1) * TT], o2_t, sg_t))
            fpool.put(k2, to4)
            fpool.put(k3, to4)
        o4_tok = to4
        xT_free = pr.mark("pe")

        evac_last = None
        for sweep in range(2):
            bank_tok = [None] * 8
            for d in range(DC):
                bi = d % 3
                wto = qsp.dma(wob[bi][:],
                              woB[d * P:(d + 1) * P, sweep * 1024:(sweep + 1) * 1024],
                              pre=[wob_free[bi], woB_tok[d // 4]])
                W("pe", wto)
                W("pe", o4_tok)
                for tb in range(TB):
                    for eg in range(2):
                        k = tb * 2 + eg
                        if d == 0:
                            W("pe", pb_free[k])
                        mo = te.matmul(pb[k][:],
                                       o4[:, d * TT + tb * P: d * TT + (tb + 1) * P],
                                       wob[bi][:, eg * TT:(eg + 1) * TT],
                                       start=(d == 0), stop=(d == DC - 1))
                tok_d = I("pe", mo)
                if d == DC - 1:
                    for tb in range(TB):
                        for eg in range(2):
                            bank_tok[tb * 2 + eg] = tok_d
                wob_free[bi] = tok_d
            for tb in range(TB):
                for eg in range(2):
                    k = tb * 2 + eg
                    idx = tb * 4 + sweep * 2 + eg
                    col = tb * D + sweep * 1024 + eg * TT
                    W("v", bank_tok[k])
                    W("v", xtok_tok)
                    if sweep == 0 and tb == 0 and eg == 0:
                        W("v", hbuf_war)
                    tv = I("v", ve.scalar_tensor_tensor(r1[:, col:col + TT], pb[k][:], 1.0,
                                                        xtok[:, col:col + TT],
                                                        ALU.mult, ALU.add,
                                                        accum_out=accm[:, idx:idx + 1]))
                    pb_free[k] = tv
                    W("s", tv)
                    ksr, scr_t = fpool.get("s")
                    tss = I("s", se.activation(scr_t, r1[:, col:col + TT], AF.Square,
                                               accum_out=accs[:, idx:idx + 1]))
                    fpool.put(ksr, tss)
                    evac_last = (tv, tss)
        o4_free = pr.mark("pe")
        W("v", evac_last[0])
        W("v", evac_last[1])
        trs = ln_finalize(16, TB)
        y1_toks = []
        for tb in range(TB):
            W("v", trs)
            ty = I("v", ve.tensor_scalar(r1[:, tb * D:(tb + 1) * D], r1[:, tb * D:(tb + 1) * D],
                                         lnstat[:, 2 * TB + tb:2 * TB + tb + 1],
                                         lnstat[:, 7 * TB + tb:7 * TB + tb + 1],
                                         ALU.subtract, ALU.mult))
            y1_toks.append(ty)
        xtok_v_done = pr.mark("v")

        xrot_toks = []

        def emit_h128d(pz):
            d2, kx2, x1a, tok2 = pz
            W("pe", tok2)
            W("pe", ld["h128"])
            kz = bank()
            tmz = I("pe", te.matmul(pb[kz][:, 0:HALF], h128[:], x1a[:, 0:HALF],
                                    start=True, stop=True))
            bpool.put(kx2, tmz)
            W("s", tmz)
            if th == 0 and d2 == 0:
                W("s", o4_free)
                W("s", o4_war)
                W("s", fb_free)
            if th == 1 and d2 == 0:
                W("s", xrot_toks[0])
            tev2 = I("s", se.copy(fb[:, d2 * HALF:(d2 + 1) * HALF], pb[kz][:, 0:HALF]))
            pb_free[kz] = tev2
            return tev2

        for th in range(2):
            pendD = None
            for d in range(DC):
                W("pe", y1_toks[2 * th])
                W("pe", y1_toks[2 * th + 1])
                W("pe", ld["ident"])
                kz2 = bank()
                for tbl in range(2):
                    tb2 = 2 * th + tbl
                    ttr = I("pe", te.transpose(
                        pb[kz2][:, tbl * P:(tbl + 1) * P],
                        r1[:, tb2 * D + d * P: tb2 * D + (d + 1) * P], ident[:]))
                W("s", ttr)
                W("s", ld["suug"])
                kx, x1_t = bpool.get("s")
                tx1 = I("s", se.activation(x1_t[:, 0:HALF], pb[kz2][:, 0:HALF],
                                           AF.Copy, scale=suug[:, d:d + 1]))
                pb_free[kz2] = tx1
                if pendD is not None:
                    tev = emit_h128d(pendD)
                pendD = (d, kx, x1_t, tx1)
            tev = emit_h128d(pendD)
            pendD = None
            fb_tok = tev
            src_b, dst_b = fb, fb2
            tok = fb_tok
            for li, s_ in enumerate([1, 2, 4, 8]):
                blk = s_ * HALF
                vin = src_b[:].rearrange("p (a two bt) -> p a two bt", two=2, bt=blk)
                vout = dst_b[:].rearrange("p (a two bt) -> p a two bt", two=2, bt=blk)
                W("v", tok)
                tp = I("v", ve.tensor_add(vout[:, :, 0, :], vin[:, :, 0, :], vin[:, :, 1, :]))
                tok = I("v", ve.tensor_sub(vout[:, :, 1, :], vin[:, :, 0, :], vin[:, :, 1, :]))
                src_b, dst_b = dst_b, src_b
            W("v", tok)
            if th == 0:
                W("v", xT_free)
            vz_in = src_b[:].rearrange("p (c t) -> p c t", t=HALF)
            vz_out = xrot[:].rearrange("p (c tt) -> p c tt", tt=TT)[:, :, th * HALF:(th + 1) * HALF]
            txr = I("v", ve.tensor_copy(vz_out, vz_in))
            xrot_toks.append(txr)
        xrot_tok = xrot_toks[-1]
        fb_free = xrot_tok
        r1_pe_done = pr.mark("pe")

        if p == 1:
            emit_retile()
            upT_fin = [sem_maxima(upT_tok[ib]) for ib in range(4)]
            gtT_fin = [sem_maxima(gtT_tok[ib]) for ib in range(4)]
        pend = []
        zt_toks = [None, None]

        def emit_h128_evac(pz):
            ic2, k42, t4a, tok42 = pz
            W("pe", tok42)
            kz = bank()
            tmz = I("pe", te.matmul(pb[kz][:], h128[:], t4a, start=True, stop=True))
            bpool.put(k42, tmz)
            W("s", tmz)
            if ic2 == 0:
                W("s", o4_free)
                W("s", fb_free)
            te1 = I("s", se.copy(zt0[:, ic2 * HALF:(ic2 + 1) * HALF], pb[kz][:, 0:HALF]))
            if ic2 == 0:
                W("s", xtok_v_done)
            te2 = I("s", se.copy(zt1[:, ic2 * HALF:(ic2 + 1) * HALF], pb[kz][:, HALF:TT]))
            pb_free[kz] = te2
            zt_toks[0] = te1
            zt_toks[1] = te2

        W("s", wsg_tok)
        q_gt = qsp if p == 1 else qac
        for igg in range(32):
            gi = igg % 2
            pre_u = [wub_free[gi]] + upT_fin[igg // 8]
            if igg == 0:
                pre_u = pre_u + wib_free + wfb_free + wob_free
            up_tok = qsp.dma(wub[gi][:], wupT[igg], pre=pre_u)
            gt_tok = q_gt.dma(wgb[gi][:], wgtT[igg],
                              pre=[wgb_free[gi]] + gtT_fin[igg // 8])
            for icl in range(2):
                ic = igg * 2 + icl
                W("pe", up_tok)
                W("pe", xrot_tok)
                ku = bank()
                for d in range(DC):
                    mu = te.matmul(pb[ku][:],
                                   wub[gi][:, d * HALF + icl * P: d * HALF + (icl + 1) * P],
                                   xrot[:, d * TT:(d + 1) * TT], start=(d == 0), stop=(d == DC - 1))
                tok_u = I("pe", mu)
                W("pe", gt_tok)
                kg2 = bank()
                for d in range(DC):
                    mgt = te.matmul(pb[kg2][:],
                                    wgb[gi][:, d * HALF + icl * P: d * HALF + (icl + 1) * P],
                                    xrot[:, d * TT:(d + 1) * TT], start=(d == 0), stop=(d == DC - 1))
                tok_g2 = I("pe", mgt)
                if icl == 1:
                    wub_free[gi] = tok_u
                    wgb_free[gi] = tok_g2
                if len(pend) >= 2:
                    emit_h128_evac(pend.pop(0))
                W("s", tok_g2)
                k1, t1_t = fpool.get("s")
                tt1 = I("s", se.activation(t1_t, pb[kg2][:], AF.Sigmoid,
                                           scale=wsg[:, ic:ic + 1]))
                W("v", tt1)
                k2, t2_t = fpool.get("v")
                tt2 = I("v", ve.scalar_tensor_tensor(t2_t, pb[kg2][:], wsg[:, ic:ic + 1],
                                                     t1_t, ALU.mult, ALU.mult))
                pb_free[kg2] = tt2
                fpool.put(k1, tt2)
                W("v", tok_u)
                W("v", tt2)
                k3, t3_t = fpool.get("v")
                tt3 = I("v", ve.scalar_tensor_tensor(t3_t, pb[ku][:], wsu[:, ic:ic + 1],
                                                     t2_t, ALU.mult, ALU.mult))
                pb_free[ku] = tt3
                fpool.put(k2, tt3)
                W("v", sdp_tok)
                W("v", tt3)
                k4, t4_t = bpool.get("v")
                tt4 = I("v", ve.tensor_scalar(t4_t, t3_t, sdpc[:, ic:ic + 1], None, ALU.mult))
                fpool.put(k3, tt4)
                pend.append((ic, k4, t4_t, tt4))
        while pend:
            emit_h128_evac(pend.pop(0))
        xrot_free = pr.mark("pe")

        if p == 1:
            ring3 = [foldf[2], foldf2[2], foldf2[1]]
            free3 = [None, None, None]
            prepb_free = [None, None]
            jj = 0
            for piece in range(8):
                for blk in range(DC):
                    s3 = jj % 3
                    p2 = jj % 2
                    pre_l = [free3[s3]] + (fold_all if jj == 0 else []) + fold2_tok[blk]
                    lt = qac.dma(ring3[s3][:],
                                 wdnF[blk * P:(blk + 1) * P, piece * 1024:(piece + 1) * 1024],
                                 pre=pre_l)
                    W("v", lt)
                    W("v", wsd_tok)
                    W("v", prepb_free[p2])
                    vt = I("v", ve.tensor_scalar(prepb[p2][:], ring3[s3][:],
                                                 wsd[:, blk:blk + 1], None, ALU.mult))
                    free3[s3] = vt
                    st = qsp.dma(wdnB[blk * P:(blk + 1) * P, piece * 1024:(piece + 1) * 1024],
                                 prepb[p2][:], pre=[vt])
                    prepb_free[p2] = st
                    wdnp_tok[piece].append(st)
                    jj += 1
            for piece in range(8):
                wdnp_tok[piece] = sem_maxima(wdnp_tok[piece])

        preA = {}
        if p + 1 < n_pass:
            for e2 in range(3):
                preA[("i", e2)] = qsp.dma(
                    wib[e2 % 3][:].rearrange("p (db q) -> p db q", q=P),
                    wiB.rearrange("eb (db pp) q -> eb pp db q", pp=P)[e2],
                    pre=[xrot_free, wib_free[e2 % 3], wiB_tok[e2]])
                preA[("f", e2)] = qac.dma(
                    wfb[e2 % 3][:].rearrange("p (db q) -> p db q", q=P),
                    wfB.rearrange("eb (db pp) q -> eb pp db q", pp=P)[e2],
                    pre=[wfb_free[e2 % 3], wfB_tok[e2]])

        if p + 1 < n_pass:
            xT_pre[p + 1] = load_xT(p + 1, extra_pre=[xrot_free])
        ev_last = None
        for sweep in range(2):
            bank_tok = [None] * 8
            for c in range(IC):
                if p == 1 and sweep == 0:
                    pi2 = c % 2
                    ttw = qac.dma(wdb[pi2][:], wdnB[0:D, c * P:(c + 1) * P],
                                  pre=[wdb_tok[pi2]] + wdnp_tok[c // 8],
                                  transpose=True)
                    st2 = qac.dma(wdnT[c], wdb[pi2][:], pre=[ttw])
                    wdb_tok[pi2] = st2
                    dnT_tok.append(st2)
                gslot = grr[0]
                grr[0] = (grr[0] + 1) % len(wdbg)
                pre_w = [wdbg_free[gslot]]
                if p == 1:
                    pre_w.append(dnT_tok[c])
                    if sweep == 0 and c == 0:
                        pre_w += fold_all
                elif p == 2 and sweep == 0 and c < len(wdbg):
                    pre_w += dnT_last
                wtd = qsp.dma(wdbg[gslot][:], wdnT[c][:, sweep * 1024:(sweep + 1) * 1024],
                              pre=pre_w)
                W("pe", wtd)
                W("pe", zt_toks[0])
                W("pe", zt_toks[1])
                for tbl in range(4):
                    ztX = zt0 if tbl < 2 else zt1
                    tl = tbl % 2
                    for jg in range(2):
                        k = tbl * 2 + jg
                        if c == 0:
                            W("pe", pb_free[k])
                        md = te.matmul(pb[k][:],
                                       ztX[:, c * HALF + tl * P: c * HALF + (tl + 1) * P],
                                       wdbg[gslot][:, jg * TT:(jg + 1) * TT],
                                       start=(c == 0), stop=(c == IC - 1))
                tok_c = I("pe", md)
                wdbg_free[gslot] = tok_c
                if c == IC - 1:
                    for k in range(8):
                        bank_tok[k] = tok_c
            for tbl in range(4):
                for jg in range(2):
                    k = tbl * 2 + jg
                    idx = tbl * 4 + sweep * 2 + jg
                    col = tbl * D + sweep * 1024 + jg * TT
                    W("v", bank_tok[k])
                    tv = I("v", ve.scalar_tensor_tensor(r1[:, col:col + TT], pb[k][:], 1.0,
                                                        r1[:, col:col + TT],
                                                        ALU.mult, ALU.add,
                                                        accum_out=accm[:, idx:idx + 1]))
                    pb_free[k] = tv
                    W("s", tv)
                    ksr, scr_t = fpool.get("s")
                    tss = I("s", se.activation(scr_t, r1[:, col:col + TT], AF.Square,
                                               accum_out=accs[:, idx:idx + 1]))
                    fpool.put(ksr, tss)
                    ev_last = (tv, tss)
        if p == 1:
            dnT_last = sem_maxima(dnT_tok)
        W("v", ev_last[0])
        W("v", ev_last[1])
        trs2 = ln_finalize(16, TB)
        out_toks = []
        for tb in range(TB):
            W("v", trs2)
            ty2 = I("v", ve.tensor_scalar(r1[:, tb * D:(tb + 1) * D],
                                          r1[:, tb * D:(tb + 1) * D],
                                          lnstat[:, 2 * TB + tb:2 * TB + tb + 1],
                                          lnstat[:, 7 * TB + tb:7 * TB + tb + 1],
                                          ALU.subtract, ALU.mult))
            tdo = qdo.dma(outD[(p - 1) * TT + tb * P: (p - 1) * TT + (tb + 1) * P, :],
                          r1[:, tb * D:(tb + 1) * D], pre=[ty2])
            out_toks.append(tdo)
        g_pe_done = pr.mark("pe")
        xtok_war = g_pe_done
        o4_war = g_pe_done
        hbuf_war = list(out_toks)

    for k, v in qdo.all_last():
        sy.wait_ge(pr.sem[k], v)
        ge.wait_ge(pr.sem[k], v)
    es.close()
    assert nc.sbuf_base <= 17408, f"auto allocs overflow reserve: {nc.sbuf_base}"
    return nc


_NC_CACHE = {}


def _get_nc(n_full=8):
    if n_full not in _NC_CACHE:
        _NC_CACHE[n_full] = build_nc(n_full)
    return _NC_CACHE[n_full]


def make_aux(full):
    return {
        "wi": np.ascontiguousarray(full["L0_attn_Wi"], dtype=np.float32),
        "wf": np.ascontiguousarray(full["L0_attn_Wf"], dtype=np.float32),
        "wg": np.ascontiguousarray(full["L0_attn_Wg"], dtype=np.float32),
        "wo": np.ascontiguousarray(full["L0_attn_Wo"], dtype=np.float32),
        "wup": np.ascontiguousarray(full["L0_ffn_W_up"], dtype=np.float32),
        "wgt": np.ascontiguousarray(full["L0_ffn_W_gate"], dtype=np.float32),
        "wdn": np.ascontiguousarray(full["L0_ffn_W_down"], dtype=np.float32),
        "gnw": np.ascontiguousarray(np.asarray(full["L0_attn_g_norm_weight"], dtype=np.float32).reshape(DC, P).T),
        "suug": np.ascontiguousarray(np.asarray(full["L0_ffn_SU_ug"], dtype=np.float32).reshape(DC, P).T),
        "sudp": np.ascontiguousarray(np.asarray(full["L0_ffn_SU_dp"], dtype=np.float32).reshape(IC, P).T),
        "wsu": np.ascontiguousarray(np.asarray(full["L0_ffn_Wscale_ug"], dtype=np.float32)[:INTER].reshape(IC, P).T),
        "wsg": np.ascontiguousarray(np.asarray(full["L0_ffn_Wscale_ug"], dtype=np.float32)[INTER:].reshape(IC, P).T),
        "wsd": np.ascontiguousarray(np.asarray(full["L0_ffn_Wscale_dp"], dtype=np.float32).reshape(DC, P).T),
        "h128": _hadamard128(),
        "ident": np.eye(P, dtype=np.float32),
    }


def kernel(**inputs):
    n_full = 8
    ct = n_full * TT
    rows = (n_full + 1) * TT
    nc = _get_nc(n_full)
    x = np.asarray(inputs["L0_attn_hidden_states"])
    aux = make_aux(inputs)
    in_maps = []
    for c in range(8):
        b, q = c // 4, c % 4
        xs = np.zeros((rows, D), np.float32)
        t0 = q * ct
        w0 = max(0, t0 - NWARM)
        nw = t0 - w0
        if nw > 0:
            xs[NWARM - nw:NWARM] = x[b, w0:t0]
        xs[NWARM:] = x[b, t0:t0 + ct]
        m = dict(aux)
        m["x"] = xs
        in_maps.append(m)
    res = run_bass_kernel_spmd(nc, in_maps, core_ids=list(range(8)))
    out = np.empty((2, 16384, D), np.float32)
    for c in range(8):
        b, q = c // 4, c % 4
        out[b, q * ct:(q + 1) * ct] = res.results[c]["out"]
    return out


# version 12
# speedup vs baseline: 1.3153x; 1.0088x over previous
import numpy as np
import ml_dtypes

import concourse.bass as bass
import concourse.mybir as mybir
from concourse.bass_utils import run_bass_kernel_spmd

F32 = mybir.dt.float32
BF16 = mybir.dt.bfloat16
AF = mybir.ActivationFunctionType
ALU = mybir.AluOpType
BF = ml_dtypes.bfloat16

P = 128
D = 2048
DC = 16
INTER = 8192
IC = 64
TT = 512
TB = 4
HALF = 256
NWARM = 512
S0 = 1.0 / (64.0 * np.sqrt(2048.0))
EPS = 1e-5
ENG = ("pe", "v", "s", "g", "sp")


def _hadamard128():
    i = np.arange(128)
    pc = np.zeros((128, 128), np.int64)
    for b in range(7):
        pc += ((i[:, None] >> b) & 1) * ((i[None, :] >> b) & 1)
    return np.where(pc % 2 == 0, 1.0, -1.0).astype(BF)


class Prog:
    def __init__(self, nc, es):
        self.nc = nc
        self.sem = {}
        self.cnt = {}
        for name in ("pe", "v", "s", "g"):
            self.sem[name] = es.enter_context(nc.semaphore(name))
            self.cnt[name] = 0
        self.wm = {e: {} for e in ENG}
        self.eng = {"pe": nc.tensor, "v": nc.vector, "s": nc.scalar,
                    "g": nc.gpsimd, "sp": nc.sync}

    def wait(self, eng, tok):
        if tok is None:
            return
        if isinstance(tok, list):
            for t in tok:
                self.wait(eng, t)
            return
        sname, val = tok
        if self.wm[eng].get(sname, 0) >= val:
            return
        self.eng[eng].wait_ge(self.sem[sname], val)
        self.wm[eng][sname] = val

    def inc(self, eng, inst):
        self.cnt[eng] += 1
        inst.then_inc(self.sem[eng], 1)
        return (eng, self.cnt[eng])

    def mark(self, eng):
        return (eng, self.cnt[eng])


class DQ:

    def __init__(self, pr, eng_name, basename, es, nsem):
        self.pr = pr
        self.eng_name = eng_name
        self.eng = pr.eng[eng_name]
        self.names = []
        for i in range(nsem):
            name = f"{basename}{i}"
            pr.sem[name] = es.enter_context(pr.nc.semaphore(name))
            pr.cnt[name] = 0
            self.names.append(name)
        self.i = 0
        self.cur = None

    def dma(self, out, in_, pre=None, group=False, **kw):
        pr = self.pr
        if group and self.cur is not None:
            k = self.cur
        else:
            k = self.names[self.i]
            self.i = (self.i + 1) % len(self.names)
            if pr.cnt[k] > 0:
                pr.wait(self.eng_name, (k, pr.cnt[k]))
            self.cur = k
        pr.wait(self.eng_name, pre)
        inst = self.eng.dma_start(out=out, in_=in_, **kw)
        pr.cnt[k] += 16
        inst.then_inc(pr.sem[k], 16)
        return (k, pr.cnt[k])

    def close(self):
        self.cur = None

    def all_last(self):
        return [(k, self.pr.cnt[k]) for k in self.names if self.pr.cnt[k] > 0]


class Pool:

    def __init__(self, pr, tiles):
        self.pr = pr
        self.tiles = tiles
        self.free = [None] * len(tiles)
        self.i = 0

    def get(self, eng):
        k = self.i
        self.i = (self.i + 1) % len(self.tiles)
        self.pr.wait(eng, self.free[k])
        return k, self.tiles[k]

    def put(self, k, tok):
        self.free[k] = tok


def sem_maxima(toks):
    seen = {}
    for t_ in toks:
        if t_ is None:
            continue
        seen[t_[0]] = max(seen.get(t_[0], 0), t_[1])
    return [(k, v) for k, v in seen.items()]


def build_nc(n_full=8):
    nc = bass.Bass("TRN2", target_bir_lowering=False)
    n_pass = n_full + 1
    rows = n_pass * TT

    xD = nc.dram_tensor("x", [rows, D], F32, kind="ExternalInput").ap()
    wiD = nc.dram_tensor("wi", [D, D], F32, kind="ExternalInput").ap()
    wfD = nc.dram_tensor("wf", [D, D], F32, kind="ExternalInput").ap()
    wgD = nc.dram_tensor("wg", [D, D], F32, kind="ExternalInput").ap()
    woD = nc.dram_tensor("wo", [D, D], F32, kind="ExternalInput").ap()
    wupD = nc.dram_tensor("wup", [INTER, D], F32, kind="ExternalInput").ap()
    wgtD = nc.dram_tensor("wgt", [INTER, D], F32, kind="ExternalInput").ap()
    wdnD = nc.dram_tensor("wdn", [D, INTER], F32, kind="ExternalInput").ap()
    gnwD = nc.dram_tensor("gnw", [P, DC], F32, kind="ExternalInput").ap()
    suugD = nc.dram_tensor("suug", [P, DC], F32, kind="ExternalInput").ap()
    sudpD = nc.dram_tensor("sudp", [P, IC], F32, kind="ExternalInput").ap()
    wsuD = nc.dram_tensor("wsu", [P, IC], F32, kind="ExternalInput").ap()
    wsgD = nc.dram_tensor("wsg", [P, IC], F32, kind="ExternalInput").ap()
    wsdD = nc.dram_tensor("wsd", [P, DC], F32, kind="ExternalInput").ap()
    h128D = nc.dram_tensor("h128", [P, P], BF16, kind="ExternalInput").ap()
    identD = nc.dram_tensor("ident", [P, P], F32, kind="ExternalInput").ap()
    outD = nc.dram_tensor("out", [n_full * TT, D], F32, kind="ExternalOutput").ap()

    xbD = nc.dram_tensor("xb", [rows, D], BF16, kind="Internal").ap()
    wiB = nc.dram_tensor("wiB", [DC, D, P], BF16, kind="Internal").ap()
    wfB = nc.dram_tensor("wfB", [DC, D, P], BF16, kind="Internal").ap()
    wgB = nc.dram_tensor("wgB", [DC, D, P], BF16, kind="Internal").ap()
    woB = nc.dram_tensor("woB", [D, D], BF16, kind="Internal").ap()
    wupB = nc.dram_tensor("wupB", [INTER, D], BF16, kind="Internal").ap()
    wgtB = nc.dram_tensor("wgtB", [INTER, D], BF16, kind="Internal").ap()
    wdnB = nc.dram_tensor("wdnB", [D, INTER], BF16, kind="Internal").ap()
    wdnF = nc.dram_tensor("wdnF", [D, INTER], F32, kind="Internal").ap()
    wupT = nc.dram_tensor("wupT", [32, P, DC * HALF], BF16, kind="Internal").ap()
    wgtT = nc.dram_tensor("wgtT", [32, P, DC * HALF], BF16, kind="Internal").ap()
    wdnT = nc.dram_tensor("wdnT", [IC, P, 2048], BF16, kind="Internal").ap()

    from contextlib import ExitStack
    es = ExitStack()
    pr = Prog(nc, es)
    W, I = pr.wait, pr.inc
    te, ve, se, ge, sy = nc.tensor, nc.vector, nc.scalar, nc.gpsimd, nc.sync
    qsp = DQ(pr, "sp", "qw", es, 8)
    qac = DQ(pr, "s", "qa", es, 8)
    qgp = DQ(pr, "g", "qg", es, 6)
    qdo = DQ(pr, "sp", "qo", es, 2)

    off = [17408]

    def at(name, shape, dtype, o=None):
        nonlocal off
        nbytes = int(np.prod(shape[1:])) * mybir.dt.size(dtype)
        if o is None:
            o = off[0]
            off[0] += (nbytes + 31) // 32 * 32
        return nc.alloc_sbuf_tensor_at(name, shape, dtype, offset=o)

    h128 = at("h128", [P, P], BF16)
    ident = at("ident", [P, P], F32)
    gnw = at("gnw", [P, DC], F32)
    suug = at("suug", [P, DC], F32)
    sdpc = at("sdpc", [P, IC], F32)
    wsu = at("wsu", [P, IC], F32)
    wsg = at("wsg", [P, IC], F32)
    wsd = at("wsd", [P, DC], F32)
    ones_c = at("ones_c", [P, 1], BF16)
    ones_r = at("ones_r", [1, P], F32)
    state = at("state", [P, DC], F32)
    rmsrow = at("rmsrow", [1, TT], F32)
    rstd_b = at("rstd_b", [P, TT], F32)
    accm = at("accm", [P, 16], F32)
    accs = at("accs", [P, 16], F32)
    lnstat = at("lnstat", [P, 10 * TB], F32)
    wdb = [at(f"wdb{i}", [P, 2048], BF16) for i in range(2)]
    fpool_t = [at(f"fp{i}", [P, TT], F32) for i in range(7)]
    bpool_t = [at(f"bp{i}", [P, TT], BF16) for i in range(4)]
    prepb = [at(f"prepb{i}", [P, 1024], BF16) for i in range(2)]
    base_small = off[0]
    assert base_small <= 65504, base_small

    OFF_W = 65504
    OFF_X = 98272
    OFF_H = 131040
    OFF_Z = 163808
    OFF_WS = 196576
    OFF_XT = 212960

    wib = [at(f"wib{i}", [P, D], BF16, OFF_W + i * 4096) for i in range(3)]
    wfb = [at(f"wfb{i}", [P, D], BF16, OFF_W + 12288 + i * 4096) for i in range(3)]
    wob = [at(f"wob{i}", [P, 1024], BF16, OFF_W + 24576 + i * 2048) for i in range(3)]
    wub = [at(f"wub{i}", [P, DC * HALF], BF16, OFF_W + i * 8192) for i in range(2)]
    wgb = [at(f"wgb{i}", [P, DC * HALF], BF16, OFF_W + 16384 + i * 8192) for i in range(2)]

    xtok = at("xtok", [P, TB * D], F32, OFF_X)
    zt1 = at("zt1", [P, IC * HALF], BF16, OFF_X)

    hbuf = at("hbuf", [P, DC * TT], F32, OFF_H)
    r1 = at("r1", [P, TB * D], F32, OFF_H)

    o4 = at("o4", [P, DC * TT], BF16, OFF_Z)
    fb = at("fb", [P, DC * HALF], F32, OFF_Z)
    fb2 = at("fb2", [P, DC * HALF], F32, OFF_Z + 16 * 1024)
    zt0 = at("zt0", [P, IC * HALF], BF16, OFF_Z)

    fold3a = at("fold3a", [P, 1024], F32)
    fold3b = at("fold3b", [P, 1024], F32)
    assert off[0] <= 65504, off[0]
    foldf = [at(f"foldf{i}", [P, 1024], F32, OFF_WS + i * 4096) for i in range(2)] + [fold3a]
    foldf2 = [at(f"foldf2{i}", [P, 1024], F32, OFF_WS + 8192 + i * 4096) for i in range(2)] + [fold3b]
    wdbg = [at(f"wdbg{i}", [P, 1024], BF16, OFF_WS + i * 2048) for i in range(6)]

    xT = at("xT", [P, DC * TT], BF16, OFF_XT)
    xrot = at("xrot", [P, DC * TT], BF16, OFF_XT)

    pb = [nc.alloc_psum_tensor(f"pb{i}", [P, TT], F32) for i in range(8)]
    pb_free = [None] * 8
    rr = [0]

    def bank(eng="pe", k=None):
        if k is None:
            k = rr[0]
            rr[0] = (rr[0] + 1) % 6
        W(eng, pb_free[k])
        return k

    fpool = Pool(pr, [t[:] for t in fpool_t])
    bpool = Pool(pr, [t[:] for t in bpool_t])

    ld = {}
    ldt = None
    for name, dst, srcT in [("gnw", gnw, gnwD), ("suug", suug, suugD), ("sdp", sdpc, sudpD),
                            ("wsu", wsu, wsuD), ("wsg", wsg, wsgD), ("wsd", wsd, wsdD),
                            ("h128", h128, h128D), ("ident", ident, identD)]:
        ldt = qsp.dma(dst[:], srcT, group=(ldt is not None))
    for name in ("gnw", "suug", "sdp", "wsu", "wsg", "wsd", "h128", "ident"):
        ld[name] = ldt
    qsp.close()

    ve.memset(ones_c[:], 1.0)
    ve.memset(ones_r[:], 1.0)
    tk0 = I("v", ve.memset(state[:], 0.0))
    W("v", ld["sdp"])
    sdp_tok = I("v", ve.tensor_scalar(sdpc[:], sdpc[:], S0 / 2.0, None, ALU.mult))
    W("v", ld["wsu"])
    wsu_tok = I("v", ve.tensor_scalar(wsu[:], wsu[:], 64.0 * S0, None, ALU.mult))
    W("v", ld["wsg"])
    wsg_tok = I("v", ve.tensor_scalar(wsg[:], wsg[:], 64.0 * S0, None, ALU.mult))
    W("v", ld["wsd"])
    wsd_tok = I("v", ve.tensor_scalar(wsd[:], wsd[:], 64.0, None, ALU.mult))

    xb_tok = [None] * n_pass
    for p in range(3):
        xb_tok[p] = qgp.dma(xbD[p * TT:(p + 1) * TT, :], xD[p * TT:(p + 1) * TT, :])
    wiB_tok, wfB_tok, wgB_tok = [], [], []
    for eb in range(DC):
        s_ = wiD.rearrange("d (eb q) -> eb d q", q=P)[eb]
        wiB_tok.append(qgp.dma(wiB[eb], s_))
        s_ = wfD.rearrange("d (eb q) -> eb d q", q=P)[eb]
        wfB_tok.append(qgp.dma(wfB[eb], s_))
    for eb in range(DC):
        s_ = wgD.rearrange("d (eb q) -> eb d q", q=P)[eb]
        wgB_tok.append(qgp.dma(wgB[eb], s_))
    woB_tok = []
    for b in range(4):
        woB_tok.append(qgp.dma(woB[b * TT:(b + 1) * TT, :], woD[b * TT:(b + 1) * TT, :]))

    cast_jobs = []
    upB_tok = [[] for _ in range(4)]
    gtB_tok = [[] for _ in range(4)]
    for ib in range(4):
        for j in range(4):
            r0 = ib * 2048 + j * 512
            cast_jobs.append(("up", ib, r0))
        for j in range(4):
            r0 = ib * 2048 + j * 512
            cast_jobs.append(("gt", ib, r0))
    for p in range(3, n_pass):
        cast_jobs.append(("x", p, p * TT))
    cj_i = [0]

    def pump_cast(n):
        while n > 0 and cj_i[0] < len(cast_jobs):
            kind, a, r0 = cast_jobs[cj_i[0]]
            cj_i[0] += 1
            n -= 1
            if kind == "up":
                upB_tok[a].append(qgp.dma(wupB[r0:r0 + 512, :], wupD[r0:r0 + 512, :]))
            elif kind == "gt":
                gtB_tok[a].append(qgp.dma(wgtB[r0:r0 + 512, :], wgtD[r0:r0 + 512, :]))
            else:
                xb_tok[a] = qgp.dma(xbD[r0:r0 + TT, :], xD[r0:r0 + TT, :])

    fold1_tok = [[] for _ in range(DC)]
    fold2_tok = [[] for _ in range(DC)]
    fold_all_toks = []
    fold_blocks = []
    for sweep in range(2):
        for jb in range(DC):
            for grp in range(8):
                if sweep == 0:
                    src_ap = wdnD[jb * P:(jb + 1) * P, grp * 1024:(grp + 1) * 1024]
                    dst_ap = wdnF[jb * P:(jb + 1) * P, grp * 1024:(grp + 1) * 1024]
                    dep_jb = None
                else:
                    gath = wdnF[jb * P:(jb + 1) * P, :].rearrange(
                        "p (g sub q) -> p sub g q", g=8, q=P)[:, grp, :, :]
                    src_ap = gath
                    dst_ap = gath
                    dep_jb = jb if grp == 0 else None
                fold_blocks.append((src_ap, dst_ap, sweep, dep_jb, jb))
    NF = len(fold_blocks)
    fold_free = [None, None, None]
    load_tok = [None] * NF

    def issue_fold_load(j):
        src_ap, dst_ap, sweep, dep_jb, jb = fold_blocks[j]
        pre = [fold_free[j % 3]]
        if dep_jb is not None:
            pre = pre + fold1_tok[dep_jb]
        load_tok[j] = qgp.dma(foldf[j % 3][:], src_ap, pre=pre)

    issue_fold_load(0)
    issue_fold_load(1)
    for j in range(NF):
        src_ap, dst_ap, sweep, dep_jb, jb = fold_blocks[j]
        pi3 = j % 3
        fa, fz = foldf[pi3], foldf2[pi3]
        tokf = load_tok[j]
        for s_ in (1, 2, 4):
            blklen = s_ * P
            vin = fa[:].rearrange("p (a two bt) -> p a two bt", two=2, bt=blklen)
            vout = fz[:].rearrange("p (a two bt) -> p a two bt", two=2, bt=blklen)
            W("g", tokf)
            I("g", ge.tensor_add(vout[:, :, 0, :], vin[:, :, 0, :], vin[:, :, 1, :]))
            tokf = I("g", ge.tensor_sub(vout[:, :, 1, :], vin[:, :, 0, :], vin[:, :, 1, :]))
            fa, fz = fz, fa
        st3 = qgp.dma(dst_ap, fa[:], pre=[tokf])
        fold_free[pi3] = st3
        fold_all_toks.append(st3)
        (fold1_tok if sweep == 0 else fold2_tok)[jb].append(st3)
        if j + 2 < NF:
            issue_fold_load(j + 2)
        if j >= 8 and j % 2 == 0:
            pump_cast(1)
    pump_cast(len(cast_jobs))
    fold_all = sem_maxima(fold_all_toks)

    wdb_tok = [None, None]
    upT_tok = [[] for _ in range(4)]
    gtT_tok = [[] for _ in range(4)]

    def emit_retile():
        n = 0
        for ib in range(4):
            for mat in ("up", "gt"):
                for dch in range(DC):
                    n += 1
                    pi2 = n % 2
                    if mat == "up":
                        srcB, dstT, ctoks, toks_out = wupB, wupT, upB_tok[ib], upT_tok
                    else:
                        srcB, dstT, ctoks, toks_out = wgtB, wgtT, gtB_tok[ib], gtT_tok
                    tt = qsp.dma(wdb[pi2][:],
                                 srcB[ib * 2048:(ib + 1) * 2048, dch * P:(dch + 1) * P],
                                 pre=[wdb_tok[pi2]] + ctoks, transpose=True)
                    st2 = qsp.dma(dstT[ib * 8:(ib + 1) * 8, :, dch * HALF:(dch + 1) * HALF]
                                  .rearrange("g p t -> p g t"),
                                  wdb[pi2][:].rearrange("p (g t) -> p g t", t=HALF),
                                  pre=[tt])
                    wdb_tok[pi2] = st2
                    toks_out[ib].append(st2)

    xT_free = None
    xtok_war = None
    hbuf_war = None
    o4_war = None
    o4_free = None
    fb_free = None
    xrot_free = None
    state_toks = [tk0] * DC
    wib_free = [None, None, None]
    wfb_free = [None, None, None]
    wob_free = [None, None, None]
    wub_free = [None, None]
    wgb_free = [None, None]
    wdbg_free = [None] * 6
    grr = [0]
    dnT_tok = []
    dnT_last = []
    wdnp_tok = [[] for _ in range(8)]
    upT_fin = None
    gtT_fin = None

    def ln_finalize(n_idx, ntb):
        tm = I("v", ve.tensor_reduce(lnstat[:, 0:ntb],
                                     accm[:, 0:n_idx].rearrange("p (tb s) -> p tb s", s=4),
                                     mybir.AxisListType.X, ALU.add))
        tq = I("v", ve.tensor_reduce(lnstat[:, TB:TB + ntb],
                                     accs[:, 0:n_idx].rearrange("p (tb s) -> p tb s", s=4),
                                     mybir.AxisListType.X, ALU.add))
        W("v", tm)
        tmn = I("v", ve.tensor_scalar(lnstat[:, 2 * TB:2 * TB + ntb], lnstat[:, 0:ntb],
                                      1.0 / D, None, ALU.mult))
        W("v", tmn)
        tms = I("v", ve.tensor_mul(lnstat[:, 3 * TB:3 * TB + ntb],
                                   lnstat[:, 2 * TB:2 * TB + ntb],
                                   lnstat[:, 2 * TB:2 * TB + ntb]))
        W("v", tq)
        tv1 = I("v", ve.tensor_scalar(lnstat[:, 4 * TB:4 * TB + ntb], lnstat[:, TB:TB + ntb],
                                      1.0 / D, EPS, ALU.mult, ALU.add))
        W("v", tms)
        W("v", tv1)
        tvar = I("v", ve.tensor_sub(lnstat[:, 5 * TB:5 * TB + ntb],
                                    lnstat[:, 4 * TB:4 * TB + ntb],
                                    lnstat[:, 3 * TB:3 * TB + ntb]))
        W("s", tvar)
        tsd = I("s", se.activation(lnstat[:, 6 * TB:6 * TB + ntb],
                                   lnstat[:, 5 * TB:5 * TB + ntb], AF.Sqrt))
        W("v", tsd)
        trs = I("v", ve.reciprocal(lnstat[:, 7 * TB:7 * TB + ntb],
                                   lnstat[:, 6 * TB:6 * TB + ntb]))
        return trs

    def load_xT(p2, extra_pre=None):
        r2_ = p2 * TT
        for d in range(DC):
            pre0 = None
            if d == 0:
                pre0 = [xT_free, xb_tok[p2]] + (extra_pre or [])
            txT = qac.dma(xT[:, d * TT:(d + 1) * TT],
                          xbD[r2_:r2_ + TT, d * P:(d + 1) * P],
                          pre=pre0, group=(d > 0), transpose=True)
        qac.close()
        return txT

    xT_pre = [None] * n_pass
    preA = {}

    for p in range(n_pass):
        warm = (p == 0)
        r0 = p * TT

        if xT_pre[p] is None:
            xT_pre[p] = load_xT(p)
        xT_tok = xT_pre[p]

        pend_rms = None
        preA_cur = preA if not warm else {}

        def emit_rms(pz):
            e2, kq2, hqa, tok2 = pz
            W("pe", tok2)
            mq = te.matmul(pb[7][0:1, :], ones_c[:], hqa,
                           start=(e2 == 0), stop=(e2 == DC - 1))
            tq2 = I("pe", mq)
            bpool.put(kq2, tq2)
            return tq2

        if not warm:
            W("pe", pb_free[7])
        for e in range(DC):
            bi = e % 3
            if ("i", e) in preA_cur:
                wti = preA_cur[("i", e)]
                wtf = preA_cur[("f", e)]
            else:
                wti = qsp.dma(wib[bi][:].rearrange("p (db q) -> p db q", q=P),
                              wiB.rearrange("eb (db pp) q -> eb pp db q", pp=P)[e],
                              pre=[wib_free[bi], wiB_tok[e]])
                wtf = qac.dma(wfb[bi][:].rearrange("p (db q) -> p db q", q=P),
                              wfB.rearrange("eb (db pp) q -> eb pp db q", pp=P)[e],
                              pre=[wfb_free[bi], wfB_tok[e]])
            W("pe", wti)
            W("pe", xT_tok)
            if pend_rms is not None:
                rms_tok = emit_rms(pend_rms)
                pend_rms = None
            ki = bank()
            for d in range(DC):
                mi = te.matmul(pb[ki][:], wib[bi][:, d * P:(d + 1) * P],
                               xT[:, d * TT:(d + 1) * TT], start=(d == 0), stop=(d == DC - 1))
            tok_i = I("pe", mi)
            wib_free[bi] = tok_i
            W("pe", wtf)
            kf = bank()
            for d in range(DC):
                mf = te.matmul(pb[kf][:], wfb[bi][:, d * P:(d + 1) * P],
                               xT[:, d * TT:(d + 1) * TT], start=(d == 0), stop=(d == DC - 1))
            tok_f = I("pe", mf)
            wfb_free[bi] = tok_f
            W("s", tok_f)
            ka, a_t = fpool.get("s")
            ta = I("s", se.activation(a_t, pb[kf][:], AF.Sigmoid))
            kn, sn_t = fpool.get("s")
            tsn = I("s", se.activation(sn_t, pb[kf][:], AF.Sigmoid, scale=-1.0))
            pb_free[kf] = tsn
            W("s", tok_i)
            ks, si_t = fpool.get("s")
            tsi = I("s", se.activation(si_t, pb[ki][:], AF.Sigmoid))
            W("v", tsi)
            k1, i1_t = fpool.get("v")
            t1 = I("v", ve.tensor_mul(i1_t, pb[ki][:], si_t))
            pb_free[ki] = t1
            fpool.put(ks, t1)
            W("v", tsn)
            W("v", t1)
            k2, i2_t = fpool.get("v")
            t2 = I("v", ve.tensor_mul(i2_t, i1_t, sn_t))
            fpool.put(k1, t2)
            fpool.put(kn, t2)
            W("v", ta)
            W("v", t2)
            W("v", state_toks[e])
            if warm:
                kh, h_dst = fpool.get("v")
                hview = h_dst
            else:
                if e == 0:
                    W("v", hbuf_war)
                hview = hbuf[:, e * TT:(e + 1) * TT]
            tscan = I("v", ve.tensor_tensor_scan(hview, a_t, i2_t, state[:, e:e + 1],
                                                 ALU.mult, ALU.add))
            fpool.put(ka, tscan)
            fpool.put(k2, tscan)
            W("v", tscan)
            tst = I("v", ve.tensor_copy(state[:, e:e + 1], hview[:, TT - 1:TT]))
            state_toks[e] = tst
            if warm:
                fpool.put(kh, tst)
            else:
                kq, hq_t = bpool.get("v")
                W("v", tscan)
                thq = I("v", ve.tensor_mul(hq_t, hview, hview))
                pend_rms = (e, kq, hq_t, thq)
        if warm:
            xT_free = pr.mark("pe")
            continue
        rms_tok = emit_rms(pend_rms)
        pend_rms = None

        W("v", rms_tok)
        tr = I("v", ve.tensor_scalar(rmsrow[0:1, :], pb[7][0:1, :], 1.0 / D, EPS,
                                     ALU.mult, ALU.add))
        pb_free[7] = tr
        W("s", tr)
        tsq = I("s", se.activation(rstd_b[0:1, :], rmsrow[0:1, :], AF.Sqrt))
        W("v", tsq)
        trc = I("v", ve.reciprocal(rmsrow[0:1, :], rstd_b[0:1, :]))
        W("pe", trc)
        kb = bank()
        tbc = I("pe", te.matmul(pb[kb][:], ones_r[:], rmsrow[0:1, :], start=True, stop=True))
        W("s", tbc)
        trb = I("s", se.copy(rstd_b[:], pb[kb][:]))
        pb_free[kb] = trb

        for tb in range(TB):
            txk = qac.dma(xtok[:, tb * D:(tb + 1) * D],
                          xD[r0 + tb * P:r0 + (tb + 1) * P, :],
                          pre=[xtok_war] if tb == 0 else None, group=(tb > 0))
        qac.close()
        xtok_tok = txk
        for e in range(DC):
            bi = e % 3
            q_ = qsp if e % 2 == 0 else qac
            wtg = q_.dma(wib[bi][:].rearrange("p (db q) -> p db q", q=P),
                         wgB.rearrange("eb (db pp) q -> eb pp db q", pp=P)[e],
                         pre=[wib_free[bi], wgB_tok[e]])
            W("pe", wtg)
            kg = bank()
            for d in range(DC):
                mg = te.matmul(pb[kg][:], wib[bi][:, d * P:(d + 1) * P],
                               xT[:, d * TT:(d + 1) * TT], start=(d == 0), stop=(d == DC - 1))
            tok_g = I("pe", mg)
            wib_free[bi] = tok_g
            W("s", tok_g)
            k1, sg1_t = fpool.get("s")
            ts1 = I("s", se.activation(sg1_t, pb[kg][:], AF.Sigmoid))
            W("v", ts1)
            W("v", ld["gnw"])
            k2, sg_t = fpool.get("v")
            tsg = I("v", ve.scalar_tensor_tensor(sg_t, pb[kg][:], gnw[:, e:e + 1], sg1_t,
                                                 ALU.mult, ALU.mult))
            pb_free[kg] = tsg
            fpool.put(k1, tsg)
            W("v", trb)
            W("v", state_toks[e])
            k3, o2_t = fpool.get("v")
            to2 = I("v", ve.tensor_mul(o2_t, hbuf[:, e * TT:(e + 1) * TT], rstd_b[:]))
            if e == 0:
                W("v", o4_war)
            W("v", to2)
            W("v", tsg)
            to4 = I("v", ve.tensor_mul(o4[:, e * TT:(e + 1) * TT], o2_t, sg_t))
            fpool.put(k2, to4)
            fpool.put(k3, to4)
        o4_tok = to4
        xT_free = pr.mark("pe")

        evac_last = None
        for sweep in range(2):
            bank_tok = [None] * 8
            for d in range(DC):
                bi = d % 3
                wto = qsp.dma(wob[bi][:],
                              woB[d * P:(d + 1) * P, sweep * 1024:(sweep + 1) * 1024],
                              pre=[wob_free[bi], woB_tok[d // 4]])
                W("pe", wto)
                W("pe", o4_tok)
                for tb in range(TB):
                    for eg in range(2):
                        k = tb * 2 + eg
                        if d == 0:
                            W("pe", pb_free[k])
                        mo = te.matmul(pb[k][:],
                                       o4[:, d * TT + tb * P: d * TT + (tb + 1) * P],
                                       wob[bi][:, eg * TT:(eg + 1) * TT],
                                       start=(d == 0), stop=(d == DC - 1))
                tok_d = I("pe", mo)
                if d == DC - 1:
                    for tb in range(TB):
                        for eg in range(2):
                            bank_tok[tb * 2 + eg] = tok_d
                wob_free[bi] = tok_d
            for tb in range(TB):
                for eg in range(2):
                    k = tb * 2 + eg
                    idx = tb * 4 + sweep * 2 + eg
                    col = tb * D + sweep * 1024 + eg * TT
                    W("v", bank_tok[k])
                    W("v", xtok_tok)
                    if sweep == 0 and tb == 0 and eg == 0:
                        W("v", hbuf_war)
                    tv = I("v", ve.scalar_tensor_tensor(r1[:, col:col + TT], pb[k][:], 1.0,
                                                        xtok[:, col:col + TT],
                                                        ALU.mult, ALU.add,
                                                        accum_out=accm[:, idx:idx + 1]))
                    pb_free[k] = tv
                    W("s", tv)
                    ksr, scr_t = fpool.get("s")
                    tss = I("s", se.activation(scr_t, r1[:, col:col + TT], AF.Square,
                                               accum_out=accs[:, idx:idx + 1]))
                    fpool.put(ksr, tss)
                    evac_last = (tv, tss)
        o4_free = pr.mark("pe")
        W("v", evac_last[0])
        W("v", evac_last[1])
        trs = ln_finalize(16, TB)
        y1_toks = []
        for tb in range(TB):
            W("v", trs)
            ty = I("v", ve.tensor_scalar(r1[:, tb * D:(tb + 1) * D], r1[:, tb * D:(tb + 1) * D],
                                         lnstat[:, 2 * TB + tb:2 * TB + tb + 1],
                                         lnstat[:, 7 * TB + tb:7 * TB + tb + 1],
                                         ALU.subtract, ALU.mult))
            y1_toks.append(ty)
        xtok_v_done = pr.mark("v")

        xrot_toks = []

        def emit_h128d(pz):
            d2, kx2, x1a, tok2 = pz
            W("pe", tok2)
            W("pe", ld["h128"])
            kz = bank()
            tmz = I("pe", te.matmul(pb[kz][:, 0:HALF], h128[:], x1a[:, 0:HALF],
                                    start=True, stop=True))
            bpool.put(kx2, tmz)
            W("s", tmz)
            if th == 0 and d2 == 0:
                W("s", o4_free)
                W("s", o4_war)
                W("s", fb_free)
            if th == 1 and d2 == 0:
                W("s", xrot_toks[0])
            tev2 = I("s", se.copy(fb[:, d2 * HALF:(d2 + 1) * HALF], pb[kz][:, 0:HALF]))
            pb_free[kz] = tev2
            return tev2

        for th in range(2):
            pendD = None
            for d in range(DC):
                W("pe", y1_toks[2 * th])
                W("pe", y1_toks[2 * th + 1])
                W("pe", ld["ident"])
                kz2 = bank()
                for tbl in range(2):
                    tb2 = 2 * th + tbl
                    ttr = I("pe", te.transpose(
                        pb[kz2][:, tbl * P:(tbl + 1) * P],
                        r1[:, tb2 * D + d * P: tb2 * D + (d + 1) * P], ident[:]))
                W("s", ttr)
                W("s", ld["suug"])
                kx, x1_t = bpool.get("s")
                tx1 = I("s", se.activation(x1_t[:, 0:HALF], pb[kz2][:, 0:HALF],
                                           AF.Copy, scale=suug[:, d:d + 1]))
                pb_free[kz2] = tx1
                if pendD is not None:
                    tev = emit_h128d(pendD)
                pendD = (d, kx, x1_t, tx1)
            tev = emit_h128d(pendD)
            pendD = None
            fb_tok = tev
            src_b, dst_b = fb, fb2
            tok = fb_tok
            for li, s_ in enumerate([1, 2, 4, 8]):
                blk = s_ * HALF
                vin = src_b[:].rearrange("p (a two bt) -> p a two bt", two=2, bt=blk)
                vout = dst_b[:].rearrange("p (a two bt) -> p a two bt", two=2, bt=blk)
                W("v", tok)
                tp = I("v", ve.tensor_add(vout[:, :, 0, :], vin[:, :, 0, :], vin[:, :, 1, :]))
                tok = I("v", ve.tensor_sub(vout[:, :, 1, :], vin[:, :, 0, :], vin[:, :, 1, :]))
                src_b, dst_b = dst_b, src_b
            W("v", tok)
            if th == 0:
                W("v", xT_free)
            vz_in = src_b[:].rearrange("p (c t) -> p c t", t=HALF)
            vz_out = xrot[:].rearrange("p (c tt) -> p c tt", tt=TT)[:, :, th * HALF:(th + 1) * HALF]
            txr = I("v", ve.tensor_copy(vz_out, vz_in))
            xrot_toks.append(txr)
        xrot_tok = xrot_toks[-1]
        fb_free = xrot_tok
        r1_pe_done = pr.mark("pe")

        if p == 1:
            emit_retile()
            upT_fin = [sem_maxima(upT_tok[ib]) for ib in range(4)]
            gtT_fin = [sem_maxima(gtT_tok[ib]) for ib in range(4)]
        pend = []
        zt_toks = [None, None]

        def emit_h128_evac(pz):
            ic2, k42, t4a, tok42 = pz
            W("pe", tok42)
            kz = bank()
            tmz = I("pe", te.matmul(pb[kz][:], h128[:], t4a, start=True, stop=True))
            bpool.put(k42, tmz)
            W("s", tmz)
            if ic2 == 0:
                W("s", o4_free)
                W("s", fb_free)
            te1 = I("s", se.copy(zt0[:, ic2 * HALF:(ic2 + 1) * HALF], pb[kz][:, 0:HALF]))
            if ic2 == 0:
                W("s", xtok_v_done)
            te2 = I("s", se.copy(zt1[:, ic2 * HALF:(ic2 + 1) * HALF], pb[kz][:, HALF:TT]))
            pb_free[kz] = te2
            zt_toks[0] = te1
            zt_toks[1] = te2

        W("s", wsg_tok)
        q_gt = qsp if p == 1 else qac
        for igg in range(32):
            gi = igg % 2
            pre_u = [wub_free[gi]] + upT_fin[igg // 8]
            if igg == 0:
                pre_u = pre_u + wib_free + wfb_free + wob_free
            up_tok = qsp.dma(wub[gi][:], wupT[igg], pre=pre_u)
            gt_tok = q_gt.dma(wgb[gi][:], wgtT[igg],
                              pre=[wgb_free[gi]] + gtT_fin[igg // 8])
            for icl in range(2):
                ic = igg * 2 + icl
                W("pe", up_tok)
                W("pe", xrot_tok)
                ku = bank()
                for d in range(DC):
                    mu = te.matmul(pb[ku][:],
                                   wub[gi][:, d * HALF + icl * P: d * HALF + (icl + 1) * P],
                                   xrot[:, d * TT:(d + 1) * TT], start=(d == 0), stop=(d == DC - 1))
                tok_u = I("pe", mu)
                W("pe", gt_tok)
                kg2 = bank()
                for d in range(DC):
                    mgt = te.matmul(pb[kg2][:],
                                    wgb[gi][:, d * HALF + icl * P: d * HALF + (icl + 1) * P],
                                    xrot[:, d * TT:(d + 1) * TT], start=(d == 0), stop=(d == DC - 1))
                tok_g2 = I("pe", mgt)
                if icl == 1:
                    wub_free[gi] = tok_u
                    wgb_free[gi] = tok_g2
                if len(pend) >= 2:
                    emit_h128_evac(pend.pop(0))
                W("s", tok_g2)
                k1, t1_t = fpool.get("s")
                tt1 = I("s", se.activation(t1_t, pb[kg2][:], AF.Sigmoid,
                                           scale=wsg[:, ic:ic + 1]))
                W("v", tt1)
                k2, t2_t = fpool.get("v")
                tt2 = I("v", ve.scalar_tensor_tensor(t2_t, pb[kg2][:], wsg[:, ic:ic + 1],
                                                     t1_t, ALU.mult, ALU.mult))
                pb_free[kg2] = tt2
                fpool.put(k1, tt2)
                W("v", tok_u)
                W("v", tt2)
                k3, t3_t = fpool.get("v")
                tt3 = I("v", ve.scalar_tensor_tensor(t3_t, pb[ku][:], wsu[:, ic:ic + 1],
                                                     t2_t, ALU.mult, ALU.mult))
                pb_free[ku] = tt3
                fpool.put(k2, tt3)
                W("v", sdp_tok)
                W("v", tt3)
                k4, t4_t = bpool.get("v")
                tt4 = I("v", ve.tensor_scalar(t4_t, t3_t, sdpc[:, ic:ic + 1], None, ALU.mult))
                fpool.put(k3, tt4)
                pend.append((ic, k4, t4_t, tt4))
        while pend:
            emit_h128_evac(pend.pop(0))
        xrot_free = pr.mark("pe")

        preA = {}
        if p + 1 < n_pass:
            for e2 in range(3):
                preA[("i", e2)] = qsp.dma(
                    wib[e2 % 3][:].rearrange("p (db q) -> p db q", q=P),
                    wiB.rearrange("eb (db pp) q -> eb pp db q", pp=P)[e2],
                    pre=[xrot_free, wib_free[e2 % 3], wiB_tok[e2]])
                preA[("f", e2)] = qac.dma(
                    wfb[e2 % 3][:].rearrange("p (db q) -> p db q", q=P),
                    wfB.rearrange("eb (db pp) q -> eb pp db q", pp=P)[e2],
                    pre=[wfb_free[e2 % 3], wfB_tok[e2]])
        if p + 1 < n_pass:
            xT_pre[p + 1] = load_xT(p + 1, extra_pre=[xrot_free])

        if p == 1:
            ring3 = [foldf[2], foldf2[2], foldf2[1]]
            free3 = [None, None, None]
            prepb_free = [None, None]
            jj = 0
            for piece in range(8):
                for blk in range(DC):
                    s3 = jj % 3
                    p2 = jj % 2
                    pre_l = [free3[s3]] + (fold_all if jj == 0 else []) + fold2_tok[blk]
                    lt = qac.dma(ring3[s3][:],
                                 wdnF[blk * P:(blk + 1) * P, piece * 1024:(piece + 1) * 1024],
                                 pre=pre_l)
                    W("v", lt)
                    W("v", wsd_tok)
                    W("v", prepb_free[p2])
                    vt = I("v", ve.tensor_scalar(prepb[p2][:], ring3[s3][:],
                                                 wsd[:, blk:blk + 1], None, ALU.mult))
                    free3[s3] = vt
                    st = qsp.dma(wdnB[blk * P:(blk + 1) * P, piece * 1024:(piece + 1) * 1024],
                                 prepb[p2][:], pre=[vt])
                    prepb_free[p2] = st
                    wdnp_tok[piece].append(st)
                    jj += 1
            for piece in range(8):
                wdnp_tok[piece] = sem_maxima(wdnp_tok[piece])

        ev_last = None
        for sweep in range(2):
            bank_tok = [None] * 8
            for c in range(IC):
                if p == 1 and sweep == 0:
                    pi2 = c % 2
                    ttw = qac.dma(wdb[pi2][:], wdnB[0:D, c * P:(c + 1) * P],
                                  pre=[wdb_tok[pi2]] + wdnp_tok[c // 8],
                                  transpose=True)
                    st2 = qac.dma(wdnT[c], wdb[pi2][:], pre=[ttw])
                    wdb_tok[pi2] = st2
                    dnT_tok.append(st2)
                gslot = grr[0]
                grr[0] = (grr[0] + 1) % len(wdbg)
                pre_w = [wdbg_free[gslot]]
                if p == 1:
                    pre_w.append(dnT_tok[c])
                    if sweep == 0 and c == 0:
                        pre_w += fold_all
                elif p == 2 and sweep == 0 and c < len(wdbg):
                    pre_w += dnT_last
                wtd = qsp.dma(wdbg[gslot][:], wdnT[c][:, sweep * 1024:(sweep + 1) * 1024],
                              pre=pre_w)
                W("pe", wtd)
                W("pe", zt_toks[0])
                W("pe", zt_toks[1])
                for tbl in range(4):
                    ztX = zt0 if tbl < 2 else zt1
                    tl = tbl % 2
                    for jg in range(2):
                        k = tbl * 2 + jg
                        if c == 0:
                            W("pe", pb_free[k])
                        md = te.matmul(pb[k][:],
                                       ztX[:, c * HALF + tl * P: c * HALF + (tl + 1) * P],
                                       wdbg[gslot][:, jg * TT:(jg + 1) * TT],
                                       start=(c == 0), stop=(c == IC - 1))
                tok_c = I("pe", md)
                wdbg_free[gslot] = tok_c
                if c == IC - 1:
                    for k in range(8):
                        bank_tok[k] = tok_c
            for tbl in range(4):
                for jg in range(2):
                    k = tbl * 2 + jg
                    idx = tbl * 4 + sweep * 2 + jg
                    col = tbl * D + sweep * 1024 + jg * TT
                    W("v", bank_tok[k])
                    tv = I("v", ve.scalar_tensor_tensor(r1[:, col:col + TT], pb[k][:], 1.0,
                                                        r1[:, col:col + TT],
                                                        ALU.mult, ALU.add,
                                                        accum_out=accm[:, idx:idx + 1]))
                    pb_free[k] = tv
                    W("s", tv)
                    ksr, scr_t = fpool.get("s")
                    tss = I("s", se.activation(scr_t, r1[:, col:col + TT], AF.Square,
                                               accum_out=accs[:, idx:idx + 1]))
                    fpool.put(ksr, tss)
                    ev_last = (tv, tss)
        if p == 1:
            dnT_last = sem_maxima(dnT_tok)
        W("v", ev_last[0])
        W("v", ev_last[1])
        trs2 = ln_finalize(16, TB)
        out_toks = []
        for tb in range(TB):
            W("v", trs2)
            ty2 = I("v", ve.tensor_scalar(r1[:, tb * D:(tb + 1) * D],
                                          r1[:, tb * D:(tb + 1) * D],
                                          lnstat[:, 2 * TB + tb:2 * TB + tb + 1],
                                          lnstat[:, 7 * TB + tb:7 * TB + tb + 1],
                                          ALU.subtract, ALU.mult))
            tdo = qdo.dma(outD[(p - 1) * TT + tb * P: (p - 1) * TT + (tb + 1) * P, :],
                          r1[:, tb * D:(tb + 1) * D], pre=[ty2])
            out_toks.append(tdo)
        g_pe_done = pr.mark("pe")
        xtok_war = g_pe_done
        o4_war = g_pe_done
        hbuf_war = list(out_toks)

    for k, v in qdo.all_last():
        sy.wait_ge(pr.sem[k], v)
        ge.wait_ge(pr.sem[k], v)
    es.close()
    assert nc.sbuf_base <= 17408, f"auto allocs overflow reserve: {nc.sbuf_base}"
    return nc


_NC_CACHE = {}


def _get_nc(n_full=8):
    if n_full not in _NC_CACHE:
        _NC_CACHE[n_full] = build_nc(n_full)
    return _NC_CACHE[n_full]


def make_aux(full):
    return {
        "wi": np.ascontiguousarray(full["L0_attn_Wi"], dtype=np.float32),
        "wf": np.ascontiguousarray(full["L0_attn_Wf"], dtype=np.float32),
        "wg": np.ascontiguousarray(full["L0_attn_Wg"], dtype=np.float32),
        "wo": np.ascontiguousarray(full["L0_attn_Wo"], dtype=np.float32),
        "wup": np.ascontiguousarray(full["L0_ffn_W_up"], dtype=np.float32),
        "wgt": np.ascontiguousarray(full["L0_ffn_W_gate"], dtype=np.float32),
        "wdn": np.ascontiguousarray(full["L0_ffn_W_down"], dtype=np.float32),
        "gnw": np.ascontiguousarray(np.asarray(full["L0_attn_g_norm_weight"], dtype=np.float32).reshape(DC, P).T),
        "suug": np.ascontiguousarray(np.asarray(full["L0_ffn_SU_ug"], dtype=np.float32).reshape(DC, P).T),
        "sudp": np.ascontiguousarray(np.asarray(full["L0_ffn_SU_dp"], dtype=np.float32).reshape(IC, P).T),
        "wsu": np.ascontiguousarray(np.asarray(full["L0_ffn_Wscale_ug"], dtype=np.float32)[:INTER].reshape(IC, P).T),
        "wsg": np.ascontiguousarray(np.asarray(full["L0_ffn_Wscale_ug"], dtype=np.float32)[INTER:].reshape(IC, P).T),
        "wsd": np.ascontiguousarray(np.asarray(full["L0_ffn_Wscale_dp"], dtype=np.float32).reshape(DC, P).T),
        "h128": _hadamard128(),
        "ident": np.eye(P, dtype=np.float32),
    }


def kernel(**inputs):
    n_full = 8
    ct = n_full * TT
    rows = (n_full + 1) * TT
    nc = _get_nc(n_full)
    x = np.asarray(inputs["L0_attn_hidden_states"])
    aux = make_aux(inputs)
    in_maps = []
    for c in range(8):
        b, q = c // 4, c % 4
        xs = np.zeros((rows, D), np.float32)
        t0 = q * ct
        w0 = max(0, t0 - NWARM)
        nw = t0 - w0
        if nw > 0:
            xs[NWARM - nw:NWARM] = x[b, w0:t0]
        xs[NWARM:] = x[b, t0:t0 + ct]
        m = dict(aux)
        m["x"] = xs
        in_maps.append(m)
    res = run_bass_kernel_spmd(nc, in_maps, core_ids=list(range(8)))
    out = np.empty((2, 16384, D), np.float32)
    for c in range(8):
        b, q = c // 4, c % 4
        out[b, q * ct:(q + 1) * ct] = res.results[c]["out"]
    return out


# version 13 (final)
# speedup vs baseline: 1.4939x; 1.1358x over previous
import numpy as np
import ml_dtypes

import concourse.bass as bass
import concourse.mybir as mybir
from concourse.bass_utils import run_bass_kernel_spmd

F32 = mybir.dt.float32
BF16 = mybir.dt.bfloat16
AF = mybir.ActivationFunctionType
ALU = mybir.AluOpType
BF = ml_dtypes.bfloat16

P = 128
D = 2048
DC = 16
INTER = 8192
IC = 64
TT = 512
TB = 4
HALF = 256
NWARM = 512
S0 = 1.0 / (64.0 * np.sqrt(2048.0))
EPS = 1e-5
ENG = ("pe", "v", "s", "g", "sp")


def _hadamard128():
    i = np.arange(128)
    pc = np.zeros((128, 128), np.int64)
    for b in range(7):
        pc += ((i[:, None] >> b) & 1) * ((i[None, :] >> b) & 1)
    return np.where(pc % 2 == 0, 1.0, -1.0).astype(BF)


class Prog:
    def __init__(self, nc, es):
        self.nc = nc
        self.sem = {}
        self.cnt = {}
        for name in ("pe", "v", "s", "g"):
            self.sem[name] = es.enter_context(nc.semaphore(name))
            self.cnt[name] = 0
        self.wm = {e: {} for e in ENG}
        self.eng = {"pe": nc.tensor, "v": nc.vector, "s": nc.scalar,
                    "g": nc.gpsimd, "sp": nc.sync}

    def wait(self, eng, tok):
        if tok is None:
            return
        if isinstance(tok, list):
            for t in tok:
                self.wait(eng, t)
            return
        sname, val = tok
        if self.wm[eng].get(sname, 0) >= val:
            return
        self.eng[eng].wait_ge(self.sem[sname], val)
        self.wm[eng][sname] = val

    def inc(self, eng, inst):
        self.cnt[eng] += 1
        inst.then_inc(self.sem[eng], 1)
        return (eng, self.cnt[eng])

    def mark(self, eng):
        return (eng, self.cnt[eng])


class DQ:

    def __init__(self, pr, eng_name, basename, es, nsem):
        self.pr = pr
        self.eng_name = eng_name
        self.eng = pr.eng[eng_name]
        self.names = []
        for i in range(nsem):
            name = f"{basename}{i}"
            pr.sem[name] = es.enter_context(pr.nc.semaphore(name))
            pr.cnt[name] = 0
            self.names.append(name)
        self.i = 0
        self.cur = None

    def dma(self, out, in_, pre=None, group=False, **kw):
        pr = self.pr
        if group and self.cur is not None:
            k = self.cur
        else:
            k = self.names[self.i]
            self.i = (self.i + 1) % len(self.names)
            if pr.cnt[k] > 0:
                pr.wait(self.eng_name, (k, pr.cnt[k]))
            self.cur = k
        pr.wait(self.eng_name, pre)
        inst = self.eng.dma_start(out=out, in_=in_, **kw)
        pr.cnt[k] += 16
        inst.then_inc(pr.sem[k], 16)
        return (k, pr.cnt[k])

    def close(self):
        self.cur = None

    def all_last(self):
        return [(k, self.pr.cnt[k]) for k in self.names if self.pr.cnt[k] > 0]


class Pool:

    def __init__(self, pr, tiles):
        self.pr = pr
        self.tiles = tiles
        self.free = [None] * len(tiles)
        self.i = 0

    def get(self, eng):
        k = self.i
        self.i = (self.i + 1) % len(self.tiles)
        self.pr.wait(eng, self.free[k])
        return k, self.tiles[k]

    def put(self, k, tok):
        self.free[k] = tok


def sem_maxima(toks):
    seen = {}
    for t_ in toks:
        if t_ is None:
            continue
        seen[t_[0]] = max(seen.get(t_[0], 0), t_[1])
    return [(k, v) for k, v in seen.items()]


def build_nc(n_full=8):
    nc = bass.Bass("TRN2", target_bir_lowering=False)
    n_pass = n_full + 1
    rows = n_pass * TT

    xD = nc.dram_tensor("x", [rows, D], F32, kind="ExternalInput").ap()
    wiD = nc.dram_tensor("wi", [D, D], F32, kind="ExternalInput").ap()
    wfD = nc.dram_tensor("wf", [D, D], F32, kind="ExternalInput").ap()
    wgD = nc.dram_tensor("wg", [D, D], F32, kind="ExternalInput").ap()
    woD = nc.dram_tensor("wo", [D, D], F32, kind="ExternalInput").ap()
    wupD = nc.dram_tensor("wup", [INTER, D], F32, kind="ExternalInput").ap()
    wgtD = nc.dram_tensor("wgt", [INTER, D], F32, kind="ExternalInput").ap()
    wdnD = nc.dram_tensor("wdn", [D, INTER], F32, kind="ExternalInput").ap()
    gnwD = nc.dram_tensor("gnw", [P, DC], F32, kind="ExternalInput").ap()
    suugD = nc.dram_tensor("suug", [P, DC], F32, kind="ExternalInput").ap()
    sudpD = nc.dram_tensor("sudp", [P, IC], F32, kind="ExternalInput").ap()
    wsuD = nc.dram_tensor("wsu", [P, IC], F32, kind="ExternalInput").ap()
    wsgD = nc.dram_tensor("wsg", [P, IC], F32, kind="ExternalInput").ap()
    wsdD = nc.dram_tensor("wsd", [P, DC], F32, kind="ExternalInput").ap()
    h128D = nc.dram_tensor("h128", [P, P], BF16, kind="ExternalInput").ap()
    identD = nc.dram_tensor("ident", [P, P], F32, kind="ExternalInput").ap()
    outD = nc.dram_tensor("out", [n_full * TT, D], F32, kind="ExternalOutput").ap()

    xbD = nc.dram_tensor("xb", [rows, D], BF16, kind="Internal").ap()
    wiB = nc.dram_tensor("wiB", [DC, D, P], BF16, kind="Internal").ap()
    wfB = nc.dram_tensor("wfB", [DC, D, P], BF16, kind="Internal").ap()
    wgB = nc.dram_tensor("wgB", [DC, D, P], BF16, kind="Internal").ap()
    woB = nc.dram_tensor("woB", [D, D], BF16, kind="Internal").ap()
    wupB = nc.dram_tensor("wupB", [INTER, D], BF16, kind="Internal").ap()
    wgtB = nc.dram_tensor("wgtB", [INTER, D], BF16, kind="Internal").ap()
    wdnB = nc.dram_tensor("wdnB", [D, INTER], BF16, kind="Internal").ap()
    wdnF = nc.dram_tensor("wdnF", [D, INTER], F32, kind="Internal").ap()
    wupT = nc.dram_tensor("wupT", [32, P, DC * HALF], BF16, kind="Internal").ap()
    wgtT = nc.dram_tensor("wgtT", [32, P, DC * HALF], BF16, kind="Internal").ap()
    wdnT = nc.dram_tensor("wdnT", [IC, P, 2048], BF16, kind="Internal").ap()

    from contextlib import ExitStack
    es = ExitStack()
    pr = Prog(nc, es)
    W, I = pr.wait, pr.inc
    te, ve, se, ge, sy = nc.tensor, nc.vector, nc.scalar, nc.gpsimd, nc.sync
    qsp = DQ(pr, "sp", "qw", es, 8)
    qac = DQ(pr, "s", "qa", es, 8)
    qgp = DQ(pr, "g", "qg", es, 6)
    qdo = DQ(pr, "sp", "qo", es, 2)

    off = [17408]

    def at(name, shape, dtype, o=None):
        nonlocal off
        nbytes = int(np.prod(shape[1:])) * mybir.dt.size(dtype)
        if o is None:
            o = off[0]
            off[0] += (nbytes + 31) // 32 * 32
        return nc.alloc_sbuf_tensor_at(name, shape, dtype, offset=o)

    h128 = at("h128", [P, P], BF16)
    ident = at("ident", [P, P], F32)
    gnw = at("gnw", [P, DC], F32)
    suug = at("suug", [P, DC], F32)
    sdpc = at("sdpc", [P, IC], F32)
    wsu = at("wsu", [P, IC], F32)
    wsg = at("wsg", [P, IC], F32)
    wsd = at("wsd", [P, DC], F32)
    ones_c = at("ones_c", [P, 1], BF16)
    ones_r = at("ones_r", [1, P], F32)
    state = at("state", [P, DC], F32)
    rmsrow = at("rmsrow", [1, TT], F32)
    rstd_b = at("rstd_b", [P, TT], F32)
    accm = at("accm", [P, 16], F32)
    accs = at("accs", [P, 16], F32)
    lnstat = at("lnstat", [P, 10 * TB], F32)
    wdb = [at(f"wdb{i}", [P, 2048], BF16) for i in range(2)]
    fpool_t = [at(f"fp{i}", [P, TT], F32) for i in range(7)]
    bpool_t = [at(f"bp{i}", [P, TT], BF16) for i in range(4)]
    prepb = [at(f"prepb{i}", [P, 1024], BF16) for i in range(2)]
    base_small = off[0]
    assert base_small <= 65504, base_small

    OFF_W = 65504
    OFF_X = 98272
    OFF_H = 131040
    OFF_Z = 163808
    OFF_WS = 196576
    OFF_XT = 212960

    wib = [at(f"wib{i}", [P, D], BF16, OFF_W + i * 4096) for i in range(3)]
    wfb = [at(f"wfb{i}", [P, D], BF16, OFF_W + 12288 + i * 4096) for i in range(3)]
    wob = [at(f"wob{i}", [P, 1024], BF16, OFF_W + 24576 + i * 2048) for i in range(3)]
    wub = [at(f"wub{i}", [P, DC * HALF], BF16, OFF_W + i * 8192) for i in range(2)]
    wgb = [at(f"wgb{i}", [P, DC * HALF], BF16, OFF_W + 16384 + i * 8192) for i in range(2)]

    xtok = at("xtok", [P, TB * D], F32, OFF_X)
    zt1 = at("zt1", [P, IC * HALF], BF16, OFF_X)

    hbuf = at("hbuf", [P, DC * TT], F32, OFF_H)
    r1 = at("r1", [P, TB * D], F32, OFF_H)

    o4 = at("o4", [P, DC * TT], BF16, OFF_Z)
    fb = at("fb", [P, DC * HALF], F32, OFF_Z)
    fb2 = at("fb2", [P, DC * HALF], F32, OFF_Z + 16 * 1024)
    zt0 = at("zt0", [P, IC * HALF], BF16, OFF_Z)

    fold3a = at("fold3a", [P, 1024], F32)
    fold3b = at("fold3b", [P, 1024], F32)
    assert off[0] <= 65504, off[0]
    foldf = [at(f"foldf{i}", [P, 1024], F32, OFF_WS + i * 4096) for i in range(2)] + [fold3a]
    foldf2 = [at(f"foldf2{i}", [P, 1024], F32, OFF_WS + 8192 + i * 4096) for i in range(2)] + [fold3b]
    wdbg = [at(f"wdbg{i}", [P, 1024], BF16, OFF_WS + i * 2048) for i in range(6)]

    xT = at("xT", [P, DC * TT], BF16, OFF_XT)
    xrot = at("xrot", [P, DC * TT], BF16, OFF_XT)

    pb = [nc.alloc_psum_tensor(f"pb{i}", [P, TT], F32) for i in range(8)]
    pb_free = [None] * 8
    rr = [0]

    def bank(eng="pe", k=None):
        if k is None:
            k = rr[0]
            rr[0] = (rr[0] + 1) % 6
        W(eng, pb_free[k])
        return k

    fpool = Pool(pr, [t[:] for t in fpool_t])
    bpool = Pool(pr, [t[:] for t in bpool_t])

    ld = {}
    ldt = None
    for name, dst, srcT in [("gnw", gnw, gnwD), ("suug", suug, suugD), ("sdp", sdpc, sudpD),
                            ("wsu", wsu, wsuD), ("wsg", wsg, wsgD), ("wsd", wsd, wsdD),
                            ("h128", h128, h128D), ("ident", ident, identD)]:
        ldt = qsp.dma(dst[:], srcT, group=(ldt is not None))
    for name in ("gnw", "suug", "sdp", "wsu", "wsg", "wsd", "h128", "ident"):
        ld[name] = ldt
    qsp.close()

    ve.memset(ones_c[:], 1.0)
    ve.memset(ones_r[:], 1.0)
    tk0 = I("v", ve.memset(state[:], 0.0))
    W("v", ld["sdp"])
    sdp_tok = I("v", ve.tensor_scalar(sdpc[:], sdpc[:], S0 / 2.0, None, ALU.mult))
    W("v", ld["wsu"])
    wsu_tok = I("v", ve.tensor_scalar(wsu[:], wsu[:], 64.0 * S0, None, ALU.mult))
    W("v", ld["wsg"])
    wsg_tok = I("v", ve.tensor_scalar(wsg[:], wsg[:], 64.0 * S0, None, ALU.mult))
    W("v", ld["wsd"])
    wsd_tok = I("v", ve.tensor_scalar(wsd[:], wsd[:], 64.0, None, ALU.mult))

    xb_tok = [None] * n_pass
    for p in range(3):
        xb_tok[p] = qgp.dma(xbD[p * TT:(p + 1) * TT, :], xD[p * TT:(p + 1) * TT, :])
    wiB_tok, wfB_tok, wgB_tok = [], [], []
    for eb in range(DC):
        s_ = wiD.rearrange("d (eb q) -> eb d q", q=P)[eb]
        wiB_tok.append(qgp.dma(wiB[eb], s_))
        s_ = wfD.rearrange("d (eb q) -> eb d q", q=P)[eb]
        wfB_tok.append(qgp.dma(wfB[eb], s_))
    for eb in range(DC):
        s_ = wgD.rearrange("d (eb q) -> eb d q", q=P)[eb]
        wgB_tok.append(qgp.dma(wgB[eb], s_))
    woB_tok = []
    for b in range(4):
        woB_tok.append(qgp.dma(woB[b * TT:(b + 1) * TT, :], woD[b * TT:(b + 1) * TT, :]))

    cast_jobs = []
    upB_tok = [[] for _ in range(4)]
    gtB_tok = [[] for _ in range(4)]
    for ib in range(4):
        for j in range(4):
            r0 = ib * 2048 + j * 512
            cast_jobs.append(("up", ib, r0))
        for j in range(4):
            r0 = ib * 2048 + j * 512
            cast_jobs.append(("gt", ib, r0))
    for p in range(3, n_pass):
        cast_jobs.append(("x", p, p * TT))
    cj_i = [0]

    def pump_cast(n):
        while n > 0 and cj_i[0] < len(cast_jobs):
            kind, a, r0 = cast_jobs[cj_i[0]]
            cj_i[0] += 1
            n -= 1
            if kind == "up":
                upB_tok[a].append(qgp.dma(wupB[r0:r0 + 512, :], wupD[r0:r0 + 512, :]))
            elif kind == "gt":
                gtB_tok[a].append(qgp.dma(wgtB[r0:r0 + 512, :], wgtD[r0:r0 + 512, :]))
            else:
                xb_tok[a] = qgp.dma(xbD[r0:r0 + TT, :], xD[r0:r0 + TT, :])

    fold1_tok = [[] for _ in range(DC)]
    fold2_tok = [[] for _ in range(DC)]
    fold_all_toks = []
    fold_blocks = []
    for sweep in range(2):
        for jb in range(DC):
            for grp in range(8):
                if sweep == 0:
                    src_ap = wdnD[jb * P:(jb + 1) * P, grp * 1024:(grp + 1) * 1024]
                    dst_ap = wdnF[jb * P:(jb + 1) * P, grp * 1024:(grp + 1) * 1024]
                    dep_jb = None
                else:
                    gath = wdnF[jb * P:(jb + 1) * P, :].rearrange(
                        "p (g sub q) -> p sub g q", g=8, q=P)[:, grp, :, :]
                    src_ap = gath
                    dst_ap = gath
                    dep_jb = jb if grp == 0 else None
                fold_blocks.append((src_ap, dst_ap, sweep, dep_jb, jb))
    NF = len(fold_blocks)
    fold_free = [None, None, None]
    load_tok = [None] * NF

    def issue_fold_load(j):
        src_ap, dst_ap, sweep, dep_jb, jb = fold_blocks[j]
        pre = [fold_free[j % 3]]
        if dep_jb is not None:
            pre = pre + fold1_tok[dep_jb]
        load_tok[j] = qgp.dma(foldf[j % 3][:], src_ap, pre=pre)

    issue_fold_load(0)
    issue_fold_load(1)
    for j in range(NF):
        src_ap, dst_ap, sweep, dep_jb, jb = fold_blocks[j]
        pi3 = j % 3
        fa, fz = foldf[pi3], foldf2[pi3]
        tokf = load_tok[j]
        for s_ in (1, 2, 4):
            blklen = s_ * P
            vin = fa[:].rearrange("p (a two bt) -> p a two bt", two=2, bt=blklen)
            vout = fz[:].rearrange("p (a two bt) -> p a two bt", two=2, bt=blklen)
            W("g", tokf)
            I("g", ge.tensor_add(vout[:, :, 0, :], vin[:, :, 0, :], vin[:, :, 1, :]))
            tokf = I("g", ge.tensor_sub(vout[:, :, 1, :], vin[:, :, 0, :], vin[:, :, 1, :]))
            fa, fz = fz, fa
        st3 = qgp.dma(dst_ap, fa[:], pre=[tokf])
        fold_free[pi3] = st3
        fold_all_toks.append(st3)
        (fold1_tok if sweep == 0 else fold2_tok)[jb].append(st3)
        if j + 2 < NF:
            issue_fold_load(j + 2)
        if j >= 8 and j % 2 == 0:
            pump_cast(1)
    pump_cast(len(cast_jobs))
    fold_all = sem_maxima(fold_all_toks)

    wdb_tok = [None, None]
    upT_tok = [[] for _ in range(4)]
    gtT_tok = [[] for _ in range(4)]

    def emit_retile():
        n = 0
        for ib in range(4):
            for mat in ("up", "gt"):
                for dch in range(DC):
                    n += 1
                    pi2 = n % 2
                    if mat == "up":
                        srcB, dstT, ctoks, toks_out = wupB, wupT, upB_tok[ib], upT_tok
                    else:
                        srcB, dstT, ctoks, toks_out = wgtB, wgtT, gtB_tok[ib], gtT_tok
                    tt = qsp.dma(wdb[pi2][:],
                                 srcB[ib * 2048:(ib + 1) * 2048, dch * P:(dch + 1) * P],
                                 pre=[wdb_tok[pi2]] + ctoks, transpose=True)
                    st2 = qsp.dma(dstT[ib * 8:(ib + 1) * 8, :, dch * HALF:(dch + 1) * HALF]
                                  .rearrange("g p t -> p g t"),
                                  wdb[pi2][:].rearrange("p (g t) -> p g t", t=HALF),
                                  pre=[tt])
                    wdb_tok[pi2] = st2
                    toks_out[ib].append(st2)

    xT_free = None
    xtok_war = None
    hbuf_war = None
    o4_war = None
    o4_free = None
    fb_free = None
    xrot_free = None
    state_toks = [tk0] * DC
    wib_free = [None, None, None]
    wfb_free = [None, None, None]
    wob_free = [None, None, None]
    wub_free = [None, None]
    wgb_free = [None, None]
    wdbg_free = [None] * 6
    grr = [0]
    dnT_tok = []
    dnT_last = []
    wdnp_tok = [[] for _ in range(8)]
    upT_fin = None
    gtT_fin = None

    def ln_finalize(n_idx, ntb):
        tm = I("v", ve.tensor_reduce(lnstat[:, 0:ntb],
                                     accm[:, 0:n_idx].rearrange("p (tb s) -> p tb s", s=4),
                                     mybir.AxisListType.X, ALU.add))
        tq = I("v", ve.tensor_reduce(lnstat[:, TB:TB + ntb],
                                     accs[:, 0:n_idx].rearrange("p (tb s) -> p tb s", s=4),
                                     mybir.AxisListType.X, ALU.add))
        W("v", tm)
        tmn = I("v", ve.tensor_scalar(lnstat[:, 2 * TB:2 * TB + ntb], lnstat[:, 0:ntb],
                                      1.0 / D, None, ALU.mult))
        W("v", tmn)
        tms = I("v", ve.tensor_mul(lnstat[:, 3 * TB:3 * TB + ntb],
                                   lnstat[:, 2 * TB:2 * TB + ntb],
                                   lnstat[:, 2 * TB:2 * TB + ntb]))
        W("v", tq)
        tv1 = I("v", ve.tensor_scalar(lnstat[:, 4 * TB:4 * TB + ntb], lnstat[:, TB:TB + ntb],
                                      1.0 / D, EPS, ALU.mult, ALU.add))
        W("v", tms)
        W("v", tv1)
        tvar = I("v", ve.tensor_sub(lnstat[:, 5 * TB:5 * TB + ntb],
                                    lnstat[:, 4 * TB:4 * TB + ntb],
                                    lnstat[:, 3 * TB:3 * TB + ntb]))
        W("s", tvar)
        tsd = I("s", se.activation(lnstat[:, 6 * TB:6 * TB + ntb],
                                   lnstat[:, 5 * TB:5 * TB + ntb], AF.Sqrt))
        W("v", tsd)
        trs = I("v", ve.reciprocal(lnstat[:, 7 * TB:7 * TB + ntb],
                                   lnstat[:, 6 * TB:6 * TB + ntb]))
        return trs

    def load_xT(p2, extra_pre=None):
        r2_ = p2 * TT
        for d in range(DC):
            pre0 = None
            if d == 0:
                pre0 = [xT_free, xb_tok[p2]] + (extra_pre or [])
            txT = qac.dma(xT[:, d * TT:(d + 1) * TT],
                          xbD[r2_:r2_ + TT, d * P:(d + 1) * P],
                          pre=pre0, group=(d > 0), transpose=True)
        qac.close()
        return txT

    xT_pre = [None] * n_pass
    preA = {}

    for p in range(n_pass):
        warm = (p == 0)
        r0 = p * TT

        if xT_pre[p] is None:
            xT_pre[p] = load_xT(p)
        xT_tok = xT_pre[p]

        pend_rms = None
        preA_cur = preA if not warm else {}

        def emit_rms(pz):
            e2, kq2, hqa, tok2 = pz
            W("pe", tok2)
            mq = te.matmul(pb[7][0:1, :], ones_c[:], hqa,
                           start=(e2 == 0), stop=(e2 == DC - 1))
            tq2 = I("pe", mq)
            bpool.put(kq2, tq2)
            return tq2

        if not warm:
            W("pe", pb_free[7])
        for e in range(DC):
            bi = e % 3
            if ("i", e) in preA_cur:
                wti = preA_cur[("i", e)]
                wtf = preA_cur[("f", e)]
            else:
                wti = qsp.dma(wib[bi][:].rearrange("p (db q) -> p db q", q=P),
                              wiB.rearrange("eb (db pp) q -> eb pp db q", pp=P)[e],
                              pre=[wib_free[bi], wiB_tok[e]])
                wtf = qac.dma(wfb[bi][:].rearrange("p (db q) -> p db q", q=P),
                              wfB.rearrange("eb (db pp) q -> eb pp db q", pp=P)[e],
                              pre=[wfb_free[bi], wfB_tok[e]])
            W("pe", wti)
            W("pe", xT_tok)
            if pend_rms is not None:
                rms_tok = emit_rms(pend_rms)
                pend_rms = None
            ki = bank()
            for d in range(DC):
                mi = te.matmul(pb[ki][:], wib[bi][:, d * P:(d + 1) * P],
                               xT[:, d * TT:(d + 1) * TT], start=(d == 0), stop=(d == DC - 1))
            tok_i = I("pe", mi)
            wib_free[bi] = tok_i
            W("pe", wtf)
            kf = bank()
            for d in range(DC):
                mf = te.matmul(pb[kf][:], wfb[bi][:, d * P:(d + 1) * P],
                               xT[:, d * TT:(d + 1) * TT], start=(d == 0), stop=(d == DC - 1))
            tok_f = I("pe", mf)
            wfb_free[bi] = tok_f
            W("s", tok_f)
            ka, a_t = fpool.get("s")
            ta = I("s", se.activation(a_t, pb[kf][:], AF.Sigmoid))
            kn, sn_t = fpool.get("s")
            tsn = I("s", se.activation(sn_t, pb[kf][:], AF.Sigmoid, scale=-1.0))
            pb_free[kf] = tsn
            W("s", tok_i)
            ks, si_t = fpool.get("s")
            tsi = I("s", se.activation(si_t, pb[ki][:], AF.Sigmoid))
            W("v", tsi)
            k1, i1_t = fpool.get("v")
            t1 = I("v", ve.tensor_mul(i1_t, pb[ki][:], si_t))
            pb_free[ki] = t1
            fpool.put(ks, t1)
            W("v", tsn)
            W("v", t1)
            k2, i2_t = fpool.get("v")
            t2 = I("v", ve.tensor_mul(i2_t, i1_t, sn_t))
            fpool.put(k1, t2)
            fpool.put(kn, t2)
            W("v", ta)
            W("v", t2)
            W("v", state_toks[e])
            if warm:
                kh, h_dst = fpool.get("v")
                hview = h_dst
            else:
                if e == 0:
                    W("v", hbuf_war)
                hview = hbuf[:, e * TT:(e + 1) * TT]
            tscan = I("v", ve.tensor_tensor_scan(hview, a_t, i2_t, state[:, e:e + 1],
                                                 ALU.mult, ALU.add))
            fpool.put(ka, tscan)
            fpool.put(k2, tscan)
            W("v", tscan)
            tst = I("v", ve.tensor_copy(state[:, e:e + 1], hview[:, TT - 1:TT]))
            state_toks[e] = tst
            if warm:
                fpool.put(kh, tst)
            else:
                kq, hq_t = bpool.get("v")
                W("v", tscan)
                thq = I("v", ve.tensor_mul(hq_t, hview, hview))
                pend_rms = (e, kq, hq_t, thq)
        if warm:
            xT_free = pr.mark("pe")
            continue
        rms_tok = emit_rms(pend_rms)
        pend_rms = None

        W("v", rms_tok)
        tr = I("v", ve.tensor_scalar(rmsrow[0:1, :], pb[7][0:1, :], 1.0 / D, EPS,
                                     ALU.mult, ALU.add))
        pb_free[7] = tr
        W("s", tr)
        tsq = I("s", se.activation(rstd_b[0:1, :], rmsrow[0:1, :], AF.Sqrt))
        W("v", tsq)
        trc = I("v", ve.reciprocal(rmsrow[0:1, :], rstd_b[0:1, :]))
        W("pe", trc)
        kb = bank()
        tbc = I("pe", te.matmul(pb[kb][:], ones_r[:], rmsrow[0:1, :], start=True, stop=True))
        W("s", tbc)
        trb = I("s", se.copy(rstd_b[:], pb[kb][:]))
        pb_free[kb] = trb

        for tb in range(TB):
            txk = qac.dma(xtok[:, tb * D:(tb + 1) * D],
                          xD[r0 + tb * P:r0 + (tb + 1) * P, :],
                          pre=[xtok_war] if tb == 0 else None, group=(tb > 0))
        qac.close()
        xtok_tok = txk
        for e in range(DC):
            bi = e % 3
            q_ = qsp if e % 2 == 0 else qac
            wtg = q_.dma(wib[bi][:].rearrange("p (db q) -> p db q", q=P),
                         wgB.rearrange("eb (db pp) q -> eb pp db q", pp=P)[e],
                         pre=[wib_free[bi], wgB_tok[e]])
            W("pe", wtg)
            kg = bank()
            for d in range(DC):
                mg = te.matmul(pb[kg][:], wib[bi][:, d * P:(d + 1) * P],
                               xT[:, d * TT:(d + 1) * TT], start=(d == 0), stop=(d == DC - 1))
            tok_g = I("pe", mg)
            wib_free[bi] = tok_g
            W("s", tok_g)
            k1, sg1_t = fpool.get("s")
            ts1 = I("s", se.activation(sg1_t, pb[kg][:], AF.Sigmoid))
            W("v", ts1)
            W("v", ld["gnw"])
            k2, sg_t = fpool.get("v")
            tsg = I("v", ve.scalar_tensor_tensor(sg_t, pb[kg][:], gnw[:, e:e + 1], sg1_t,
                                                 ALU.mult, ALU.mult))
            pb_free[kg] = tsg
            fpool.put(k1, tsg)
            W("v", trb)
            W("v", state_toks[e])
            k3, o2_t = fpool.get("v")
            to2 = I("v", ve.tensor_mul(o2_t, hbuf[:, e * TT:(e + 1) * TT], rstd_b[:]))
            if e == 0:
                W("v", o4_war)
            W("v", to2)
            W("v", tsg)
            to4 = I("v", ve.tensor_mul(o4[:, e * TT:(e + 1) * TT], o2_t, sg_t))
            fpool.put(k2, to4)
            fpool.put(k3, to4)
        o4_tok = to4
        xT_free = pr.mark("pe")

        evac_last = None
        for sweep in range(2):
            bank_tok = [None] * 8
            for d in range(DC):
                bi = d % 3
                wto = qsp.dma(wob[bi][:],
                              woB[d * P:(d + 1) * P, sweep * 1024:(sweep + 1) * 1024],
                              pre=[wob_free[bi], woB_tok[d // 4]])
                W("pe", wto)
                W("pe", o4_tok)
                for tb in range(TB):
                    for eg in range(2):
                        k = tb * 2 + eg
                        if d == 0:
                            W("pe", pb_free[k])
                        mo = te.matmul(pb[k][:],
                                       o4[:, d * TT + tb * P: d * TT + (tb + 1) * P],
                                       wob[bi][:, eg * TT:(eg + 1) * TT],
                                       start=(d == 0), stop=(d == DC - 1))
                tok_d = I("pe", mo)
                if d == DC - 1:
                    for tb in range(TB):
                        for eg in range(2):
                            bank_tok[tb * 2 + eg] = tok_d
                wob_free[bi] = tok_d
            for tb in range(TB):
                for eg in range(2):
                    k = tb * 2 + eg
                    idx = tb * 4 + sweep * 2 + eg
                    col = tb * D + sweep * 1024 + eg * TT
                    W("v", bank_tok[k])
                    W("v", xtok_tok)
                    if sweep == 0 and tb == 0 and eg == 0:
                        W("v", hbuf_war)
                    tv = I("v", ve.scalar_tensor_tensor(r1[:, col:col + TT], pb[k][:], 1.0,
                                                        xtok[:, col:col + TT],
                                                        ALU.mult, ALU.add,
                                                        accum_out=accm[:, idx:idx + 1]))
                    pb_free[k] = tv
                    W("s", tv)
                    ksr, scr_t = fpool.get("s")
                    tss = I("s", se.activation(scr_t, r1[:, col:col + TT], AF.Square,
                                               accum_out=accs[:, idx:idx + 1]))
                    fpool.put(ksr, tss)
                    evac_last = (tv, tss)
        o4_free = pr.mark("pe")
        W("v", evac_last[0])
        W("v", evac_last[1])
        trs = ln_finalize(16, TB)
        y1_toks = []
        for tb in range(TB):
            W("v", trs)
            ty = I("v", ve.tensor_scalar(r1[:, tb * D:(tb + 1) * D], r1[:, tb * D:(tb + 1) * D],
                                         lnstat[:, 2 * TB + tb:2 * TB + tb + 1],
                                         lnstat[:, 7 * TB + tb:7 * TB + tb + 1],
                                         ALU.subtract, ALU.mult))
            y1_toks.append(ty)
        xtok_v_done = pr.mark("v")

        xrot_toks = []

        def emit_h128d(pz):
            d2, kx2, x1a, tok2 = pz
            W("pe", tok2)
            W("pe", ld["h128"])
            kz = bank()
            tmz = I("pe", te.matmul(pb[kz][:, 0:HALF], h128[:], x1a[:, 0:HALF],
                                    start=True, stop=True))
            bpool.put(kx2, tmz)
            W("s", tmz)
            if th == 0 and d2 == 0:
                W("s", o4_free)
                W("s", o4_war)
                W("s", fb_free)
            if th == 1 and d2 == 0:
                W("s", xrot_toks[0])
            tev2 = I("s", se.copy(fb[:, d2 * HALF:(d2 + 1) * HALF], pb[kz][:, 0:HALF]))
            pb_free[kz] = tev2
            return tev2

        for th in range(2):
            pendD = None
            for d in range(DC):
                W("pe", y1_toks[2 * th])
                W("pe", y1_toks[2 * th + 1])
                W("pe", ld["ident"])
                kz2 = bank()
                for tbl in range(2):
                    tb2 = 2 * th + tbl
                    ttr = I("pe", te.transpose(
                        pb[kz2][:, tbl * P:(tbl + 1) * P],
                        r1[:, tb2 * D + d * P: tb2 * D + (d + 1) * P], ident[:]))
                W("s", ttr)
                W("s", ld["suug"])
                kx, x1_t = bpool.get("s")
                tx1 = I("s", se.activation(x1_t[:, 0:HALF], pb[kz2][:, 0:HALF],
                                           AF.Copy, scale=suug[:, d:d + 1]))
                pb_free[kz2] = tx1
                if pendD is not None:
                    tev = emit_h128d(pendD)
                pendD = (d, kx, x1_t, tx1)
            tev = emit_h128d(pendD)
            pendD = None
            fb_tok = tev
            src_b, dst_b = fb, fb2
            tok = fb_tok
            for li, s_ in enumerate([1, 2, 4, 8]):
                blk = s_ * HALF
                vin = src_b[:].rearrange("p (a two bt) -> p a two bt", two=2, bt=blk)
                vout = dst_b[:].rearrange("p (a two bt) -> p a two bt", two=2, bt=blk)
                W("v", tok)
                tp = I("v", ve.tensor_add(vout[:, :, 0, :], vin[:, :, 0, :], vin[:, :, 1, :]))
                tok = I("v", ve.tensor_sub(vout[:, :, 1, :], vin[:, :, 0, :], vin[:, :, 1, :]))
                src_b, dst_b = dst_b, src_b
            W("v", tok)
            if th == 0:
                W("v", xT_free)
            vz_in = src_b[:].rearrange("p (c t) -> p c t", t=HALF)
            vz_out = xrot[:].rearrange("p (c tt) -> p c tt", tt=TT)[:, :, th * HALF:(th + 1) * HALF]
            txr = I("v", ve.tensor_copy(vz_out, vz_in))
            xrot_toks.append(txr)
        xrot_tok = xrot_toks[-1]
        fb_free = xrot_tok
        r1_pe_done = pr.mark("pe")

        if p == 1:
            emit_retile()
            upT_fin = [sem_maxima(upT_tok[ib]) for ib in range(4)]
            gtT_fin = [sem_maxima(gtT_tok[ib]) for ib in range(4)]
        pend = []
        zt_toks = [None, None]

        def emit_h128_evac(pz):
            ic2, k42, t4a, tok42 = pz
            W("pe", tok42)
            kz = bank()
            tmz = I("pe", te.matmul(pb[kz][:], h128[:], t4a, start=True, stop=True))
            bpool.put(k42, tmz)
            W("s", tmz)
            if ic2 == 0:
                W("s", o4_free)
                W("s", fb_free)
            te1 = I("s", se.copy(zt0[:, ic2 * HALF:(ic2 + 1) * HALF], pb[kz][:, 0:HALF]))
            if ic2 == 0:
                W("s", xtok_v_done)
            te2 = I("s", se.copy(zt1[:, ic2 * HALF:(ic2 + 1) * HALF], pb[kz][:, HALF:TT]))
            pb_free[kz] = te2
            zt_toks[0] = te1
            zt_toks[1] = te2

        W("s", wsg_tok)
        q_gt = qsp if p == 1 else qac
        for igg in range(32):
            gi = igg % 2
            pre_u = [wub_free[gi]] + upT_fin[igg // 8]
            if igg == 0:
                pre_u = pre_u + wib_free + wfb_free + wob_free
            up_tok = qsp.dma(wub[gi][:], wupT[igg], pre=pre_u)
            gt_tok = q_gt.dma(wgb[gi][:], wgtT[igg],
                              pre=[wgb_free[gi]] + gtT_fin[igg // 8])
            for icl in range(2):
                ic = igg * 2 + icl
                W("pe", up_tok)
                W("pe", xrot_tok)
                ku = bank()
                for d in range(DC):
                    mu = te.matmul(pb[ku][:],
                                   wub[gi][:, d * HALF + icl * P: d * HALF + (icl + 1) * P],
                                   xrot[:, d * TT:(d + 1) * TT], start=(d == 0), stop=(d == DC - 1))
                tok_u = I("pe", mu)
                W("pe", gt_tok)
                kg2 = bank()
                for d in range(DC):
                    mgt = te.matmul(pb[kg2][:],
                                    wgb[gi][:, d * HALF + icl * P: d * HALF + (icl + 1) * P],
                                    xrot[:, d * TT:(d + 1) * TT], start=(d == 0), stop=(d == DC - 1))
                tok_g2 = I("pe", mgt)
                if icl == 1:
                    wub_free[gi] = tok_u
                    wgb_free[gi] = tok_g2
                if len(pend) >= 2:
                    emit_h128_evac(pend.pop(0))
                W("s", tok_g2)
                k1, t1_t = fpool.get("s")
                tt1 = I("s", se.activation(t1_t, pb[kg2][:], AF.Sigmoid,
                                           scale=wsg[:, ic:ic + 1]))
                W("v", tt1)
                k2, t2_t = fpool.get("v")
                tt2 = I("v", ve.scalar_tensor_tensor(t2_t, pb[kg2][:], wsg[:, ic:ic + 1],
                                                     t1_t, ALU.mult, ALU.mult))
                pb_free[kg2] = tt2
                fpool.put(k1, tt2)
                W("v", tok_u)
                W("v", tt2)
                k3, t3_t = fpool.get("v")
                tt3 = I("v", ve.scalar_tensor_tensor(t3_t, pb[ku][:], wsu[:, ic:ic + 1],
                                                     t2_t, ALU.mult, ALU.mult))
                pb_free[ku] = tt3
                fpool.put(k2, tt3)
                W("v", sdp_tok)
                W("v", tt3)
                k4, t4_t = bpool.get("v")
                tt4 = I("v", ve.tensor_scalar(t4_t, t3_t, sdpc[:, ic:ic + 1], None, ALU.mult))
                fpool.put(k3, tt4)
                pend.append((ic, k4, t4_t, tt4))
        while pend:
            emit_h128_evac(pend.pop(0))
        xrot_free = pr.mark("pe")

        preA = {}
        if p + 1 < n_pass:
            for e2 in range(3):
                preA[("i", e2)] = qsp.dma(
                    wib[e2 % 3][:].rearrange("p (db q) -> p db q", q=P),
                    wiB.rearrange("eb (db pp) q -> eb pp db q", pp=P)[e2],
                    pre=[xrot_free, wib_free[e2 % 3], wiB_tok[e2]])
                preA[("f", e2)] = qac.dma(
                    wfb[e2 % 3][:].rearrange("p (db q) -> p db q", q=P),
                    wfB.rearrange("eb (db pp) q -> eb pp db q", pp=P)[e2],
                    pre=[wfb_free[e2 % 3], wfB_tok[e2]])
        if p + 1 < n_pass:
            xT_pre[p + 1] = load_xT(p + 1, extra_pre=[xrot_free])

        if p == 1:
            ring3 = [foldf[2], foldf2[2], foldf2[1]]
            free3 = [None, None, None]
            prepb_free = [None, None]
            jj = 0
            for piece in range(8):
                for blk in range(DC):
                    s3 = jj % 3
                    p2 = jj % 2
                    pre_l = [free3[s3]] + (fold_all if jj == 0 else []) + fold2_tok[blk]
                    lt = qac.dma(ring3[s3][:],
                                 wdnF[blk * P:(blk + 1) * P, piece * 1024:(piece + 1) * 1024],
                                 pre=pre_l)
                    W("v", lt)
                    W("v", wsd_tok)
                    W("v", prepb_free[p2])
                    vt = I("v", ve.tensor_scalar(prepb[p2][:], ring3[s3][:],
                                                 wsd[:, blk:blk + 1], None, ALU.mult))
                    free3[s3] = vt
                    st = qsp.dma(wdnB[blk * P:(blk + 1) * P, piece * 1024:(piece + 1) * 1024],
                                 prepb[p2][:], pre=[vt])
                    prepb_free[p2] = st
                    wdnp_tok[piece].append(st)
                    jj += 1
                pm = sem_maxima(wdnp_tok[piece])
                for c in range(piece * 8, piece * 8 + 8):
                    pi2 = c % 2
                    ttw = qac.dma(wdb[pi2][:], wdnB[0:D, c * P:(c + 1) * P],
                                  pre=[wdb_tok[pi2]] + pm, transpose=True)
                    st2 = qsp.dma(wdnT[c], wdb[pi2][:], pre=[ttw])
                    wdb_tok[pi2] = st2
                    dnT_tok.append(st2)

        ev_last = None
        for sweep in range(2):
            bank_tok = [None] * 8
            for c in range(IC):
                gslot = grr[0]
                grr[0] = (grr[0] + 1) % len(wdbg)
                pre_w = [wdbg_free[gslot]]
                if p == 1:
                    pre_w.append(dnT_tok[c])
                    if sweep == 0 and c == 0:
                        pre_w += fold_all
                elif p == 2 and sweep == 0 and c < len(wdbg):
                    pre_w += dnT_last
                wtd = qsp.dma(wdbg[gslot][:], wdnT[c][:, sweep * 1024:(sweep + 1) * 1024],
                              pre=pre_w)
                W("pe", wtd)
                W("pe", zt_toks[0])
                W("pe", zt_toks[1])
                for tbl in range(4):
                    ztX = zt0 if tbl < 2 else zt1
                    tl = tbl % 2
                    for jg in range(2):
                        k = tbl * 2 + jg
                        if c == 0:
                            W("pe", pb_free[k])
                        md = te.matmul(pb[k][:],
                                       ztX[:, c * HALF + tl * P: c * HALF + (tl + 1) * P],
                                       wdbg[gslot][:, jg * TT:(jg + 1) * TT],
                                       start=(c == 0), stop=(c == IC - 1))
                tok_c = I("pe", md)
                wdbg_free[gslot] = tok_c
                if c == IC - 1:
                    for k in range(8):
                        bank_tok[k] = tok_c
            for tbl in range(4):
                for jg in range(2):
                    k = tbl * 2 + jg
                    idx = tbl * 4 + sweep * 2 + jg
                    col = tbl * D + sweep * 1024 + jg * TT
                    W("v", bank_tok[k])
                    tv = I("v", ve.scalar_tensor_tensor(r1[:, col:col + TT], pb[k][:], 1.0,
                                                        r1[:, col:col + TT],
                                                        ALU.mult, ALU.add,
                                                        accum_out=accm[:, idx:idx + 1]))
                    pb_free[k] = tv
                    W("s", tv)
                    ksr, scr_t = fpool.get("s")
                    tss = I("s", se.activation(scr_t, r1[:, col:col + TT], AF.Square,
                                               accum_out=accs[:, idx:idx + 1]))
                    fpool.put(ksr, tss)
                    ev_last = (tv, tss)
        if p == 1:
            dnT_last = sem_maxima(dnT_tok)
        W("v", ev_last[0])
        W("v", ev_last[1])
        trs2 = ln_finalize(16, TB)
        out_toks = []
        for tb in range(TB):
            W("v", trs2)
            ty2 = I("v", ve.tensor_scalar(r1[:, tb * D:(tb + 1) * D],
                                          r1[:, tb * D:(tb + 1) * D],
                                          lnstat[:, 2 * TB + tb:2 * TB + tb + 1],
                                          lnstat[:, 7 * TB + tb:7 * TB + tb + 1],
                                          ALU.subtract, ALU.mult))
            tdo = qdo.dma(outD[(p - 1) * TT + tb * P: (p - 1) * TT + (tb + 1) * P, :],
                          r1[:, tb * D:(tb + 1) * D], pre=[ty2])
            out_toks.append(tdo)
        g_pe_done = pr.mark("pe")
        xtok_war = g_pe_done
        o4_war = g_pe_done
        hbuf_war = list(out_toks)

    for k, v in qdo.all_last():
        sy.wait_ge(pr.sem[k], v)
        ge.wait_ge(pr.sem[k], v)
    es.close()
    assert nc.sbuf_base <= 17408, f"auto allocs overflow reserve: {nc.sbuf_base}"
    return nc


_NC_CACHE = {}


def _get_nc(n_full=8):
    if n_full not in _NC_CACHE:
        _NC_CACHE[n_full] = build_nc(n_full)
    return _NC_CACHE[n_full]


def make_aux(full):
    return {
        "wi": np.ascontiguousarray(full["L0_attn_Wi"], dtype=np.float32),
        "wf": np.ascontiguousarray(full["L0_attn_Wf"], dtype=np.float32),
        "wg": np.ascontiguousarray(full["L0_attn_Wg"], dtype=np.float32),
        "wo": np.ascontiguousarray(full["L0_attn_Wo"], dtype=np.float32),
        "wup": np.ascontiguousarray(full["L0_ffn_W_up"], dtype=np.float32),
        "wgt": np.ascontiguousarray(full["L0_ffn_W_gate"], dtype=np.float32),
        "wdn": np.ascontiguousarray(full["L0_ffn_W_down"], dtype=np.float32),
        "gnw": np.ascontiguousarray(np.asarray(full["L0_attn_g_norm_weight"], dtype=np.float32).reshape(DC, P).T),
        "suug": np.ascontiguousarray(np.asarray(full["L0_ffn_SU_ug"], dtype=np.float32).reshape(DC, P).T),
        "sudp": np.ascontiguousarray(np.asarray(full["L0_ffn_SU_dp"], dtype=np.float32).reshape(IC, P).T),
        "wsu": np.ascontiguousarray(np.asarray(full["L0_ffn_Wscale_ug"], dtype=np.float32)[:INTER].reshape(IC, P).T),
        "wsg": np.ascontiguousarray(np.asarray(full["L0_ffn_Wscale_ug"], dtype=np.float32)[INTER:].reshape(IC, P).T),
        "wsd": np.ascontiguousarray(np.asarray(full["L0_ffn_Wscale_dp"], dtype=np.float32).reshape(DC, P).T),
        "h128": _hadamard128(),
        "ident": np.eye(P, dtype=np.float32),
    }


def kernel(**inputs):
    n_full = 8
    ct = n_full * TT
    rows = (n_full + 1) * TT
    nc = _get_nc(n_full)
    x = np.asarray(inputs["L0_attn_hidden_states"])
    aux = make_aux(inputs)
    in_maps = []
    for c in range(8):
        b, q = c // 4, c % 4
        xs = np.zeros((rows, D), np.float32)
        t0 = q * ct
        w0 = max(0, t0 - NWARM)
        nw = t0 - w0
        if nw > 0:
            xs[NWARM - nw:NWARM] = x[b, w0:t0]
        xs[NWARM:] = x[b, t0:t0 + ct]
        m = dict(aux)
        m["x"] = xs
        in_maps.append(m)
    res = run_bass_kernel_spmd(nc, in_maps, core_ids=list(range(8)))
    out = np.empty((2, 16384, D), np.float32)
    for c in range(8):
        b, q = c // 4, c % 4
        out[b, q * ct:(q + 1) * ct] = res.results[c]["out"]
    return out
